# version 01
# baseline (speedup 1.0000x reference)
import math
from contextlib import ExitStack

import numpy as np
import ml_dtypes

import concourse.bass as bass
import concourse.mybir as mybir
from concourse.bass_utils import run_bass_kernel_spmd

F32 = mybir.dt.float32
BF16 = mybir.dt.bfloat16
F16 = mybir.dt.float16
I32 = mybir.dt.int32
AF = mybir.ActivationFunctionType
ALU = mybir.AluOpType
BF = ml_dtypes.bfloat16

B, T, C = 4, 2048, 4096
NCORE = 8
TPC = 1024
LNEPS = 1e-5

_NC_CACHE = {}


def _cached(key, fn):
    if key not in _NC_CACHE:
        _NC_CACHE[key] = fn()
    return _NC_CACHE[key]


def build_ln(with_branches, emit_xn, f_dtype=BF16):
    nc = bass.Bass("TRN2", target_bir_lowering=False)
    x_d = nc.dram_tensor("x", [TPC, C], F32, kind="ExternalInput").ap()
    if with_branches:
        a_d = nc.dram_tensor("a", [TPC, C], BF16, kind="ExternalInput").ap()
        f_d = nc.dram_tensor("f", [TPC, C], f_dtype, kind="ExternalInput").ap()
        xo_d = nc.dram_tensor("xo", [TPC, C], F32, kind="ExternalOutput").ap()
    if emit_xn:
        xn_d = nc.dram_tensor("xn", [TPC, C], BF16, kind="ExternalOutput").ap()
    nt = TPC // 128
    with ExitStack() as es:
        def sb(name, shape, dt):
            return es.enter_context(nc.sbuf_tensor(name, shape, dt))
        xt = [sb(f"xt{i}", [128, C], F32) for i in range(2)]
        at = [sb(f"at{i}", [128, C], F32) for i in range(2)] if with_branches else None
        ft = [sb(f"ft{i}", [128, C], F32) for i in range(2)] if with_branches else None
        stats2 = [sb(f"stats{i}", [128, 48], F32) for i in range(2)]
        mv2 = [sb(f"mv{i}", [128, 2], F32) for i in range(2)]
        rstd2 = [sb(f"rstd{i}", [128, 1], F32) for i in range(2)]
        xnt2 = [sb(f"xnt2_{i}", [128, C], BF16) for i in range(2)] if emit_xn else None
        ld = es.enter_context(nc.semaphore("ld"))
        s_add = es.enter_context(nc.semaphore("s_add"))
        s_agg = es.enter_context(nc.semaphore("s_agg"))
        s_sq = es.enter_context(nc.semaphore("s_sq"))
        s_ts = es.enter_context(nc.semaphore("s_ts"))
        st = es.enter_context(nc.semaphore("st"))
        ini = es.enter_context(nc.semaphore("ini"))
        cstt = sb("cstt", [128, 128], F32)
        blk = es.enter_context(nc.Block())
        cst_d = nc.dram_tensor("cst", [128, 128], F32, kind="ExternalInput").ap()

        @blk.gpsimd
        def _(g):
            g.dma_start(out=cstt[:], in_=cst_d[:, :]).then_inc(ini, 16)
            g.wait_ge(ini, 16)

        @blk.scalar
        def _(sc):
            sc.wait_ge(ini, 16)

        n_in = 3 if with_branches else 1
        st_per = (16 if with_branches else 0) + (16 if emit_xn else 0)
        ld_n = [0]
        st_n = [0]
        done_mark = [0] * (nt + 2)

        def load_tile(i):
            bi = i % 2
            @blk.gpsimd
            def _(g):
                if i >= 2:
                    g.wait_ge(s_ts if emit_xn else s_add, i - 1)
                    g.wait_ge(st, (i - 1) * st_per)
                g.dma_start(out=xt[bi][:], in_=x_d[i * 128:(i + 1) * 128, :]).then_inc(ld, 16)
                if with_branches:
                    g.dma_start(out=at[bi][:], in_=a_d[i * 128:(i + 1) * 128, :]).then_inc(ld, 16)
                    g.dma_start(out=ft[bi][:], in_=f_d[i * 128:(i + 1) * 128, :]).then_inc(ld, 16)
                ld_n[0] += 16 * n_in

        load_tile(0)
        if nt > 1:
            load_tile(1)
        for i in range(nt):
            bi = i % 2
            need_ld = (i + 1) * 16 * n_in

            @blk.vector
            def _(v):
                v.wait_ge(ld, need_ld)
                if emit_xn and i >= 2:
                    v.wait_ge(st, (i - 1) * st_per)
                if with_branches:
                    v.tensor_tensor(xt[bi][:], xt[bi][:], at[bi][:], ALU.add)
                    v.tensor_tensor(xt[bi][:], xt[bi][:], ft[bi][:], ALU.add)
                    v.drain().then_inc(s_add, 1)
                if emit_xn:
                    for j in range(8):
                        v.bn_stats(stats2[bi][:, j * 6:(j + 1) * 6], xt[bi][:, j * 512:(j + 1) * 512])
                    v.drain()
                    v.bn_aggr(mv2[bi][:], stats2[bi][:])
                    v.drain().then_inc(s_agg, 1)
            if with_branches:
                @blk.sync
                def _(s):
                    s.wait_ge(s_add, i + 1)
                    s.dma_start(out=xo_d[i * 128:(i + 1) * 128, :], in_=xt[bi][:]).then_inc(st, 16)
                    st_n[0] += 16
            if emit_xn:
                @blk.scalar
                def _(sc):
                    sc.wait_ge(s_agg, i + 1)
                    sc.activation(rstd2[bi][:], mv2[bi][:, 1:2], AF.Sqrt, bias=cstt[:, 0:1], scale=1.0)
                    sc.drain().then_inc(s_sq, 1)

                @blk.vector
                def _(v):
                    v.wait_ge(s_sq, i + 1)
                    v.reciprocal(rstd2[bi][:], rstd2[bi][:])
                    v.tensor_scalar(xnt2[bi][:], xt[bi][:], mv2[bi][:, 0:1], rstd2[bi][:], ALU.subtract, ALU.mult)
                    v.drain().then_inc(s_ts, 1)

                @blk.sync
                def _(s):
                    s.wait_ge(s_ts, i + 1)
                    s.dma_start(out=xn_d[i * 128:(i + 1) * 128, :], in_=xnt2[bi][:]).then_inc(st, 16)
                    st_n[0] += 16
            if i + 2 < nt:
                load_tile(i + 2)
        @blk.sync
        def _(s):
            s.wait_ge(st, st_n[0])
    return nc


def build_proj(K, O, w_mode, with_bias, out_dt=BF16):
    nc = bass.Bass("TRN2", target_bir_lowering=False)
    x_d = nc.dram_tensor("x", [TPC, K], BF16, kind="ExternalInput").ap()
    if w_mode == "f32T":
        w_h = nc.dram_tensor("w", [K, O], F32, kind="ExternalInput")
    else:
        w_h = nc.dram_tensor("w", [K, O], BF16, kind="ExternalInput")
    w_d = w_h.ap()
    if with_bias:
        b_d = nc.dram_tensor("b", [1, O], F32, kind="ExternalInput").ap()
        o_d = nc.dram_tensor("onesr", [1, 128], F32, kind="ExternalInput").ap()
    y_d = nc.dram_tensor("y", [TPC, O], out_dt, kind="ExternalOutput").ap()
    nt = TPC // 128
    nk = K // 128
    no = O // 512
    with ExitStack() as es:
        def sb(name, shape, dt):
            return es.enter_context(nc.sbuf_tensor(name, shape, dt))
        xf = sb("xf", [128, nk * TPC], BF16)
        wblk = [sb(f"wblk{i}", [128, nk * 512], BF16) for i in range(2)]
        ones = sb("ones", [1, 128], F32)
        bias = sb("bias", [1, O], F32) if with_bias else None
        outb = [sb(f"outb{i}", [128, 512], out_dt) for i in range(2)]
        ps = [es.enter_context(nc.psum_tensor(f"ps{i}", [128, 512], F32)) for i in range(2)]
        ldx = es.enter_context(nc.semaphore("ldx"))
        ldw = es.enter_context(nc.semaphore("ldw"))
        s_mm = es.enter_context(nc.semaphore("s_mm"))
        s_ev = es.enter_context(nc.semaphore("s_ev"))
        st = es.enter_context(nc.semaphore("st"))
        blk = es.enter_context(nc.Block())

        @blk.sync
        def _(s):
            for kc in range(nk):
                s.dma_start_transpose(
                    out=xf[:, kc * TPC:(kc + 1) * TPC],
                    in_=x_d[:, kc * 128:(kc + 1) * 128]).then_inc(ldx, 16)

        if with_bias:
            @blk.gpsimd
            def _(g):
                g.dma_start(out=bias[:, :], in_=b_d[:, :]).then_inc(ldx, 16)
                g.dma_start(out=ones[:, :], in_=o_d[:, :]).then_inc(ldx, 16)

        def load_w(o, buf):
            @blk.gpsimd
            def _(g):
                if o >= 2:
                    g.wait_ge(s_mm, (o - 1) * nt)
                src = bass.AP(w_h, o * 512,
                              [[O, 128], [128 * O, nk], [1, 512]])
                dst = wblk[buf].rearrange("p (kc o) -> p kc o", kc=nk)
                g.dma_start(out=dst, in_=src).then_inc(ldw, 16)

        w_dmas = 1
        load_w(0, 0)
        if no > 1:
            load_w(1, 1)
        mm_n = 0
        ev_n = 0
        st_n = 0
        for o in range(no):
            buf = o % 2
            for t in range(nt):
                pb = (o * nt + t) % 2

                @blk.tensor
                def _(te):
                    if o == 0 and t == 0:
                        te.wait_ge(ldx, (nk + (2 if with_bias else 0)) * 16)
                    te.wait_ge(ldw, (o + 1) * w_dmas * 16)
                    if mm_n >= 2:
                        te.wait_ge(s_ev, mm_n - 1)
                    for kc in range(nk):
                        last = (kc == nk - 1 and not with_bias)
                        mm = te.matmul(ps[pb][:],
                                       xf[:, kc * TPC + t * 128: kc * TPC + (t + 1) * 128],
                                       wblk[buf][:, kc * 512:(kc + 1) * 512],
                                       start=(kc == 0), stop=last)
                        if last:
                            mm.then_inc(s_mm, 1)
                    if with_bias:
                        te.matmul(ps[pb][:], ones[:, :],
                                  bias[:, o * 512:(o + 1) * 512],
                                  start=False, stop=True).then_inc(s_mm, 1)
                mm_n += 1

                @blk.vector
                def _(v):
                    v.wait_ge(s_mm, mm_n)
                    if ev_n >= 2:
                        v.wait_ge(st, (ev_n - 1) * 16)
                    v.tensor_copy(outb[pb][:], ps[pb][:])
                    v.drain().then_inc(s_ev, 1)
                ev_n += 1

                @blk.sync
                def _(s):
                    s.wait_ge(s_ev, ev_n)
                    s.dma_start(out=y_d[t * 128:(t + 1) * 128, o * 512:(o + 1) * 512],
                                in_=outb[pb][:]).then_inc(st, 16)
                st_n += 16
            if o + 2 < no:
                load_w(o + 2, buf)
        @blk.sync
        def _(s):
            s.wait_ge(st, st_n)
    return nc


def build_prep():
    nc = bass.Bass("TRN2", target_bir_lowering=False)
    gq = nc.dram_tensor("gq", [1408, 512], I32, kind="ExternalInput").ap()
    gs = nc.dram_tensor("gs", [1408, 32], F16, kind="ExternalInput").ap()
    gz = nc.dram_tensor("gz", [1408, 32], I32, kind="ExternalInput").ap()
    uq = nc.dram_tensor("uq", [1408, 512], I32, kind="ExternalInput").ap()
    us = nc.dram_tensor("us", [1408, 32], F16, kind="ExternalInput").ap()
    uz = nc.dram_tensor("uz", [1408, 32], I32, kind="ExternalInput").ap()
    dq = nc.dram_tensor("dq", [512, 1376], I32, kind="ExternalInput").ap()
    ds = nc.dram_tensor("ds", [512, 86], F16, kind="ExternalInput").ap()
    dz = nc.dram_tensor("dz", [512, 86], I32, kind="ExternalInput").ap()
    wp = nc.dram_tensor("wp", [512, 2048], I32, kind="ExternalInput").ap()
    l3sz = nc.dram_tensor("l3sz", [512, 128], F32, kind="ExternalInput").ap()
    shf = nc.dram_tensor("shf", [128, 128], I32, kind="ExternalInput").ap()
    idn = nc.dram_tensor("idn", [128, 128], BF16, kind="ExternalInput").ap()
    gw = nc.dram_tensor("gw", [1408, 4096], BF16, kind="ExternalOutput").ap()
    uw = nc.dram_tensor("uw", [1408, 4096], BF16, kind="ExternalOutput").ap()
    dw = nc.dram_tensor("dw", [512, 11008], BF16, kind="ExternalOutput").ap()
    l3w = nc.dram_tensor("l3w", [512, 4096], BF16, kind="ExternalOutput").ap()

    with ExitStack() as es:
        def sb(name, shape, dt):
            return es.enter_context(nc.sbuf_tensor(name, shape, dt))
        qt = [sb(f"qt{i}", [128, 2048], I32) for i in range(2)]
        nib = sb("nib", [128, 11008], I32)
        outw = [sb(f"outw{i}", [128, 11008], BF16) for i in range(2)]
        tbuf = sb("tbuf", [128, 11008], BF16)
        svec = [sb(f"svec{i}", [128, 96], F16) for i in range(2)]
        zvec = [sb(f"zvec{i}", [128, 96], I32) for i in range(2)]
        sf32 = sb("sf32", [128, 96], F32)
        zpf = sb("zpf", [128, 96], F32)
        shft = sb("shft", [128, 128], I32)
        idnt = sb("idnt", [128, 128], BF16)
        l3sc = [sb(f"l3sc{i}", [128, 128], F32) for i in range(2)]
        pst = [es.enter_context(nc.psum_tensor(f"pst{i}", [128, 512], BF16)) for i in range(2)]
        ld = es.enter_context(nc.semaphore("ld"))
        s_dq = es.enter_context(nc.semaphore("s_dq"))
        s_tp = es.enter_context(nc.semaphore("s_tp"))
        s_tb = es.enter_context(nc.semaphore("s_tb"))
        st = es.enter_context(nc.semaphore("st"))
        blk = es.enter_context(nc.Block())

        @blk.gpsimd
        def _(g):
            g.dma_start(out=shft[:], in_=shf[:, :]).then_inc(ld, 16)
            g.dma_start(out=idnt[:], in_=idn[:, :]).then_inc(ld, 16)

        ld_n = [32]
        dq_n = [0]
        tp_n = [0]
        tb_n = [0]
        st_n = [0]

        def emit_tile(w_out, ti, K_, body_loads, body_dequant):
            bi = dq_n[0] % 2
            body_loads(bi)

            @blk.vector
            def _(v):
                v.wait_ge(ld, ld_n[0])
                if dq_n[0] >= 2:
                    v.wait_ge(s_tp, tp_marks[dq_n[0] - 2])
                body_dequant(v, bi)
                v.drain().then_inc(s_dq, 1)
            dq_n[0] += 1

            nchunk = K_ // 128
            groups = [(gi, min(4, nchunk - gi * 4)) for gi in range((nchunk + 3) // 4)]
            ng = len(groups)

            @blk.tensor
            def _(te):
                te.wait_ge(s_dq, dq_n[0])
                if dq_n[0] == 1:
                    te.wait_ge(ld, 32)
                for gi, gl in groups:
                    pb = (tp_n[0] + gi) % 2
                    te.wait_ge(s_tb, max(0, tp_n[0] + gi - 1))
                    for q4 in range(gl):
                        kc = gi * 4 + q4
                        mm = te.transpose(pst[pb][:, q4 * 128:(q4 + 1) * 128],
                                          outw[bi][:, kc * 128:(kc + 1) * 128], idnt[:])
                        if q4 == gl - 1:
                            mm.then_inc(s_tp, 1)

            @blk.vector
            def _(v):
                for gi, gl in groups:
                    pb = (tp_n[0] + gi) % 2
                    v.wait_ge(s_tp, tp_n[0] + gi + 1)
                    if dq_n[0] >= 2 and gi == 0:
                        v.wait_ge(st, st_marks[dq_n[0] - 2])
                    v.tensor_copy(tbuf[:, gi * 512:gi * 512 + gl * 128], pst[pb][:, :gl * 128])
                    v.drain().then_inc(s_tb, 1)
            tp_n[0] += ng
            tb_n[0] += ng
            tp_marks.append(tp_n[0])

            @blk.sync
            def _(s):
                s.wait_ge(s_tb, tb_n[0])
                s.dma_start(out=w_out[ti * 128:(ti + 1) * 128, :K_],
                            in_=tbuf[:, :K_]).then_inc(st, 16)
            st_n[0] += 16
            st_marks.append(st_n[0])

        tp_marks = []
        st_marks = []

        def gptq(q_d, s_d, z_d, w_out, ti, nwords, ngrp, K_):
            def loads(bi):
                @blk.gpsimd
                def _(g):
                    if dq_n[0] >= 2:
                        g.wait_ge(s_dq, dq_n[0] - 1)
                    g.dma_start(out=qt[bi][:, :nwords], in_=q_d[ti * 128:(ti + 1) * 128, :]).then_inc(ld, 16)
                    g.dma_start(out=svec[bi][:, :ngrp], in_=s_d[ti * 128:(ti + 1) * 128, :]).then_inc(ld, 16)
                    g.dma_start(out=zvec[bi][:, :ngrp], in_=z_d[ti * 128:(ti + 1) * 128, :]).then_inc(ld, 16)
                ld_n[0] += 48

            def dequant(v, bi):
                for j in range(8):
                    outap = nib[:, j:K_:8]
                    if j == 0:
                        v.tensor_scalar(outap, qt[bi][:, :nwords], 15, None,
                                        ALU.bitwise_and, ALU.bypass)
                    else:
                        v.tensor_scalar(outap, qt[bi][:, :nwords], 4 * j, 15,
                                        ALU.logical_shift_right, ALU.bitwise_and)
                v.tensor_scalar(zvec[bi][:, :ngrp], zvec[bi][:, :ngrp], shft[:, 0:1], 15,
                                ALU.logical_shift_right, ALU.bitwise_and)
                v.tensor_copy(sf32[:, :ngrp], svec[bi][:, :ngrp])
                v.drain()
                v.scalar_tensor_tensor(zpf[:, :ngrp], zvec[bi][:, :ngrp], 1.0,
                                       sf32[:, :ngrp], ALU.add, ALU.mult)
                v.drain()
                for g_ in range(ngrp):
                    v.tensor_scalar(outw[bi][:, g_ * 128:(g_ + 1) * 128],
                                    nib[:, g_ * 128:(g_ + 1) * 128],
                                    sf32[:, g_:g_ + 1], zpf[:, g_:g_ + 1],
                                    ALU.mult, ALU.subtract)
            emit_tile(w_out, ti, K_, loads, dequant)

        for ti in range(11):
            gptq(gq, gs, gz, gw, ti, 512, 32, 4096)
        for ti in range(11):
            gptq(uq, us, uz, uw, ti, 512, 32, 4096)
        for ti in range(4):
            gptq(dq, ds, dz, dw, ti, 1376, 86, 11008)

        for ti in range(4):
            def loads(bi, ti=ti):
                @blk.gpsimd
                def _(g):
                    g.wait_ge(s_dq, dq_n[0] - 1)
                    g.dma_start(out=qt[bi][:, :2048], in_=wp[ti * 128:(ti + 1) * 128, :]).then_inc(ld, 16)
                    g.dma_start(out=l3sc[bi][:], in_=l3sz[ti * 128:(ti + 1) * 128, :]).then_inc(ld, 16)
                ld_n[0] += 32

            def dequant(v, bi, ti=ti):
                v.tensor_scalar(nib[:, 0:4096:2], qt[bi][:, :2048], 15, None,
                                ALU.bitwise_and, ALU.bypass)
                v.tensor_scalar(nib[:, 1:4096:2], qt[bi][:, :2048], 4, 15,
                                ALU.logical_shift_right, ALU.bitwise_and)
                v.drain()
                v.tensor_scalar(outw[bi][:, :4096], nib[:, :4096],
                                l3sc[bi][:, 1:2], l3sc[bi][:, 0:1],
                                ALU.subtract, ALU.mult)
            emit_tile(l3w, ti, 4096, loads, dequant)

        @blk.sync
        def _(s):
            s.wait_ge(st, st_n[0])
    return nc


def build_ffn0():
    NI, NIP = 11008, 11264
    nc = bass.Bass("TRN2", target_bir_lowering=False)
    x_d = nc.dram_tensor("x", [TPC, C], BF16, kind="ExternalInput").ap()
    gw_h = nc.dram_tensor("gwt", [C, NIP], BF16, kind="ExternalInput")
    uw_h = nc.dram_tensor("uwt", [C, NIP], BF16, kind="ExternalInput")
    dw_h = nc.dram_tensor("dwt", [NI, C], BF16, kind="ExternalInput")
    y_d = nc.dram_tensor("y", [TPC, C], F16, kind="ExternalOutput").ap()
    TP = 512
    npan = TPC // TP
    nk = C // 128
    nnc = NI // 128
    OG = 256
    with ExitStack() as es:
        def sb(name, shape, dt):
            return es.enter_context(nc.sbuf_tensor(name, shape, dt))
        inter = sb("inter", [128, nnc * TP], BF16)
        arena = sb("arena", [128, 3 * nk * 512], BF16)
        xfp = arena[:, 0:nk * TP]
        wg = arena[:, nk * 512:2 * nk * 512]
        wu = arena[:, 2 * nk * 512:3 * nk * 512]
        dwb = [arena[:, i * nnc * OG:(i + 1) * nnc * OG] for i in range(2)]
        gsil = sb("gsil", [128, TP], F32)
        outb = [sb(f"outb{i}", [128, OG], F16) for i in range(2)]
        ps_g = [es.enter_context(nc.psum_tensor(f"psg{i}", [128, TP], F32)) for i in range(2)]
        ps_d = [es.enter_context(nc.psum_tensor(f"psd{i}", [128, 512], F32)) for i in range(4)]
        ldx = es.enter_context(nc.semaphore("ldx"))
        ldw = es.enter_context(nc.semaphore("ldw"))
        ldd = es.enter_context(nc.semaphore("ldd"))
        s_gmm = es.enter_context(nc.semaphore("s_gmm"))
        s_sil = es.enter_context(nc.semaphore("s_sil"))
        s_int = es.enter_context(nc.semaphore("s_int"))
        s_dmm = es.enter_context(nc.semaphore("s_dmm"))
        s_ev = es.enter_context(nc.semaphore("s_ev"))
        st = es.enter_context(nc.semaphore("st"))
        blk = es.enter_context(nc.Block())

        @blk.scalar
        def _(sc):
            sc.preload_act_table(AF.Silu)

        ldx_n = [0]
        ldw_n = [0]
        ldd_n = [0]
        gmm_n = [0]
        sil_n = [0]
        int_n = [0]
        dmm_n = [0]
        ev_n = [0]
        st_n = [0]

        for pan in range(npan):
            t0 = pan * TP

            @blk.sync
            def _(s):
                if pan > 0:
                    s.wait_ge(s_int, pan * nnc)
                    s.wait_ge(s_dmm, pan * (C // OG) * (TP // 128))
                    s.wait_ge(st, pan * (C // OG) * (TP // 128) * 16)
                for kc in range(nk):
                    s.dma_start_transpose(
                        out=xfp[:, kc * TP:(kc + 1) * TP],
                        in_=x_d[t0:t0 + TP, kc * 128:(kc + 1) * 128]).then_inc(ldx, 16)
            ldx_n[0] += nk * 16

            for grp in range(NIP // 512):
                n0 = grp * 512

                dnow = dmm_n[0]

                @blk.gpsimd
                def _(g):
                    if gmm_n[0] > 0:
                        g.wait_ge(s_gmm, gmm_n[0])
                    if grp == 0 and dnow > 0:
                        g.wait_ge(s_dmm, dnow)
                    src_g = bass.AP(gw_h, n0, [[NIP, 128], [128 * NIP, nk], [1, 512]])
                    src_u = bass.AP(uw_h, n0, [[NIP, 128], [128 * NIP, nk], [1, 512]])
                    g.dma_start(out=wg.rearrange("p (kc n) -> p kc n", n=512), in_=src_g).then_inc(ldw, 16)
                    g.dma_start(out=wu.rearrange("p (kc n) -> p kc n", n=512), in_=src_u).then_inc(ldw, 16)
                ldw_n[0] += 32

                for j in range(4):
                    ncg = grp * 4 + j
                    if ncg >= nnc:
                        continue
                    pg = int_n[0] % 2

                    @blk.tensor
                    def _(te):
                        if gmm_n[0] == 0:
                            te.wait_ge(ldx, ldx_n[0])
                        elif j == 0 and grp == 0:
                            te.wait_ge(ldx, ldx_n[0])
                        te.wait_ge(ldw, ldw_n[0])
                        if int_n[0] >= 2:
                            te.wait_ge(s_int, int_n[0] - 1)
                        for kc in range(nk):
                            mm = te.matmul(ps_g[pg][:],
                                           wg[:, kc * 512 + j * 128: kc * 512 + (j + 1) * 128],
                                           xfp[:, kc * TP:(kc + 1) * TP],
                                           start=(kc == 0), stop=(kc == nk - 1))
                            if kc == nk - 1:
                                mm.then_inc(s_gmm, 1)
                    gmm_n[0] += 1

                    @blk.scalar
                    def _(sc):
                        sc.wait_ge(s_gmm, gmm_n[0])
                        sc.activation(gsil[:], ps_g[pg][:], AF.Silu)
                        sc.drain().then_inc(s_sil, 1)
                    sil_n[0] += 1

                    @blk.tensor
                    def _(te):
                        te.wait_ge(s_sil, sil_n[0])
                        for kc in range(nk):
                            mm = te.matmul(ps_g[pg][:],
                                           wu[:, kc * 512 + j * 128: kc * 512 + (j + 1) * 128],
                                           xfp[:, kc * TP:(kc + 1) * TP],
                                           start=(kc == 0), stop=(kc == nk - 1))
                            if kc == nk - 1:
                                mm.then_inc(s_gmm, 1)
                    gmm_n[0] += 1

                    @blk.vector
                    def _(v):
                        v.wait_ge(s_gmm, gmm_n[0])
                        v.tensor_tensor(inter[:, ncg * TP:(ncg + 1) * TP],
                                        gsil[:], ps_g[pg][:], ALU.mult)
                        v.drain().then_inc(s_int, 1)
                    int_n[0] += 1

            def load_dw(og, buf):
                dm_now = dmm_n[0]
                int_need = int_n[0]

                @blk.sync
                def _(s):
                    if dm_now > 0:
                        s.wait_ge(s_dmm, dm_now)
                    if og < 2:
                        s.wait_ge(s_int, int_need)
                    src = bass.AP(dw_h, og * OG, [[C, 128], [128 * C, nnc], [1, OG]])
                    s.dma_start(out=dwb[buf].rearrange("p (kc o) -> p kc o", o=OG),
                                in_=src).then_inc(ldd, 16)

            load_dw(0, 0)
            load_dw(1, 1)
            for og in range(C // OG):
                buf = og % 2
                ldd_n[0] += 16
                for tt in range(TP // 128):
                    pd = tt

                    @blk.tensor
                    def _(te):
                        te.wait_ge(ldd, (og + 1) * 16 + pan * (C // OG) * 16)
                        if tt == 0 and og == 0:
                            te.wait_ge(s_int, (pan + 1) * nnc)
                        if dmm_n[0] >= 4:
                            te.wait_ge(s_ev, dmm_n[0] - 3)
                        for kc in range(nnc):
                            mm = te.matmul(ps_d[pd][:, :OG],
                                           inter[:, kc * TP + tt * 128: kc * TP + (tt + 1) * 128],
                                           dwb[buf][:, kc * OG:(kc + 1) * OG],
                                           start=(kc == 0), stop=(kc == nnc - 1))
                            if kc == nnc - 1:
                                mm.then_inc(s_dmm, 1)
                    dmm_n[0] += 1

                    @blk.vector
                    def _(v):
                        v.wait_ge(s_dmm, dmm_n[0])
                        if ev_n[0] >= 2:
                            v.wait_ge(st, (ev_n[0] - 1) * 16)
                        v.tensor_copy(outb[ev_n[0] % 2][:], ps_d[pd][:, :OG])
                        v.drain().then_inc(s_ev, 1)
                    ev_n[0] += 1

                    @blk.sync
                    def _(s):
                        s.wait_ge(s_ev, ev_n[0])
                        s.dma_start(out=y_d[t0 + tt * 128: t0 + (tt + 1) * 128,
                                            og * OG:(og + 1) * OG],
                                    in_=outb[(ev_n[0] - 1) % 2][:]).then_inc(st, 16)
                    st_n[0] += 16
                if og + 2 < C // OG:
                    load_dw(og + 2, buf)
        @blk.sync
        def _(s):
            s.wait_ge(st, st_n[0])
    return nc


def run_ffn0(xs, GWT, UWT, DWT):
    nc = _cached(("ffn0",), build_ffn0)
    maps = [{"x": xs[i], "gwt": GWT, "uwt": UWT, "dwt": DWT} for i in range(NCORE)]
    res = run_bass_kernel_spmd(nc, maps, core_ids=list(range(NCORE)))
    return [r["y"] for r in res.results]


def build_attn1():
    nc = bass.Bass("TRN2", target_bir_lowering=False)
    xn_h = nc.dram_tensor("xnb", [T, 2048], BF16, kind="ExternalInput")
    xn_d = xn_h.ap()
    bv_d = nc.dram_tensor("bvs", [6, 1024], F32, kind="ExternalInput").ap()
    bvf_d = nc.dram_tensor("bvf", [1, 6144], F32, kind="ExternalInput").ap()
    idn_d = nc.dram_tensor("idn32", [128, 128], F32, kind="ExternalInput").ap()
    one_d = nc.dram_tensor("onesr", [1, 128], F32, kind="ExternalInput").ap()
    tri_d = nc.dram_tensor("triT", [128, 128], BF16, kind="ExternalInput").ap()
    y_d = nc.dram_tensor("y", [T, 2048], BF16, kind="ExternalOutput").ap()
    HD = 1024
    ndc = HD // 128
    NG = T // 512
    NKB = T // 128
    SCL = 4.0 / 32.0
    with ExitStack() as es:
        def sb(name, shape, dt):
            return es.enter_context(nc.sbuf_tensor(name, shape, dt))
        bvt = sb("bvt", [6, 1024], F32)
        sgn = sb("sgn", [6, 1024], F32)
        u2 = sb("u2", [6, 1024], F32)
        alpha = sb("alpha", [6, 1], F32)
        bcol = sb("bcol", [128, 8 * 6], F32)
        bvr = sb("bvr", [1, 6144], F32)
        sgr = sb("sgr", [1, 6144], F32)
        u2r = sb("u2r", [1, 6144], F32)
        alr = sb("alr", [1, 6], F32)
        onesc = sb("onesc", [1, 128], F32)
        idnt = sb("idnt", [128, 128], F32)
        trit = sb("trit", [128, 128], BF16)
        vbb = sb("vbb", [128, HD], F32)
        qf = sb("qf", [128, ndc * T], BF16)
        kf = sb("kf", [128, ndc * T], BF16)
        xv = sb("xv", [128, NKB * HD], BF16)
        pt = sb("pt", [128, NKB * 512], BF16)
        outb = [sb(f"outb{i}", [128, 512], BF16) for i in range(2)]
        ps_s = [es.enter_context(nc.psum_tensor(f"pss{i}", [128, 512], F32)) for i in range(2)]
        ps_o = [es.enter_context(nc.psum_tensor(f"pso{i}", [128, 512], F32)) for i in range(4)]
        ps_t = es.enter_context(nc.psum_tensor("pstr", [128, 512], F32))
        ld = es.enter_context(nc.semaphore("ld"))
        s_b = es.enter_context(nc.semaphore("s_b"))
        s_qk = es.enter_context(nc.semaphore("s_qk"))
        s_vb = es.enter_context(nc.semaphore("s_vb"))
        s_sm = es.enter_context(nc.semaphore("s_sm"))
        s_sg = es.enter_context(nc.semaphore("s_sg"))
        s_p = es.enter_context(nc.semaphore("s_p"))
        s_pv = es.enter_context(nc.semaphore("s_pv"))
        s_ev = es.enter_context(nc.semaphore("s_ev"))
        st = es.enter_context(nc.semaphore("st"))
        blk = es.enter_context(nc.Block())

        @blk.scalar
        def _(sc):
            sc.preload_act_table(AF.Sigmoid)

        @blk.gpsimd
        def _(g):
            g.dma_start(out=bvt[:], in_=bv_d[:, :]).then_inc(ld, 16)
            g.dma_start(out=bvr[:], in_=bvf_d[:, :]).then_inc(ld, 16)
            g.dma_start(out=idnt[:], in_=idn_d[:, :]).then_inc(ld, 16)
            g.dma_start(out=trit[:], in_=tri_d[:, :]).then_inc(ld, 16)
            g.dma_start(out=onesc[:], in_=one_d[:, :]).then_inc(ld, 16)

        @blk.vector
        def _(v):
            v.wait_ge(ld, 80)
            v.tensor_reduce(alpha[:], bvt[:], mybir.AxisListType.X, ALU.add,
                            apply_absolute_value=True)
            v.tensor_reduce(alr[:], bvr[:].rearrange("p (r h) -> p r h", h=1024),
                            mybir.AxisListType.X, ALU.add, apply_absolute_value=True)
            v.drain().then_inc(s_b, 1)

        @blk.scalar
        def _(sc):
            sc.wait_ge(s_b, 1)
            sc.activation(sgn[:], bvt[:], AF.Sign)
            sc.activation(sgr[:], bvr[:], AF.Sign)
            sc.drain().then_inc(s_b, 1)

        @blk.vector
        def _(v):
            v.wait_ge(s_b, 2)
            for (S_, U_, A_, B_, nseg) in ((sgn, u2, alpha, bvt, None), (sgr, u2r, alr, bvr, 6)):
                v.tensor_tensor(U_[:], S_[:], S_[:], ALU.mult)
                v.drain()
                v.scalar_tensor_tensor(U_[:], U_[:], -1.0, S_[:], ALU.mult, ALU.add)
                v.tensor_scalar(U_[:], U_[:], 1.0, None, ALU.add)
                v.tensor_scalar(A_[:], A_[:], 1.0 / 1024.0, None, ALU.mult)
                v.drain()
                if nseg is None:
                    v.tensor_scalar(B_[:], U_[:], A_[:], None, ALU.mult)
                else:
                    for r in range(nseg):
                        v.tensor_scalar(B_[0:1, r * 1024:(r + 1) * 1024],
                                        U_[0:1, r * 1024:(r + 1) * 1024],
                                        A_[0:1, r:r + 1], None, ALU.mult)
                v.drain()
            v.drain().then_inc(s_b, 1)

        @blk.tensor
        def _(te):
            te.wait_ge(s_b, 3)
            for dc in range(8):
                mm = te.transpose(ps_t[:, dc * 8:dc * 8 + 6],
                                  bvt[:, dc * 128:(dc + 1) * 128], idnt[0:6, 0:6])
                if dc == 7:
                    mm.then_inc(s_b, 1)

        @blk.vector
        def _(v):
            v.wait_ge(s_b, 4)
            for dc in range(8):
                v.tensor_copy(bcol[:, dc * 6:(dc + 1) * 6], ps_t[:, dc * 8:dc * 8 + 6])
            v.drain().then_inc(s_b, 1)

        st_n = [0]
        sm_n = [0]
        sg_n = [0]
        p_n = [0]
        pv_n = [0]
        ev_n = [0]
        ld_n = [80]

        for hh in range(2):
            c0 = hh * 1024

            vb_ps = [ps_t, ps_o[0]]

            @blk.tensor
            def _(te):
                te.wait_ge(s_b, 5)
                if hh > 0:
                    te.wait_ge(s_ev, ev_n[0])
                for half in range(2):
                    te.matmul(vb_ps[half][:, :512], onesc[:, :],
                              bvr[0:1, 4096 + c0 + half * 512: 4096 + c0 + (half + 1) * 512],
                              start=True, stop=True).then_inc(s_vb, 1)

            for half in range(2):
                @blk.vector
                def _(v):
                    v.wait_ge(s_vb, hh * 2 + half + 1)
                    v.tensor_copy(vbb[:, half * 512:(half + 1) * 512], vb_ps[half][:, :512])
                    v.drain().then_inc(s_b, 1)

            @blk.sync
            def _(s):
                if hh > 0:
                    s.wait_ge(s_pv, pv_n[0])
                for dc in range(ndc):
                    s.dma_start_transpose(
                        out=qf[:, dc * T:(dc + 1) * T],
                        in_=xn_d[:, c0 + dc * 128:c0 + (dc + 1) * 128]).then_inc(ld, 16)
                s.dma_start(out=xv.rearrange("p (kb d) -> p kb d", d=HD),
                            in_=xn_h.ap().rearrange("(kb p) d -> p kb d", p=128)[:, :, c0:c0 + HD]
                            ).then_inc(ld, 16)
            ld_n[0] += (ndc + 1) * 16

            ldq = ld_n[0]

            @blk.vector
            def _(v):
                v.wait_ge(s_b, 5)
                v.wait_ge(ld, ldq)
                if hh > 0:
                    v.wait_ge(s_pv, pv_n[0])
                for dc in range(ndc):
                    qcol = bcol[:, dc * 6 + hh:dc * 6 + hh + 1]
                    kcol = bcol[:, dc * 6 + 2 + hh:dc * 6 + 2 + hh + 1]
                    v.tensor_scalar(kf[:, dc * T:(dc + 1) * T], qf[:, dc * T:(dc + 1) * T],
                                    kcol, None, ALU.mult)
                    v.tensor_scalar(qf[:, dc * T:(dc + 1) * T], qf[:, dc * T:(dc + 1) * T],
                                    qcol, None, ALU.mult)
                v.drain().then_inc(s_qk, 1)

            for g_ in range(NG):
                trips = 4 * (g_ + 1)
                for kb in range(trips):
                    pb = (sm_n[0]) % 2

                    @blk.tensor
                    def _(te):
                        te.wait_ge(s_qk, hh + 1)
                        if g_ == 0 and kb == 0 and hh == 0:
                            pass
                        if sm_n[0] >= 2:
                            te.wait_ge(s_p, sm_n[0] - 1)
                        if kb == 0:
                            te.wait_ge(s_pv, pv_n[0])
                        for dc in range(ndc):
                            mm = te.matmul(ps_s[pb][:],
                                           kf[:, dc * T + kb * 128:dc * T + (kb + 1) * 128],
                                           qf[:, dc * T + g_ * 512:dc * T + (g_ + 1) * 512],
                                           start=(dc == 0), stop=(dc == ndc - 1))
                            if dc == ndc - 1:
                                mm.then_inc(s_sm, 1)
                    sm_n[0] += 1

                    @blk.scalar
                    def _(sc):
                        sc.wait_ge(s_sm, sm_n[0])
                        sc.activation(pt[:, kb * 512:(kb + 1) * 512], ps_s[pb][:],
                                      AF.Sigmoid, scale=SCL)
                        sc.drain().then_inc(s_sg, 1)
                    sg_n[0] += 1

                    @blk.vector
                    def _(v):
                        v.wait_ge(s_sg, sg_n[0])
                        jstar = kb - 4 * g_
                        if jstar >= 0:
                            for j in range(4):
                                sl = pt[:, kb * 512 + j * 128: kb * 512 + (j + 1) * 128]
                                if j < jstar:
                                    v.memset(sl, 0.0)
                                elif j == jstar:
                                    v.tensor_tensor(sl, sl, trit[:], ALU.mult)
                        v.drain().then_inc(s_p, 1)
                    p_n[0] += 1

                for dg in range(2):
                    for j in range(4):
                        pj = j

                        @blk.tensor
                        def _(te):
                            te.wait_ge(s_p, p_n[0])
                            te.wait_ge(s_b, 7 + hh * 2)
                            if pv_n[0] >= 4:
                                te.wait_ge(s_ev, pv_n[0] - 3)
                            for kb in range(trips):
                                mm = te.matmul(ps_o[pj][:],
                                               pt[:, kb * 512 + j * 128:kb * 512 + (j + 1) * 128],
                                               xv[:, kb * HD + dg * 512:kb * HD + (dg + 1) * 512],
                                               start=(kb == 0), stop=(kb == trips - 1))
                                if kb == trips - 1:
                                    mm.then_inc(s_pv, 1)
                        pv_n[0] += 1

                        @blk.vector
                        def _(v):
                            v.wait_ge(s_pv, pv_n[0])
                            v.wait_ge(s_b, 7 + hh * 2)
                            if ev_n[0] >= 2:
                                v.wait_ge(st, (ev_n[0] - 1) * 16)
                            v.tensor_tensor(outb[ev_n[0] % 2][:], ps_o[pj][:],
                                            vbb[:, dg * 512:(dg + 1) * 512], ALU.mult)
                            v.drain().then_inc(s_ev, 1)
                        ev_n[0] += 1

                        @blk.sync
                        def _(s):
                            s.wait_ge(s_ev, ev_n[0])
                            s.dma_start(
                                out=y_d[g_ * 512 + j * 128:g_ * 512 + (j + 1) * 128,
                                        c0 + dg * 512:c0 + (dg + 1) * 512],
                                in_=outb[(ev_n[0] - 1) % 2][:]).then_inc(st, 16)
                        st_n[0] += 16
        @blk.sync
        def _(s):
            s.wait_ge(st, st_n[0])
    return nc


def run_attn1(xn_full, bv_q, bv_k, bv_v):
    nc = _cached(("attn1",), build_attn1)
    idn32 = np.eye(128, dtype=np.float32)
    onesr = np.ones((1, 128), np.float32)
    i_ = np.arange(128)
    triT = (i_[None, :] >= i_[:, None]).astype(BF)
    maps = []
    for c in range(NCORE):
        b_, half = c // 2, c % 2
        h0 = half * 2
        xnb = np.ascontiguousarray(xn_full[b_][:, h0 * 1024:(h0 + 2) * 1024])
        bvs = np.concatenate([bv_q[h0:h0 + 2], bv_k[h0:h0 + 2], bv_v[h0:h0 + 2]], 0).astype(np.float32)
        maps.append({"xnb": xnb, "bvs": np.ascontiguousarray(bvs),
                     "bvf": np.ascontiguousarray(bvs.reshape(1, -1)),
                     "idn32": idn32, "onesr": onesr, "triT": triT})
    res = run_bass_kernel_spmd(nc, maps, core_ids=list(range(NCORE)))
    out = np.zeros((B, T, C), BF)
    for c in range(NCORE):
        b_, half = c // 2, c % 2
        out[b_][:, half * 2048:(half + 1) * 2048] = res.results[c]["y"]
    return out


def build_attn_qk(layer, dbg_phase1=False, dbg_heads=16):
    NQ = 2048
    NKV = 2048 if layer == 0 else 512
    QKVD = NQ + 2 * NKV
    nc = bass.Bass("TRN2", target_bir_lowering=False)
    xn_h = nc.dram_tensor("xnb", [T, C], BF16, kind="ExternalInput")
    w_h = nc.dram_tensor("wcat", [C, QKVD], F32, kind="ExternalInput")
    ct_d = nc.dram_tensor("ropeC", [T, 128], F32, kind="ExternalInput").ap()
    sx_d = nc.dram_tensor("ropeSx", [T, 128], F32, kind="ExternalInput").ap()
    cst_d = nc.dram_tensor("cstb", [128, 128], F32, kind="ExternalInput").ap()
    oneb_d = nc.dram_tensor("onesb", [128, 128], BF16, kind="ExternalInput").ap()
    idb_d = nc.dram_tensor("idnb", [128, 128], BF16, kind="ExternalInput").ap()
    id3_d = nc.dram_tensor("idn32", [128, 128], F32, kind="ExternalInput").ap()
    tri_d = nc.dram_tensor("triT", [128, 128], BF16, kind="ExternalInput").ap()
    y_d = nc.dram_tensor("y", [T, NQ], BF16, kind="ExternalOutput").ap()
    qkv_h = nc.dram_tensor("qkv", [T, QKVD], BF16,
                           kind="ExternalOutput" if dbg_phase1 else "Internal")
    qkv_d = qkv_h.ap()
    nk = C // 128
    ntt = T // 128
    nog = QKVD // 512
    NG = T // 512
    NKB = T // 128
    SCL = 1.0 / math.sqrt(128.0)
    with ExitStack() as es:
        def sb(name, shape, dt):
            return es.enter_context(nc.sbuf_tensor(name, shape, dt))
        xf = sb("xf", [128, nk * T], BF16)
        wblk = sb("wblk", [128, nk * 512], BF16)
        ctall = sb("ctall", [128, 16 * 128], F32)
        sxall = sb("sxall", [128, 16 * 128], F32)
        evf = sb("evf", [128, 512], F32)
        rsc = sb("rsc", [128, 512], F32)
        outq = [sb(f"outq{i}", [128, 512], BF16) for i in range(2)]
        cstt = sb("cstt", [128, 128], F32)
        onesb = sb("onesb_t", [128, 128], BF16)
        idnb = sb("idnb_t", [128, 128], BF16)
        idn3 = sb("idn3_t", [128, 128], F32)
        trit = sb("trit", [128, 128], BF16)
        qh = xf[:, 0:T]
        kh = xf[:, T:2 * T]
        vh = xf[:, 2 * T:2 * T + NKB * 128]
        pt = xf[:, 4 * T:4 * T + NKB * 512]
        ofm = xf[:, 4 * T + NKB * 512:4 * T + NKB * 512 + 512]
        lrow = sb("lrow", [1, 512], F32)
        lcol = sb("lcol", [128, 4], F32)
        outb = [sb(f"outb{i}", [128, 128], BF16) for i in range(2)]
        ps_p = [es.enter_context(nc.psum_tensor(f"psp{i}", [128, 512], F32)) for i in range(2)]
        ps_s = [es.enter_context(nc.psum_tensor(f"pss{i}", [128, 512], F32)) for i in range(2)]
        ps_l = es.enter_context(nc.psum_tensor("psl", [128, 512], F32))
        ps_o = es.enter_context(nc.psum_tensor("pso", [128, 512], F32))
        ps_tr = es.enter_context(nc.psum_tensor("pstr", [128, 512], F32))
        ps_tb = es.enter_context(nc.psum_tensor("pstb", [128, 512], BF16))
        ld = es.enter_context(nc.semaphore("ld"))
        ldw = es.enter_context(nc.semaphore("ldw"))
        ldh = es.enter_context(nc.semaphore("ldh"))
        s_pm = es.enter_context(nc.semaphore("s_pm"))
        s_pe = es.enter_context(nc.semaphore("s_pe"))
        s_sm = es.enter_context(nc.semaphore("s_sm"))
        s_sg = es.enter_context(nc.semaphore("s_sg"))
        s_p = es.enter_context(nc.semaphore("s_p"))
        s_l = es.enter_context(nc.semaphore("s_l"))
        s_pv = es.enter_context(nc.semaphore("s_pv"))
        s_tr = es.enter_context(nc.semaphore("s_tr"))
        s_ev = es.enter_context(nc.semaphore("s_ev"))
        st = es.enter_context(nc.semaphore("st"))
        stq = es.enter_context(nc.semaphore("stq"))
        blk = es.enter_context(nc.Block())

        @blk.scalar
        def _(sc):
            sc.preload_act_table(AF.Exp)

        @blk.gpsimd
        def _(g):
            g.dma_start(out=cstt[:], in_=cst_d[:, :]).then_inc(ld, 16)
            g.dma_start(out=onesb[:], in_=oneb_d[:, :]).then_inc(ld, 16)
            g.dma_start(out=idnb[:], in_=idb_d[:, :]).then_inc(ld, 16)
            g.dma_start(out=idn3[:], in_=id3_d[:, :]).then_inc(ld, 16)
            g.dma_start(out=trit[:], in_=tri_d[:, :]).then_inc(ld, 16)

        @blk.sync
        def _(s):
            for kc in range(nk):
                s.dma_start_transpose(
                    out=xf[:, kc * T:(kc + 1) * T],
                    in_=xn_h.ap()[:, kc * 128:(kc + 1) * 128]).then_inc(ld, 16)

        @blk.gpsimd
        def _(g):
            g.dma_start(out=ctall.rearrange("p (tt j) -> p tt j", j=128),
                        in_=ct_d.rearrange("(tt p) j -> p tt j", p=128)).then_inc(ldh, 16)
            g.dma_start(out=sxall.rearrange("p (tt j) -> p tt j", j=128),
                        in_=sx_d.rearrange("(tt p) j -> p tt j", p=128)).then_inc(ldh, 16)

        ldw_n = [0]
        pm_n = [0]
        pe_n = [0]
        stq_n = [0]

        def load_w(og):
            @blk.gpsimd
            def _(g):
                if og > 0:
                    g.wait_ge(s_pm, og * ntt)
                src = bass.AP(w_h, og * 512, [[QKVD, 128], [128 * QKVD, nk], [1, 512]])
                g.dma_start(out=wblk.rearrange("p (kc n) -> p kc n", n=512),
                            in_=src).then_inc(ldw, 16)

        for og in range(nog):
            load_w(og)
            if layer == 0:
                is_rope = og < 8
            else:
                is_rope = og < 5
            for tt in range(ntt):
                pb = pm_n[0] % 2

                @blk.tensor
                def _(te):
                    if og == 0 and tt == 0:
                        te.wait_ge(ld, 80 + nk * 16)
                    te.wait_ge(ldw, (og + 1) * 16)
                    if pm_n[0] >= 2:
                        te.wait_ge(s_pe, pm_n[0] - 1)
                    for kc in range(nk):
                        mm = te.matmul(ps_p[pb][:],
                                       xf[:, kc * T + tt * 128:kc * T + (tt + 1) * 128],
                                       wblk[:, kc * 512:(kc + 1) * 512],
                                       start=(kc == 0), stop=(kc == nk - 1))
                        if kc == nk - 1:
                            mm.then_inc(s_pm, 1)
                pm_n[0] += 1

                if is_rope:
                    @blk.vector
                    def _(v):
                        v.wait_ge(s_pm, pm_n[0])
                        v.wait_ge(ldh, 32)
                        if pe_n[0] >= 2:
                            v.wait_ge(stq, (pe_n[0] - 1) * 16)
                        v.tensor_copy(evf[:], ps_p[pb][:])
                        v.drain()
                        crep = bass.AP(ctall, tt * 128, [[16 * 128, 128], [0, 4], [1, 128]])
                        swap = bass.AP(evf, 64, [[512, 128], [128, 4], [-64, 2], [1, 64]])
                        sxr = bass.AP(sxall, tt * 128, [[16 * 128, 128], [0, 4], [64, 2], [1, 64]])
                        v.tensor_tensor(
                            rsc[:].rearrange("p (h r j) -> p h r j", h=4, r=2),
                            swap, sxr, ALU.mult)
                        v.tensor_tensor(
                            evf[:].rearrange("p (h j) -> p h j", j=128),
                            evf[:].rearrange("p (h j) -> p h j", j=128),
                            crep, ALU.mult)
                        v.drain()
                        v.tensor_tensor(outq[pb][:], evf[:], rsc[:], ALU.add)
                        v.drain().then_inc(s_pe, 1)
                else:
                    @blk.vector
                    def _(v):
                        v.wait_ge(s_pm, pm_n[0])
                        if pe_n[0] >= 2:
                            v.wait_ge(stq, (pe_n[0] - 1) * 16)
                        v.tensor_copy(outq[pb][:], ps_p[pb][:])
                        v.drain().then_inc(s_pe, 1)
                pe_n[0] += 1

                @blk.sync
                def _(s2):
                    s2.wait_ge(s_pe, pe_n[0])
                    s2.dma_start(out=qkv_d[tt * 128:(tt + 1) * 128, og * 512:(og + 1) * 512],
                                 in_=outq[pb][:]).then_inc(stq, 16)
                stq_n[0] += 16

        sm_n = [0]
        sg_n = [0]
        p_n = [0]
        l_n = [0]
        pv_n = [0]
        tr_n = [0]
        ev_n = [0]
        st_n = [0]
        ldh_n = [32]

        if dbg_phase1:
            @blk.sync
            def _(s2):
                s2.wait_ge(stq, stq_n[0])
                s2.dma_start(out=y_d[0:128, 0:512], in_=outq[0][:]).then_inc(st, 16)
                s2.wait_ge(st, 16)
            es.close() if False else None
        for head in range(0 if not dbg_phase1 else 16, dbg_heads):
            qc = head * 128
            if layer == 0:
                kc0 = NQ + head * 128
                vc0 = NQ + NKV + head * 128
            else:
                kc0 = NQ + (head // 4) * 128
                vc0 = NQ + NKV + (head // 4) * 128

            @blk.sync
            def _(s2):
                if head == 0:
                    s2.wait_ge(stq, stq_n[0])
                else:
                    s2.wait_ge(s_pv, pv_n[0])
                s2.dma_start_transpose(out=qh[:], in_=qkv_d[:, qc:qc + 128]).then_inc(ldh, 16)
                s2.dma_start_transpose(out=kh[:], in_=qkv_d[:, kc0:kc0 + 128]).then_inc(ldh, 16)
                for kb in range(NKB):
                    s2.dma_start(out=vh[:, kb * 128:(kb + 1) * 128],
                                 in_=qkv_d[kb * 128:(kb + 1) * 128, vc0:vc0 + 128]).then_inc(ldh, 16)
            ldh_n[0] += (2 + NKB) * 16

            for g_ in range(NG):
                trips = 4 * (g_ + 1)
                for kb in range(trips):
                    pb = sm_n[0] % 2

                    @blk.tensor
                    def _(te):
                        te.wait_ge(ldh, ldh_n[0])
                        if sm_n[0] >= 2:
                            te.wait_ge(s_p, sm_n[0] - 1)
                        if kb == 0:
                            te.wait_ge(s_pv, pv_n[0])
                        te.matmul(ps_s[pb][:], kh[:, kb * 128:(kb + 1) * 128],
                                  qh[:, g_ * 512:(g_ + 1) * 512],
                                  start=True, stop=True).then_inc(s_sm, 1)
                    sm_n[0] += 1

                    @blk.scalar
                    def _(sc):
                        sc.wait_ge(s_sm, sm_n[0])
                        sc.activation(pt[:, kb * 512:(kb + 1) * 512], ps_s[pb][:],
                                      AF.Exp, bias=cstt[:, 0:1], scale=SCL)
                        sc.drain().then_inc(s_sg, 1)
                    sg_n[0] += 1

                    jstar = kb - 4 * g_

                    @blk.vector
                    def _(v):
                        v.wait_ge(s_sg, sg_n[0])
                        if jstar >= 0:
                            for j in range(4):
                                sl = pt[:, kb * 512 + j * 128: kb * 512 + (j + 1) * 128]
                                if j < jstar:
                                    v.memset(sl, 0.0)
                                elif j == jstar:
                                    v.tensor_tensor(sl, sl, trit[:], ALU.mult)
                        v.drain().then_inc(s_p, 1)
                    p_n[0] += 1

                @blk.tensor
                def _(te):
                    te.wait_ge(s_p, p_n[0])
                    te.wait_ge(s_l, max(0, l_n[0] - 2))
                    for kb in range(trips):
                        mm = te.matmul(ps_l[0:1, :], onesb[:, 0:1], pt[:, kb * 512:(kb + 1) * 512],
                                       start=(kb == 0), stop=(kb == trips - 1))
                        if kb == trips - 1:
                            mm.then_inc(s_l, 1)
                l_n[0] += 1

                @blk.vector
                def _(v):
                    v.wait_ge(s_l, l_n[0])
                    v.tensor_copy(lrow[:], ps_l[0:1, :])
                    v.drain()
                    v.reciprocal(lrow[:], lrow[:])
                    v.drain().then_inc(s_l, 1)
                l_n[0] += 1

                @blk.tensor
                def _(te):
                    te.wait_ge(s_l, l_n[0])
                    for j in range(4):
                        mm = te.transpose(ps_tr[:, j:j + 1], lrow[0:1, j * 128:(j + 1) * 128],
                                          idn3[0:1, 0:1])
                        if j == 3:
                            mm.then_inc(s_l, 1)
                l_n[0] += 1

                @blk.vector
                def _(v):
                    v.wait_ge(s_l, l_n[0])
                    v.tensor_copy(lcol[:], ps_tr[:, 0:4])
                    v.drain().then_inc(s_l, 1)
                l_n[0] += 1

                @blk.tensor
                def _(te):
                    te.wait_ge(s_p, p_n[0])
                    if pv_n[0] >= 1:
                        te.wait_ge(s_tr, tr_n[0])
                    for kb in range(trips):
                        mm = te.matmul(ps_o[:], vh[:, kb * 128:(kb + 1) * 128],
                                       pt[:, kb * 512:(kb + 1) * 512],
                                       start=(kb == 0), stop=(kb == trips - 1))
                        if kb == trips - 1:
                            mm.then_inc(s_pv, 1)
                pv_n[0] += 1

                @blk.vector
                def _(v):
                    v.wait_ge(s_pv, pv_n[0])
                    v.tensor_copy(ofm[:], ps_o[:])
                    v.drain().then_inc(s_pv, 1)
                pv_n[0] += 1

                @blk.tensor
                def _(te):
                    te.wait_ge(s_pv, pv_n[0])
                    if tr_n[0] >= 1:
                        te.wait_ge(s_ev, ev_n[0])
                    for j in range(4):
                        mm = te.transpose(ps_tb[:, j * 128:(j + 1) * 128],
                                          ofm[:, j * 128:(j + 1) * 128], idnb[:])
                        if j == 3:
                            mm.then_inc(s_tr, 1)
                tr_n[0] += 1

                for j in range(4):
                    @blk.vector
                    def _(v):
                        v.wait_ge(s_tr, tr_n[0])
                        if ev_n[0] >= 2:
                            v.wait_ge(st, (ev_n[0] - 1) * 16)
                        v.tensor_scalar(outb[ev_n[0] % 2][:], ps_tb[:, j * 128:(j + 1) * 128],
                                        lcol[:, j:j + 1], None, ALU.mult)
                        v.drain().then_inc(s_ev, 1)
                    ev_n[0] += 1

                    @blk.sync
                    def _(s2):
                        s2.wait_ge(s_ev, ev_n[0])
                        s2.dma_start(out=y_d[g_ * 512 + j * 128:g_ * 512 + (j + 1) * 128,
                                             qc:qc + 128],
                                     in_=outb[(ev_n[0] - 1) % 2][:]).then_inc(st, 16)
                    st_n[0] += 16
        @blk.sync
        def _(s2):
            s2.wait_ge(st, st_n[0])
    return nc


def _rope_tables(layer):
    pos = np.arange(T, dtype=np.float64)
    th = 1.0 / 10000.0 ** (np.arange(0, 128, 2, dtype=np.float64) / 128.0)
    ptab = np.outer(pos, th)
    c = np.cos(ptab).astype(np.float32)
    s = np.sin(ptab).astype(np.float32)
    Ct = np.concatenate([c, c], 1)
    Sx = np.concatenate([-s, s], 1)
    return Ct, Sx


def run_attn_qk(layer, xn_full, inputs, dbg_phase1=False, dbg_heads=16):
    nc = _cached(("attnqk", layer, dbg_phase1, dbg_heads),
                 lambda: build_attn_qk(layer, dbg_phase1, dbg_heads))
    Ct, Sx = _rope_tables(layer)
    cstb = np.zeros((128, 128), np.float32)
    cstb[:, 0] = -25.0
    onesb = np.zeros((128, 128), np.float32)
    onesb[:, 0] = 1.0
    onesb = onesb.astype(BF)
    idnb = np.eye(128, dtype=BF)
    idn32 = np.eye(128, dtype=np.float32)
    i_ = np.arange(128)
    triT = (i_[None, :] >= i_[:, None]).astype(BF)
    if layer == 0:
        wqT = np.asarray(inputs["L0_attn_wq"]).T
        wkT = np.asarray(inputs["L0_attn_wk"]).T
        wvT = np.asarray(inputs["L0_attn_wv"]).T
    else:
        WT = np.asarray(inputs["L2_attn_Wqkv"]).T
    maps = []
    for cix in range(NCORE):
        b_, half = cix // 2, cix % 2
        if layer == 0:
            wcat = np.concatenate([wqT[:, half * 2048:(half + 1) * 2048],
                                   wkT[:, half * 2048:(half + 1) * 2048],
                                   wvT[:, half * 2048:(half + 1) * 2048]], 1)
        else:
            q = WT[:, half * 2048:(half + 1) * 2048]
            k = WT[:, 4096 + half * 512:4096 + (half + 1) * 512]
            v = WT[:, 5120 + half * 512:5120 + (half + 1) * 512]
            wcat = np.concatenate([q, k, v], 1)
        maps.append({"xnb": np.ascontiguousarray(xn_full[b_]),
                     "wcat": np.ascontiguousarray(wcat.astype(np.float32)),
                     "ropeC": Ct, "ropeSx": Sx, "cstb": cstb, "onesb": onesb,
                     "idnb": idnb, "idn32": idn32, "triT": triT})
    res = run_bass_kernel_spmd(nc, maps, core_ids=list(range(NCORE)))
    if dbg_phase1:
        return [r["qkv"] for r in res.results]
    out = np.zeros((B, T, C), BF)
    for cix in range(NCORE):
        b_, half = cix // 2, cix % 2
        out[b_][:, half * 2048:(half + 1) * 2048] = res.results[cix]["y"]
    return out


def build_ffn1():
    nc = bass.Bass("TRN2", target_bir_lowering=False)
    x_d = nc.dram_tensor("x", [TPC, C], BF16, kind="ExternalInput").ap()
    w_h = nc.dram_tensor("w", [C, C], F32, kind="ExternalInput")
    b_d = nc.dram_tensor("b", [1, C], F32, kind="ExternalInput").ap()
    aux_h = nc.dram_tensor("waux", [C, 128], F32, kind="ExternalInput")
    baux_d = nc.dram_tensor("baux", [1, 68], F32, kind="ExternalInput").ap()
    bt_h = nc.dram_tensor("bT", [64, C], F32, kind="ExternalInput")
    one_d = nc.dram_tensor("onesr", [1, 128], F32, kind="ExternalInput").ap()
    idb_d = nc.dram_tensor("idnb", [128, 128], BF16, kind="ExternalInput").ap()
    y_d = nc.dram_tensor("y", [TPC, C], BF16, kind="ExternalOutput").ap()
    nt = TPC // 128
    nk = C // 128
    no = C // 512
    with ExitStack() as es:
        def sb(name, shape, dt):
            return es.enter_context(nc.sbuf_tensor(name, shape, dt))
        xf = sb("xf", [128, nk * TPC], BF16)
        wblk = [sb(f"wblk{i}", [128, nk * 512], BF16) for i in range(2)]
        auxb = sb("auxb", [128, nk * 128], BF16)
        btb = sb("btb", [64, C], BF16)
        bias = sb("bias", [1, C], F32)
        bauxb = sb("bauxb", [1, 68], F32)
        ones = sb("ones", [1, 128], F32)
        idnb = sb("idnb_t", [128, 128], BF16)
        us = sb("us", [128, 68], F32)
        msk = sb("msk", [128, 8], F32)
        umb = sb("umb", [128, 64], BF16)
        umT = sb("umT", [64, nt * 128], BF16)
        outb = [sb(f"outb{i}", [128, 512], BF16) for i in range(2)]
        ps_u = es.enter_context(nc.psum_tensor("psu", [128, 512], F32))
        ps_tr = es.enter_context(nc.psum_tensor("pstr", [128, 512], BF16))
        ps = [es.enter_context(nc.psum_tensor(f"ps{i}", [128, 512], F32)) for i in range(2)]
        ldx = es.enter_context(nc.semaphore("ldx"))
        ldw = es.enter_context(nc.semaphore("ldw"))
        s_um = es.enter_context(nc.semaphore("s_um"))
        s_mm = es.enter_context(nc.semaphore("s_mm"))
        s_ev = es.enter_context(nc.semaphore("s_ev"))
        st = es.enter_context(nc.semaphore("st"))
        blk = es.enter_context(nc.Block())

        @blk.gpsimd
        def _(g):
            g.dma_start(out=bias[:], in_=b_d[:, :]).then_inc(ldx, 16)
            g.dma_start(out=bauxb[:], in_=baux_d[:, :]).then_inc(ldx, 16)
            g.dma_start(out=ones[:], in_=one_d[:, :]).then_inc(ldx, 16)
            g.dma_start(out=idnb[:], in_=idb_d[:, :]).then_inc(ldx, 16)
            src = bass.AP(aux_h, 0, [[128, 128], [128 * 128, nk], [1, 128]])
            g.dma_start(out=auxb.rearrange("p (kc n) -> p kc n", n=128), in_=src).then_inc(ldx, 16)
            g.dma_start(out=btb[:], in_=bt_h.ap()[:, :]).then_inc(ldx, 16)

        @blk.sync
        def _(s):
            for kc in range(nk):
                s.dma_start_transpose(
                    out=xf[:, kc * TPC:(kc + 1) * TPC],
                    in_=x_d[:, kc * 128:(kc + 1) * 128]).then_inc(ldx, 16)
        ldx_all = 96 + nk * 16

        um_n = [0]
        for t in range(nt):
            @blk.tensor
            def _(te):
                te.wait_ge(ldx, ldx_all)
                if t > 0:
                    te.wait_ge(s_um, um_n[0])
                for kc in range(nk):
                    te.matmul(ps_u[:, :68], xf[:, kc * TPC + t * 128:kc * TPC + (t + 1) * 128],
                              auxb[:, kc * 128:kc * 128 + 68], start=(kc == 0), stop=False)
                te.matmul(ps_u[:, :68], ones[:, :], bauxb[:, :],
                          start=False, stop=True).then_inc(s_um, 1)
            um_n[0] += 1

            @blk.vector
            def _(v):
                v.wait_ge(s_um, um_n[0])
                v.tensor_copy(us[:], ps_u[:, :68])
                v.drain()
                v.tensor_reduce(msk[:, 0:1], us[:, 64:68], mybir.AxisListType.X, ALU.max)
                v.tensor_tensor(msk[:, 1:2], us[:, 64:65], us[:, 65:66], ALU.max)
                v.drain()
                v.tensor_tensor(msk[:, 2:3], msk[:, 1:2], us[:, 66:67], ALU.max)
                v.drain()
                v.tensor_scalar(msk[:, 3:4], us[:, 64:65], msk[:, 0:1], None, ALU.is_lt)
                v.tensor_scalar(msk[:, 4:5], msk[:, 1:2], msk[:, 0:1], None, ALU.is_lt)
                v.tensor_scalar(msk[:, 5:6], msk[:, 2:3], msk[:, 0:1], None, ALU.is_lt)
                v.drain()
                v.tensor_scalar(us[:, 8:16], us[:, 8:16], msk[:, 3:4], None, ALU.mult)
                v.tensor_scalar(us[:, 16:32], us[:, 16:32], msk[:, 4:5], None, ALU.mult)
                v.tensor_scalar(us[:, 32:64], us[:, 32:64], msk[:, 5:6], None, ALU.mult)
                v.drain()
                v.tensor_copy(umb[:], us[:, 0:64])
                v.drain().then_inc(s_um, 1)
            um_n[0] += 1

            @blk.tensor
            def _(te):
                te.wait_ge(s_um, um_n[0])
                te.transpose(ps_tr[0:64, 0:128], umb[:], idnb[:]).then_inc(s_um, 1)
            um_n[0] += 1

            @blk.vector
            def _(v):
                v.wait_ge(s_um, um_n[0])
                v.tensor_copy(umT[:, t * 128:(t + 1) * 128], ps_tr[0:64, 0:128])
                v.drain().then_inc(s_um, 1)
            um_n[0] += 1

        mm_n = [0]
        ev_n = [0]
        st_n = [0]

        def load_w(og, buf):
            @blk.gpsimd
            def _(g):
                if og >= 2:
                    g.wait_ge(s_mm, (og - 1) * nt)
                src = bass.AP(w_h, og * 512, [[C, 128], [128 * C, nk], [1, 512]])
                g.dma_start(out=wblk[buf].rearrange("p (kc n) -> p kc n", n=512),
                            in_=src).then_inc(ldw, 16)

        load_w(0, 0)
        load_w(1, 1)
        for og in range(no):
            buf = og % 2
            for t in range(nt):
                pb = mm_n[0] % 2

                @blk.tensor
                def _(te):
                    te.wait_ge(ldw, (og + 1) * 16)
                    if og == 0 and t == 0:
                        te.wait_ge(s_um, um_n[0])
                    if mm_n[0] >= 2:
                        te.wait_ge(s_ev, mm_n[0] - 1)
                    for kc in range(nk):
                        te.matmul(ps[pb][:], xf[:, kc * TPC + t * 128:kc * TPC + (t + 1) * 128],
                                  wblk[buf][:, kc * 512:(kc + 1) * 512],
                                  start=(kc == 0), stop=False)
                    te.matmul(ps[pb][:], umT[:, t * 128:(t + 1) * 128],
                              btb[:, og * 512:(og + 1) * 512], start=False, stop=False)
                    te.matmul(ps[pb][:], ones[:, :], bias[:, og * 512:(og + 1) * 512],
                              start=False, stop=True).then_inc(s_mm, 1)
                mm_n[0] += 1

                @blk.vector
                def _(v):
                    v.wait_ge(s_mm, mm_n[0])
                    if ev_n[0] >= 2:
                        v.wait_ge(st, (ev_n[0] - 1) * 16)
                    v.tensor_copy(outb[pb][:], ps[pb][:])
                    v.drain().then_inc(s_ev, 1)
                ev_n[0] += 1

                @blk.sync
                def _(s):
                    s.wait_ge(s_ev, ev_n[0])
                    s.dma_start(out=y_d[t * 128:(t + 1) * 128, og * 512:(og + 1) * 512],
                                in_=outb[pb][:]).then_inc(st, 16)
                st_n[0] += 16
            if og + 2 < no:
                load_w(og + 2, buf)
        @blk.sync
        def _(s):
            s.wait_ge(st, st_n[0])
    return nc


def run_ffn1(xs, inputs):
    nc = _cached(("ffn1",), build_ffn1)
    A = np.asarray(inputs["L1_ffn_A"]).astype(np.float32)
    gw = np.asarray(inputs["L1_ffn_gate_w"]).astype(np.float32)
    waux = np.concatenate([A.T, gw.T, np.zeros((C, 60), np.float32)], 1)
    baux = np.zeros((1, 68), np.float32)
    baux[0, 64:] = np.asarray(inputs["L1_ffn_gate_b"])
    m = {
        "w": np.ascontiguousarray(np.asarray(inputs["L1_ffn_W"]).astype(np.float32).T),
        "b": np.asarray(inputs["L1_ffn_b"]).reshape(1, C).astype(np.float32),
        "waux": np.ascontiguousarray(waux),
        "baux": baux,
        "bT": np.ascontiguousarray(np.asarray(inputs["L1_ffn_B"]).astype(np.float32).T),
        "onesr": np.ones((1, 128), np.float32),
        "idnb": np.eye(128, dtype=BF),
    }
    maps = [dict(m, x=xs[i]) for i in range(NCORE)]
    res = run_bass_kernel_spmd(nc, maps, core_ids=list(range(NCORE)))
    return [r["y"] for r in res.results]


def build_ffn2():
    nc = bass.Bass("TRN2", target_bir_lowering=False)
    x_d = nc.dram_tensor("x", [TPC, C], BF16, kind="ExternalInput").ap()
    w_h = nc.dram_tensor("w", [C, C], F32, kind="ExternalInput")
    b_d = nc.dram_tensor("b", [1, C], F32, kind="ExternalInput").ap()
    aux_h = nc.dram_tensor("waux", [C, 128], F32, kind="ExternalInput")
    cor_d = nc.dram_tensor("coresF", [128, 16], F32, kind="ExternalInput").ap()
    bw_d = nc.dram_tensor("bwT", [16, C], F32, kind="ExternalInput").ap()
    one_d = nc.dram_tensor("onesr", [1, 128], F32, kind="ExternalInput").ap()
    idb_d = nc.dram_tensor("idnb", [128, 128], BF16, kind="ExternalInput").ap()
    y_d = nc.dram_tensor("y", [TPC, C], BF16, kind="ExternalOutput").ap()
    nt = TPC // 128
    nk = C // 128
    no = C // 512
    with ExitStack() as es:
        def sb(name, shape, dt):
            return es.enter_context(nc.sbuf_tensor(name, shape, dt))
        xf = sb("xf", [128, nk * TPC], BF16)
        wblk = [sb(f"wblk{i}", [128, nk * 512], BF16) for i in range(2)]
        auxb = sb("auxb", [128, nk * 128], BF16)
        corb = sb("corb", [128, 16], BF16)
        bwb = sb("bwb", [16, C], BF16)
        bias = sb("bias", [1, C], F32)
        ones = sb("ones", [1, 128], F32)
        idnb = sb("idnb_t", [128, 128], BF16)
        us = sb("us", [128, 24], F32)
        pr = sb("pr", [128, 8], F32)
        nm = sb("nm", [128, 2], F32)
        opb = sb("opb", [128, 128], BF16)
        mxb = sb("mxb", [128, 16], BF16)
        mxT = sb("mxT", [16, nt * 128], BF16)
        outb = [sb(f"outb{i}", [128, 512], BF16) for i in range(2)]
        ps_u = es.enter_context(nc.psum_tensor("psu", [128, 512], F32))
        ps_tr = es.enter_context(nc.psum_tensor("pstr", [128, 512], BF16))
        ps_m = es.enter_context(nc.psum_tensor("psm", [128, 512], F32))
        ps = [es.enter_context(nc.psum_tensor(f"ps{i}", [128, 512], F32)) for i in range(2)]
        ldx = es.enter_context(nc.semaphore("ldx"))
        ldw = es.enter_context(nc.semaphore("ldw"))
        s_um = es.enter_context(nc.semaphore("s_um"))
        s_mm = es.enter_context(nc.semaphore("s_mm"))
        s_ev = es.enter_context(nc.semaphore("s_ev"))
        st = es.enter_context(nc.semaphore("st"))
        blk = es.enter_context(nc.Block())

        @blk.scalar
        def _(sc):
            sc.preload_act_table(AF.Exp)

        @blk.gpsimd
        def _(g):
            g.dma_start(out=bias[:], in_=b_d[:, :]).then_inc(ldx, 16)
            g.dma_start(out=ones[:], in_=one_d[:, :]).then_inc(ldx, 16)
            g.dma_start(out=idnb[:], in_=idb_d[:, :]).then_inc(ldx, 16)
            g.dma_start(out=corb[:], in_=cor_d[:, :]).then_inc(ldx, 16)
            g.dma_start(out=bwb[:], in_=bw_d[:, :]).then_inc(ldx, 16)
            src = bass.AP(aux_h, 0, [[128, 128], [128 * 128, nk], [1, 128]])
            g.dma_start(out=auxb.rearrange("p (kc n) -> p kc n", n=128), in_=src).then_inc(ldx, 16)

        @blk.sync
        def _(s):
            for kc in range(nk):
                s.dma_start_transpose(
                    out=xf[:, kc * TPC:(kc + 1) * TPC],
                    in_=x_d[:, kc * 128:(kc + 1) * 128]).then_inc(ldx, 16)
        ldx_all = 96 + nk * 16

        um_n = [0]
        for t in range(nt):
            @blk.tensor
            def _(te):
                te.wait_ge(ldx, ldx_all)
                if t > 0:
                    te.wait_ge(s_um, um_n[0])
                for kc in range(nk):
                    te.matmul(ps_u[:, :24], xf[:, kc * TPC + t * 128:kc * TPC + (t + 1) * 128],
                              auxb[:, kc * 128:kc * 128 + 24],
                              start=(kc == 0), stop=(kc == nk - 1)).then_inc(s_um, 1) \
                        if kc == nk - 1 else te.matmul(
                            ps_u[:, :24], xf[:, kc * TPC + t * 128:kc * TPC + (t + 1) * 128],
                            auxb[:, kc * 128:kc * 128 + 24], start=(kc == 0), stop=False)
            um_n[0] += 1

            @blk.vector
            def _(v):
                v.wait_ge(s_um, um_n[0])
                v.tensor_copy(us[:], ps_u[:, :24])
                v.drain()
                v.tensor_reduce(nm[:, 0:1], us[:, 16:24], mybir.AxisListType.X, ALU.max,
                                negate=True)
                v.drain().then_inc(s_um, 1)
            um_n[0] += 1

            @blk.scalar
            def _(sc):
                sc.wait_ge(s_um, um_n[0])
                sc.activation(pr[:], us[:, 16:24], AF.Exp, bias=nm[:, 0:1],
                              accum_out=nm[:, 1:2])
                sc.drain().then_inc(s_um, 1)
            um_n[0] += 1

            @blk.vector
            def _(v):
                v.wait_ge(s_um, um_n[0])
                v.reciprocal(nm[:, 1:2], nm[:, 1:2])
                v.drain()
                v.tensor_scalar(pr[:], pr[:], nm[:, 1:2], None, ALU.mult)
                v.drain()
                for e in range(8):
                    v.tensor_scalar(opb[:, e * 16:(e + 1) * 16], us[:, 0:16],
                                    pr[:, e:e + 1], None, ALU.mult)
                v.drain().then_inc(s_um, 1)
            um_n[0] += 1

            @blk.tensor
            def _(te):
                te.wait_ge(s_um, um_n[0])
                te.transpose(ps_tr[:, 0:128], opb[:], idnb[:]).then_inc(s_um, 1)
            um_n[0] += 1

            @blk.vector
            def _(v):
                v.wait_ge(s_um, um_n[0])
                v.tensor_copy(opb[:], ps_tr[:, 0:128])
                v.drain().then_inc(s_um, 1)
            um_n[0] += 1

            @blk.tensor
            def _(te):
                te.wait_ge(s_um, um_n[0])
                te.matmul(ps_m[:, :16], opb[:], corb[:], start=True, stop=True).then_inc(s_um, 1)
            um_n[0] += 1

            @blk.vector
            def _(v):
                v.wait_ge(s_um, um_n[0])
                v.tensor_scalar(mxb[:], ps_m[:, :16], 2.0, None, ALU.mult)
                v.drain().then_inc(s_um, 1)
            um_n[0] += 1

            @blk.tensor
            def _(te):
                te.wait_ge(s_um, um_n[0])
                te.transpose(ps_tr[0:16, 0:128], mxb[:], idnb[:]).then_inc(s_um, 1)
            um_n[0] += 1

            @blk.vector
            def _(v):
                v.wait_ge(s_um, um_n[0])
                v.tensor_copy(mxT[:, t * 128:(t + 1) * 128], ps_tr[0:16, 0:128])
                v.drain().then_inc(s_um, 1)
            um_n[0] += 1

        mm_n = [0]
        ev_n = [0]
        st_n = [0]

        def load_w(og, buf):
            @blk.gpsimd
            def _(g):
                if og >= 2:
                    g.wait_ge(s_mm, (og - 1) * nt)
                src = bass.AP(w_h, og * 512, [[C, 128], [128 * C, nk], [1, 512]])
                g.dma_start(out=wblk[buf].rearrange("p (kc n) -> p kc n", n=512),
                            in_=src).then_inc(ldw, 16)

        load_w(0, 0)
        load_w(1, 1)
        for og in range(no):
            buf = og % 2
            for t in range(nt):
                pb = mm_n[0] % 2

                @blk.tensor
                def _(te):
                    te.wait_ge(ldw, (og + 1) * 16)
                    if og == 0 and t == 0:
                        te.wait_ge(s_um, um_n[0])
                    if mm_n[0] >= 2:
                        te.wait_ge(s_ev, mm_n[0] - 1)
                    for kc in range(nk):
                        te.matmul(ps[pb][:], xf[:, kc * TPC + t * 128:kc * TPC + (t + 1) * 128],
                                  wblk[buf][:, kc * 512:(kc + 1) * 512],
                                  start=(kc == 0), stop=False)
                    te.matmul(ps[pb][:], mxT[:, t * 128:(t + 1) * 128],
                              bwb[:, og * 512:(og + 1) * 512], start=False, stop=False)
                    te.matmul(ps[pb][:], ones[:, :], bias[:, og * 512:(og + 1) * 512],
                              start=False, stop=True).then_inc(s_mm, 1)
                mm_n[0] += 1

                @blk.vector
                def _(v):
                    v.wait_ge(s_mm, mm_n[0])
                    if ev_n[0] >= 2:
                        v.wait_ge(st, (ev_n[0] - 1) * 16)
                    v.tensor_copy(outb[pb][:], ps[pb][:])
                    v.drain().then_inc(s_ev, 1)
                ev_n[0] += 1

                @blk.sync
                def _(s):
                    s.wait_ge(s_ev, ev_n[0])
                    s.dma_start(out=y_d[t * 128:(t + 1) * 128, og * 512:(og + 1) * 512],
                                in_=outb[pb][:]).then_inc(st, 16)
                st_n[0] += 16
            if og + 2 < no:
                load_w(og + 2, buf)
        @blk.sync
        def _(s):
            s.wait_ge(st, st_n[0])
    return nc


def run_ffn2(xs, inputs):
    nc = _cached(("ffn2",), build_ffn2)
    Aw = np.asarray(inputs["L2_ffn_A_w"]).astype(np.float32)
    rw = np.asarray(inputs["L2_ffn_router_w"]).astype(np.float32)
    waux = np.concatenate([Aw.T, rw.T, np.zeros((C, 104), np.float32)], 1)
    cores = np.asarray(inputs["L2_ffn_cores"]).astype(np.float32)
    m = {
        "w": np.ascontiguousarray(np.asarray(inputs["L2_ffn_W_base"]).astype(np.float32).T),
        "b": np.asarray(inputs["L2_ffn_b_base"]).reshape(1, C).astype(np.float32),
        "waux": np.ascontiguousarray(waux),
        "coresF": np.ascontiguousarray(cores.reshape(128, 16)),
        "bwT": np.ascontiguousarray(np.asarray(inputs["L2_ffn_B_w"]).astype(np.float32).T),
        "onesr": np.ones((1, 128), np.float32),
        "idnb": np.eye(128, dtype=BF),
    }
    maps = [dict(m, x=xs[i]) for i in range(NCORE)]
    res = run_bass_kernel_spmd(nc, maps, core_ids=list(range(NCORE)))
    return [r["y"] for r in res.results]


def build_attn3():
    nc = bass.Bass("TRN2", target_bir_lowering=False)
    x_d = nc.dram_tensor("x", [TPC, C], BF16, kind="ExternalInput").ap()
    wq_h = nc.dram_tensor("wqselT", [C, 1024], F32, kind="ExternalInput")
    bq_d = nc.dram_tensor("bqsel", [1, 1024], F32, kind="ExternalInput").ap()
    mz_d = nc.dram_tensor("mzcat", [128, 8 * 129], F32, kind="ExternalInput").ap()
    wo_h = nc.dram_tensor("woT", [C, C], F32, kind="ExternalInput")
    one_d = nc.dram_tensor("onesr", [1, 128], F32, kind="ExternalInput").ap()
    idb_d = nc.dram_tensor("idnb", [128, 128], BF16, kind="ExternalInput").ap()
    y_d = nc.dram_tensor("y", [TPC, C], BF16, kind="ExternalOutput").ap()
    nt = TPC // 128
    nk = C // 128
    no = C // 512
    EPS3 = 1e-6
    with ExitStack() as es:
        def sb(name, shape, dt):
            return es.enter_context(nc.sbuf_tensor(name, shape, dt))
        xf = sb("xf", [128, nk * TPC], BF16)
        wqb = sb("wqb", [128, nk * 1024], BF16)
        wblk = [sb("wblk0", [128, nk * 512], BF16)] * 2
        mzb = sb("mzb", [128, 8 * 129 + 8], BF16)
        bq = sb("bq", [1, 1024], F32)
        ones = sb("ones", [1, 128], F32)
        idnb = sb("idnb_t", [128, 128], BF16)
        sqf = sb("sqf", [128, 1024], F32)
        sqm = sb("sqm", [128, 1024], F32)
        sqb = sb("sqb", [128, 1024], BF16)
        den = sb("den", [128, 1], F32)
        rfm = sb("rfm", [128, nt * 1024], BF16)
        rtm = sb("rtm", [128, 128], BF16)
        outb = [sb(f"outb{i}", [128, 512], BF16) for i in range(2)]
        ps_q = es.enter_context(nc.psum_tensor("psq", [128, 512], F32))
        ps_r = es.enter_context(nc.psum_tensor("psr", [128, 512], F32))
        ps_tr = es.enter_context(nc.psum_tensor("pstr", [128, 512], BF16))
        ps = [es.enter_context(nc.psum_tensor(f"ps{i}", [128, 512], F32)) for i in range(2)]
        ldx = es.enter_context(nc.semaphore("ldx"))
        ldw = es.enter_context(nc.semaphore("ldw"))
        s_q = es.enter_context(nc.semaphore("s_q"))
        s_mm = es.enter_context(nc.semaphore("s_mm"))
        s_ev = es.enter_context(nc.semaphore("s_ev"))
        st = es.enter_context(nc.semaphore("st"))
        blk = es.enter_context(nc.Block())

        @blk.scalar
        def _(sc):
            sc.preload_act_table(AF.Exp)

        @blk.gpsimd
        def _(g):
            g.dma_start(out=bq[:], in_=bq_d[:, :]).then_inc(ldx, 16)
            g.dma_start(out=ones[:], in_=one_d[:, :]).then_inc(ldx, 16)
            g.dma_start(out=idnb[:], in_=idb_d[:, :]).then_inc(ldx, 16)
            g.dma_start(out=mzb[:, :8 * 129], in_=mz_d[:, :]).then_inc(ldx, 16)
            src = bass.AP(wq_h, 0, [[1024, 128], [128 * 1024, nk], [1, 1024]])
            g.dma_start(out=wqb.rearrange("p (kc n) -> p kc n", n=1024), in_=src).then_inc(ldx, 16)

        @blk.sync
        def _(s):
            for kc in range(nk):
                s.dma_start_transpose(
                    out=xf[:, kc * TPC:(kc + 1) * TPC],
                    in_=x_d[:, kc * 128:(kc + 1) * 128]).then_inc(ldx, 16)
        ldx_all = 80 + nk * 16

        q_n = [0]
        for t in range(nt):
            for half in range(2):
                @blk.tensor
                def _(te):
                    te.wait_ge(ldx, ldx_all)
                    if t > 0 and half == 0:
                        te.wait_ge(s_q, q_n[0])
                    elif half == 1:
                        te.wait_ge(s_q, q_n[0] - 1)
                    for kc in range(nk):
                        te.matmul(ps_q[:], xf[:, kc * TPC + t * 128:kc * TPC + (t + 1) * 128],
                                  wqb[:, kc * 1024 + half * 512:kc * 1024 + (half + 1) * 512],
                                  start=(kc == 0), stop=False)
                    te.matmul(ps_q[:], ones[:, :], bq[:, half * 512:(half + 1) * 512],
                              start=False, stop=True).then_inc(s_q, 1)
                q_n[0] += 1

                @blk.scalar
                def _(sc):
                    sc.wait_ge(s_q, q_n[0])
                    sc.activation(sqf[:, half * 512:(half + 1) * 512], ps_q[:], AF.Relu)
                    sc.drain().then_inc(s_q, 1)
                q_n[0] += 1

                @blk.vector
                def _(v):
                    v.wait_ge(s_q, q_n[0])
                    v.tensor_scalar(sqm[:, half * 512:(half + 1) * 512], ps_q[:],
                                    0.0, None, ALU.min)
                    v.drain().then_inc(s_q, 1)
                q_n[0] += 1

                @blk.scalar
                def _(sc):
                    sc.wait_ge(s_q, q_n[0])
                    sc.activation(sqm[:, half * 512:(half + 1) * 512],
                                  sqm[:, half * 512:(half + 1) * 512], AF.Exp)
                    sc.drain().then_inc(s_q, 1)
                q_n[0] += 1

            @blk.vector
            def _(v):
                v.wait_ge(s_q, q_n[0])
                v.tensor_tensor(sqb[:], sqf[:], sqm[:], ALU.add)
                v.drain().then_inc(s_q, 1)
            q_n[0] += 1

            for j in range(8):
                @blk.tensor
                def _(te):
                    te.wait_ge(s_q, q_n[0])
                    te.transpose(ps_tr[:, 0:128], sqb[:, j * 128:(j + 1) * 128],
                                 idnb[:]).then_inc(s_q, 1)
                q_n[0] += 1

                @blk.vector
                def _(v):
                    v.wait_ge(s_q, q_n[0])
                    v.tensor_copy(rtm[:], ps_tr[:, 0:128])
                    v.drain().then_inc(s_q, 1)
                q_n[0] += 1

                @blk.tensor
                def _(te):
                    te.wait_ge(s_q, q_n[0])
                    te.matmul(ps_r[:, :129], rtm[:], mzb[:, j * 129:(j + 1) * 129],
                              start=True, stop=True).then_inc(s_q, 1)
                q_n[0] += 1

                @blk.vector
                def _(v):
                    v.wait_ge(s_q, q_n[0])
                    v.tensor_scalar(den[:], ps_r[:, 128:129], EPS3, None, ALU.add)
                    v.drain()
                    v.reciprocal(den[:], den[:])
                    v.drain()
                    v.tensor_scalar(rtm[:], ps_r[:, 0:128], den[:], None, ALU.mult)
                    v.drain().then_inc(s_q, 1)
                q_n[0] += 1

                @blk.tensor
                def _(te):
                    te.wait_ge(s_q, q_n[0])
                    te.transpose(ps_tr[:, 0:128], rtm[:], idnb[:]).then_inc(s_q, 1)
                q_n[0] += 1

                @blk.vector
                def _(v):
                    v.wait_ge(s_q, q_n[0])
                    v.tensor_copy(rfm[:, t * 1024 + j * 128:t * 1024 + (j + 1) * 128],
                                  ps_tr[:, 0:128])
                    v.drain().then_inc(s_q, 1)
                q_n[0] += 1

        mm_n = [0]
        ev_n = [0]
        st_n = [0]

        def load_w(og, buf):
            @blk.gpsimd
            def _(g):
                if og >= 1:
                    g.wait_ge(s_mm, og * nt)
                src = bass.AP(wo_h, og * 512, [[C, 128], [128 * C, nk], [1, 512]])
                g.dma_start(out=wblk[buf].rearrange("p (kc n) -> p kc n", n=512),
                            in_=src).then_inc(ldw, 16)

        load_w(0, 0)
        for og in range(no):
            buf = og % 2
            for t in range(nt):
                pb = mm_n[0] % 2

                @blk.tensor
                def _(te):
                    te.wait_ge(ldw, (og + 1) * 16)
                    if og == 0 and t == 0:
                        te.wait_ge(s_q, q_n[0])
                    if mm_n[0] >= 2:
                        te.wait_ge(s_ev, mm_n[0] - 1)
                    for kc in range(nk):
                        mm = te.matmul(ps[pb][:],
                                       rfm[:, t * 1024 + (kc // 4) * 128:t * 1024 + (kc // 4 + 1) * 128],
                                       wblk[buf][:, kc * 512:(kc + 1) * 512],
                                       start=(kc == 0), stop=(kc == nk - 1))
                        if kc == nk - 1:
                            mm.then_inc(s_mm, 1)
                mm_n[0] += 1

                @blk.vector
                def _(v):
                    v.wait_ge(s_mm, mm_n[0])
                    if ev_n[0] >= 2:
                        v.wait_ge(st, (ev_n[0] - 1) * 16)
                    v.tensor_copy(outb[pb][:], ps[pb][:])
                    v.drain().then_inc(s_ev, 1)
                ev_n[0] += 1

                @blk.sync
                def _(s):
                    s.wait_ge(s_ev, ev_n[0])
                    s.dma_start(out=y_d[t * 128:(t + 1) * 128, og * 512:(og + 1) * 512],
                                in_=outb[pb][:]).then_inc(st, 16)
                st_n[0] += 16
            if og + 1 < no:
                load_w(og + 1, buf)
        @blk.sync
        def _(s):
            s.wait_ge(st, st_n[0])
    return nc


def run_attn3(xs, inputs):
    nc = _cached(("attn3",), build_attn3)
    Wq = np.asarray(inputs["L3_attn_Wq"]).astype(np.float32)
    bq = np.asarray(inputs["L3_attn_bq"]).astype(np.float32)
    sel = np.concatenate([np.arange(h * 128, (h + 1) * 128) for h in range(0, 32, 4)])
    wqselT = np.ascontiguousarray(Wq[sel].T)
    bqsel = bq[sel].reshape(1, 1024)
    M = np.asarray(inputs["L3_attn_M"]).astype(np.float32)
    z = np.asarray(inputs["L3_attn_z"]).astype(np.float32)
    mz = np.concatenate([np.concatenate([M[j], z[j][:, None]], 1) for j in range(8)], 1)
    m = {
        "wqselT": wqselT, "bqsel": bqsel,
        "mzcat": np.ascontiguousarray(mz),
        "woT": np.ascontiguousarray(np.asarray(inputs["L3_attn_Wo"]).astype(np.float32).T),
        "onesr": np.ones((1, 128), np.float32),
        "idnb": np.eye(128, dtype=BF),
    }
    maps = [dict(m, x=xs[i]) for i in range(NCORE)]
    res = run_bass_kernel_spmd(nc, maps, core_ids=list(range(NCORE)))
    return [r["y"] for r in res.results]


def _prep_maps(inputs):
    def pad(qT, rows):
        qT = np.ascontiguousarray(qT)
        if qT.shape[0] < rows:
            qT = np.concatenate([qT, np.zeros((rows - qT.shape[0],) + qT.shape[1:], qT.dtype)], 0)
        return qT

    def tpad(q, rows):
        return pad(np.asarray(q).T, rows)
    gqT = tpad(inputs["L0_ffn_gate_qweight"], 11264)
    gsT = tpad(inputs["L0_ffn_gate_scales"], 11264)
    gzR = pad(np.repeat(np.asarray(inputs["L0_ffn_gate_zeros"]).T, 8, axis=0), 11264)
    uqT = tpad(inputs["L0_ffn_up_qweight"], 11264)
    usT = tpad(inputs["L0_ffn_up_scales"], 11264)
    uzR = pad(np.repeat(np.asarray(inputs["L0_ffn_up_zeros"]).T, 8, axis=0), 11264)
    dqT = np.ascontiguousarray(np.asarray(inputs["L0_ffn_down_qweight"]).T)
    dsT = np.ascontiguousarray(np.asarray(inputs["L0_ffn_down_scales"]).T)
    dzR = np.ascontiguousarray(np.repeat(np.asarray(inputs["L0_ffn_down_zeros"]).T, 8, axis=0))
    wpr = np.asarray(inputs["L3_ffn_weight_packed"]).reshape(4096, 2048)
    l3sz = np.zeros((4096, 128), np.float32)
    l3sz[:, 0] = np.asarray(inputs["L3_ffn_scale"]).reshape(-1)
    l3sz[:, 1] = np.asarray(inputs["L3_ffn_zero_point"]).reshape(-1)
    shf = np.zeros((128, 128), np.int32)
    shf[:, 0] = 4 * (np.arange(128) % 8)
    idn = np.eye(128, dtype=BF)
    maps = []
    for c in range(NCORE):
        g0, g1 = c * 1408, (c + 1) * 1408
        d0, d1 = c * 512, (c + 1) * 512
        maps.append({
            "gq": np.ascontiguousarray(gqT[g0:g1]), "gs": np.ascontiguousarray(gsT[g0:g1]),
            "gz": np.ascontiguousarray(gzR[g0:g1]),
            "uq": np.ascontiguousarray(uqT[g0:g1]), "us": np.ascontiguousarray(usT[g0:g1]),
            "uz": np.ascontiguousarray(uzR[g0:g1]),
            "dq": np.ascontiguousarray(dqT[d0:d1]), "ds": np.ascontiguousarray(dsT[d0:d1]),
            "dz": np.ascontiguousarray(dzR[d0:d1]),
            "wp": np.ascontiguousarray(wpr[d0:d1]), "l3sz": np.ascontiguousarray(l3sz[d0:d1]),
            "shf": shf, "idn": idn,
        })
    return maps


def run_prep(inputs):
    nc = _cached(("prep",), build_prep)
    maps = _prep_maps(inputs)
    res = run_bass_kernel_spmd(nc, maps, core_ids=list(range(NCORE)))

    def unscramble(a, K_):
        nt = a.shape[0] // 128
        return a.reshape(nt, 128, K_ // 128, 128).transpose(2, 1, 0, 3).reshape(K_, nt * 128)
    GWT = np.concatenate([unscramble(r["gw"], 4096) for r in res.results], 1)
    UWT = np.concatenate([unscramble(r["uw"], 4096) for r in res.results], 1)
    DWT = np.concatenate([unscramble(r["dw"], 11008) for r in res.results], 1)
    L3WT = np.concatenate([unscramble(r["l3w"], 4096) for r in res.results], 1)
    return GWT, UWT, DWT, L3WT


def run_proj(xs, w, bias=None, w_mode="f32T", out_dt=BF16, K=C, O=C):
    nc = _cached(("proj", K, O, w_mode, bias is not None, str(out_dt)),
                 lambda: build_proj(K, O, w_mode, bias is not None, out_dt))
    maps = []
    for i in range(NCORE):
        m = {"x": xs[i], "w": w}
        if bias is not None:
            m["b"] = bias.reshape(1, O).astype(np.float32)
            m["onesr"] = np.ones((1, 128), np.float32)
        maps.append(m)
    res = run_bass_kernel_spmd(nc, maps, core_ids=list(range(NCORE)))
    return [r["y"] for r in res.results]


def run_ln(xs, as_=None, fs=None, emit_xn=True):
    wb = as_ is not None
    f_dtype = BF16
    if wb and fs[0].dtype == np.float16:
        f_dtype = F16
    nc = _cached(("ln", wb, emit_xn, str(None if not wb else fs[0].dtype)),
                 lambda: build_ln(wb, emit_xn, f_dtype))
    cst = np.full((128, 128), LNEPS, np.float32)
    maps = []
    for i in range(NCORE):
        m = {"x": xs[i], "cst": cst}
        if wb:
            m["a"] = as_[i]
            m["f"] = fs[i]
        maps.append(m)
    res = run_bass_kernel_spmd(nc, maps, core_ids=list(range(NCORE)))
    xo = [r.get("xo") for r in res.results] if wb else xs
    xn = [r.get("xn") for r in res.results] if emit_xn else None
    return xo, xn


def _shard_rows(full):
    return [np.ascontiguousarray(full[c // 2, (c % 2) * TPC:(c % 2 + 1) * TPC])
            for c in range(NCORE)]


def _join_rows(parts, dtype):
    out = np.empty((B, T) + parts[0].shape[1:], dtype)
    for c in range(NCORE):
        out[c // 2, (c % 2) * TPC:(c % 2 + 1) * TPC] = parts[c]
    return out


def kernel(**inputs):
    inputs = {k_: np.asarray(v) for k_, v in inputs.items()}
    x0 = inputs["L0_attn_x"].astype(np.float32)
    xs = _shard_rows(x0)

    GWT, UWT, DWT, L3WT = run_prep(inputs)
    GWT = np.ascontiguousarray(GWT)
    UWT = np.ascontiguousarray(UWT)
    DWT = np.ascontiguousarray(DWT)
    L3WT = np.ascontiguousarray(L3WT)

    _, xn = run_ln(xs, emit_xn=True)
    xn_full = _join_rows(xn, BF)
    O0 = run_attn_qk(0, xn_full, inputs)
    a0 = run_proj(_shard_rows(O0),
                  np.ascontiguousarray(inputs["L0_attn_wo"].astype(np.float32).T))
    f0 = run_ffn0(xn, GWT, UWT, DWT)

    xs1, xn1 = run_ln(xs, a0, f0)
    xn1_full = _join_rows(xn1, BF)
    O1 = run_attn1(xn1_full, inputs["L1_attn_bv_q"].astype(np.float32),
                   inputs["L1_attn_bv_k"].astype(np.float32),
                   inputs["L1_attn_bv_v"].astype(np.float32))
    a1 = _shard_rows(O1)
    f1 = run_ffn1(xn1, inputs)

    xs2, xn2 = run_ln(xs1, a1, f1)
    xn2_full = _join_rows(xn2, BF)
    O2 = run_attn_qk(2, xn2_full, inputs)
    a2 = run_proj(_shard_rows(O2),
                  np.ascontiguousarray(inputs["L2_attn_Wo"].astype(np.float32).T))
    f2 = run_ffn2(xn2, inputs)

    xs3, xn3 = run_ln(xs2, a2, f2)
    a3 = run_attn3(xn3, inputs)
    f3 = run_proj(xn3, L3WT, bias=inputs["L3_ffn_bias"].astype(np.float32),
                  w_mode="bf16T")

    xs4, _ = run_ln(xs3, a3, f3, emit_xn=False)
    return _join_rows(xs4, np.float32).reshape(B, T, C)


# version 02
# speedup vs baseline: 1.2071x; 1.2071x over previous
import math
from contextlib import ExitStack

import numpy as np
import ml_dtypes

import concourse.bass as bass
import concourse.mybir as mybir
from concourse.bass_utils import run_bass_kernel_spmd

F32 = mybir.dt.float32
BF16 = mybir.dt.bfloat16
F16 = mybir.dt.float16
I32 = mybir.dt.int32
AF = mybir.ActivationFunctionType
ALU = mybir.AluOpType
BF = ml_dtypes.bfloat16

B, T, C = 4, 2048, 4096
NCORE = 8
TPC = 1024
LNEPS = 1e-5

_NC_CACHE = {}


def _cached(key, fn):
    if key not in _NC_CACHE:
        _NC_CACHE[key] = fn()
    return _NC_CACHE[key]


def build_ln(with_branches, emit_xn, f_dtype=BF16):
    nc = bass.Bass("TRN2", target_bir_lowering=False)
    x_d = nc.dram_tensor("x", [TPC, C], F32, kind="ExternalInput").ap()
    if with_branches:
        a_d = nc.dram_tensor("a", [TPC, C], BF16, kind="ExternalInput").ap()
        f_d = nc.dram_tensor("f", [TPC, C], f_dtype, kind="ExternalInput").ap()
        xo_d = nc.dram_tensor("xo", [TPC, C], F32, kind="ExternalOutput").ap()
    if emit_xn:
        xn_d = nc.dram_tensor("xn", [TPC, C], BF16, kind="ExternalOutput").ap()
    nt = TPC // 128
    with ExitStack() as es:
        def sb(name, shape, dt):
            return es.enter_context(nc.sbuf_tensor(name, shape, dt))
        xt = [sb(f"xt{i}", [128, C], F32) for i in range(2)]
        at = [sb(f"at{i}", [128, C], F32) for i in range(2)] if with_branches else None
        ft = [sb(f"ft{i}", [128, C], F32) for i in range(2)] if with_branches else None
        stats2 = [sb(f"stats{i}", [128, 48], F32) for i in range(2)]
        mv2 = [sb(f"mv{i}", [128, 2], F32) for i in range(2)]
        rstd2 = [sb(f"rstd{i}", [128, 1], F32) for i in range(2)]
        xnt2 = [sb(f"xnt2_{i}", [128, C], BF16) for i in range(2)] if emit_xn else None
        ld = es.enter_context(nc.semaphore("ld"))
        s_add = es.enter_context(nc.semaphore("s_add"))
        s_agg = es.enter_context(nc.semaphore("s_agg"))
        s_sq = es.enter_context(nc.semaphore("s_sq"))
        s_ts = es.enter_context(nc.semaphore("s_ts"))
        st = es.enter_context(nc.semaphore("st"))
        ini = es.enter_context(nc.semaphore("ini"))
        cstt = sb("cstt", [128, 128], F32)
        blk = es.enter_context(nc.Block())
        cst_d = nc.dram_tensor("cst", [128, 128], F32, kind="ExternalInput").ap()

        @blk.gpsimd
        def _(g):
            g.dma_start(out=cstt[:], in_=cst_d[:, :]).then_inc(ini, 16)
            g.wait_ge(ini, 16)

        @blk.scalar
        def _(sc):
            sc.wait_ge(ini, 16)

        n_in = 3 if with_branches else 1
        st_per = (16 if with_branches else 0) + (16 if emit_xn else 0)
        ld_n = [0]
        st_n = [0]
        done_mark = [0] * (nt + 2)

        def load_tile(i):
            bi = i % 2
            @blk.gpsimd
            def _(g):
                if i >= 2:
                    g.wait_ge(s_ts if emit_xn else s_add, i - 1)
                    g.wait_ge(st, (i - 1) * st_per)
                g.dma_start(out=xt[bi][:], in_=x_d[i * 128:(i + 1) * 128, :]).then_inc(ld, 16)
                if with_branches:
                    g.dma_start(out=at[bi][:], in_=a_d[i * 128:(i + 1) * 128, :]).then_inc(ld, 16)
                    g.dma_start(out=ft[bi][:], in_=f_d[i * 128:(i + 1) * 128, :]).then_inc(ld, 16)
                ld_n[0] += 16 * n_in

        load_tile(0)
        if nt > 1:
            load_tile(1)
        for i in range(nt):
            bi = i % 2
            need_ld = (i + 1) * 16 * n_in

            @blk.vector
            def _(v):
                v.wait_ge(ld, need_ld)
                if emit_xn and i >= 2:
                    v.wait_ge(st, (i - 1) * st_per)
                if with_branches:
                    v.tensor_tensor(xt[bi][:], xt[bi][:], at[bi][:], ALU.add)
                    v.tensor_tensor(xt[bi][:], xt[bi][:], ft[bi][:], ALU.add)
                    v.drain().then_inc(s_add, 1)
                if emit_xn:
                    for j in range(8):
                        v.bn_stats(stats2[bi][:, j * 6:(j + 1) * 6], xt[bi][:, j * 512:(j + 1) * 512])
                    v.drain()
                    v.bn_aggr(mv2[bi][:], stats2[bi][:])
                    v.drain().then_inc(s_agg, 1)
            if with_branches:
                @blk.sync
                def _(s):
                    s.wait_ge(s_add, i + 1)
                    s.dma_start(out=xo_d[i * 128:(i + 1) * 128, :], in_=xt[bi][:]).then_inc(st, 16)
                    st_n[0] += 16
            if emit_xn:
                @blk.scalar
                def _(sc):
                    sc.wait_ge(s_agg, i + 1)
                    sc.activation(rstd2[bi][:], mv2[bi][:, 1:2], AF.Sqrt, bias=cstt[:, 0:1], scale=1.0)
                    sc.drain().then_inc(s_sq, 1)

                @blk.vector
                def _(v):
                    v.wait_ge(s_sq, i + 1)
                    v.reciprocal(rstd2[bi][:], rstd2[bi][:])
                    v.tensor_scalar(xnt2[bi][:], xt[bi][:], mv2[bi][:, 0:1], rstd2[bi][:], ALU.subtract, ALU.mult)
                    v.drain().then_inc(s_ts, 1)

                @blk.sync
                def _(s):
                    s.wait_ge(s_ts, i + 1)
                    s.dma_start(out=xn_d[i * 128:(i + 1) * 128, :], in_=xnt2[bi][:]).then_inc(st, 16)
                    st_n[0] += 16
            if i + 2 < nt:
                load_tile(i + 2)
        @blk.sync
        def _(s):
            s.wait_ge(st, st_n[0])
    return nc


def build_proj(K, O, w_mode, with_bias, out_dt=BF16):
    nc = bass.Bass("TRN2", target_bir_lowering=False)
    x_d = nc.dram_tensor("x", [TPC, K], BF16, kind="ExternalInput").ap()
    if w_mode == "f32T":
        w_h = nc.dram_tensor("w", [K, O], F32, kind="ExternalInput")
    else:
        w_h = nc.dram_tensor("w", [K, O], BF16, kind="ExternalInput")
    w_d = w_h.ap()
    if with_bias:
        b_d = nc.dram_tensor("b", [1, O], F32, kind="ExternalInput").ap()
        o_d = nc.dram_tensor("onesr", [1, 128], F32, kind="ExternalInput").ap()
    y_d = nc.dram_tensor("y", [TPC, O], out_dt, kind="ExternalOutput").ap()
    nt = TPC // 128
    nk = K // 128
    no = O // 512
    with ExitStack() as es:
        def sb(name, shape, dt):
            return es.enter_context(nc.sbuf_tensor(name, shape, dt))
        xf = sb("xf", [128, nk * TPC], BF16)
        wblk = [sb(f"wblk{i}", [128, nk * 512], BF16) for i in range(2)]
        ones = sb("ones", [1, 128], F32)
        bias = sb("bias", [1, O], F32) if with_bias else None
        outb = [sb(f"outb{i}", [128, 512], out_dt) for i in range(2)]
        ps = [es.enter_context(nc.psum_tensor(f"ps{i}", [128, 512], F32)) for i in range(2)]
        ldx = es.enter_context(nc.semaphore("ldx"))
        ldw = es.enter_context(nc.semaphore("ldw"))
        s_mm = es.enter_context(nc.semaphore("s_mm"))
        s_ev = es.enter_context(nc.semaphore("s_ev"))
        st = es.enter_context(nc.semaphore("st"))
        blk = es.enter_context(nc.Block())

        @blk.sync
        def _(s):
            for kc in range(nk):
                s.dma_start_transpose(
                    out=xf[:, kc * TPC:(kc + 1) * TPC],
                    in_=x_d[:, kc * 128:(kc + 1) * 128]).then_inc(ldx, 16)

        if with_bias:
            @blk.gpsimd
            def _(g):
                g.dma_start(out=bias[:, :], in_=b_d[:, :]).then_inc(ldx, 16)
                g.dma_start(out=ones[:, :], in_=o_d[:, :]).then_inc(ldx, 16)

        def load_w(o, buf):
            @blk.gpsimd
            def _(g):
                if o >= 2:
                    g.wait_ge(s_mm, (o - 1) * nt)
                src = bass.AP(w_h, o * 512,
                              [[O, 128], [128 * O, nk], [1, 512]])
                dst = wblk[buf].rearrange("p (kc o) -> p kc o", kc=nk)
                g.dma_start(out=dst, in_=src).then_inc(ldw, 16)

        w_dmas = 1
        load_w(0, 0)
        if no > 1:
            load_w(1, 1)
        mm_n = 0
        ev_n = 0
        st_n = 0
        for o in range(no):
            buf = o % 2
            for t in range(nt):
                pb = (o * nt + t) % 2

                @blk.tensor
                def _(te):
                    if o == 0 and t == 0:
                        te.wait_ge(ldx, (nk + (2 if with_bias else 0)) * 16)
                    te.wait_ge(ldw, (o + 1) * w_dmas * 16)
                    if mm_n >= 2:
                        te.wait_ge(s_ev, mm_n - 1)
                    for kc in range(nk):
                        last = (kc == nk - 1 and not with_bias)
                        mm = te.matmul(ps[pb][:],
                                       xf[:, kc * TPC + t * 128: kc * TPC + (t + 1) * 128],
                                       wblk[buf][:, kc * 512:(kc + 1) * 512],
                                       start=(kc == 0), stop=last)
                        if last:
                            mm.then_inc(s_mm, 1)
                    if with_bias:
                        te.matmul(ps[pb][:], ones[:, :],
                                  bias[:, o * 512:(o + 1) * 512],
                                  start=False, stop=True).then_inc(s_mm, 1)
                mm_n += 1

                @blk.vector
                def _(v):
                    v.wait_ge(s_mm, mm_n)
                    if ev_n >= 2:
                        v.wait_ge(st, (ev_n - 1) * 16)
                    v.tensor_copy(outb[pb][:], ps[pb][:])
                    v.drain().then_inc(s_ev, 1)
                ev_n += 1

                @blk.sync
                def _(s):
                    s.wait_ge(s_ev, ev_n)
                    s.dma_start(out=y_d[t * 128:(t + 1) * 128, o * 512:(o + 1) * 512],
                                in_=outb[pb][:]).then_inc(st, 16)
                st_n += 16
            if o + 2 < no:
                load_w(o + 2, buf)
        @blk.sync
        def _(s):
            s.wait_ge(st, st_n)
    return nc


def build_prep():
    nc = bass.Bass("TRN2", target_bir_lowering=False)
    gq = nc.dram_tensor("gq", [1408, 512], I32, kind="ExternalInput").ap()
    gs = nc.dram_tensor("gs", [1408, 32], F16, kind="ExternalInput").ap()
    gz = nc.dram_tensor("gz", [1408, 32], I32, kind="ExternalInput").ap()
    uq = nc.dram_tensor("uq", [1408, 512], I32, kind="ExternalInput").ap()
    us = nc.dram_tensor("us", [1408, 32], F16, kind="ExternalInput").ap()
    uz = nc.dram_tensor("uz", [1408, 32], I32, kind="ExternalInput").ap()
    dq = nc.dram_tensor("dq", [512, 1376], I32, kind="ExternalInput").ap()
    ds = nc.dram_tensor("ds", [512, 86], F16, kind="ExternalInput").ap()
    dz = nc.dram_tensor("dz", [512, 86], I32, kind="ExternalInput").ap()
    wp = nc.dram_tensor("wp", [512, 2048], I32, kind="ExternalInput").ap()
    l3sz = nc.dram_tensor("l3sz", [512, 128], F32, kind="ExternalInput").ap()
    shf = nc.dram_tensor("shf", [128, 128], I32, kind="ExternalInput").ap()
    idn = nc.dram_tensor("idn", [128, 128], BF16, kind="ExternalInput").ap()
    gw = nc.dram_tensor("gw", [1408, 4096], BF16, kind="ExternalOutput").ap()
    uw = nc.dram_tensor("uw", [1408, 4096], BF16, kind="ExternalOutput").ap()
    dw = nc.dram_tensor("dw", [512, 11008], BF16, kind="ExternalOutput").ap()
    l3w = nc.dram_tensor("l3w", [512, 4096], BF16, kind="ExternalOutput").ap()

    with ExitStack() as es:
        def sb(name, shape, dt):
            return es.enter_context(nc.sbuf_tensor(name, shape, dt))
        qt = [sb(f"qt{i}", [128, 2048], I32) for i in range(2)]
        nib = sb("nib", [128, 11008], I32)
        outw = [sb(f"outw{i}", [128, 11008], BF16) for i in range(2)]
        tbuf = sb("tbuf", [128, 11008], BF16)
        svec = [sb(f"svec{i}", [128, 96], F16) for i in range(2)]
        zvec = [sb(f"zvec{i}", [128, 96], I32) for i in range(2)]
        sf32 = sb("sf32", [128, 96], F32)
        zpf = sb("zpf", [128, 96], F32)
        shft = sb("shft", [128, 128], I32)
        idnt = sb("idnt", [128, 128], BF16)
        l3sc = [sb(f"l3sc{i}", [128, 128], F32) for i in range(2)]
        pst = [es.enter_context(nc.psum_tensor(f"pst{i}", [128, 512], BF16)) for i in range(2)]
        ld = es.enter_context(nc.semaphore("ld"))
        s_dq = es.enter_context(nc.semaphore("s_dq"))
        s_tp = es.enter_context(nc.semaphore("s_tp"))
        s_tb = es.enter_context(nc.semaphore("s_tb"))
        st = es.enter_context(nc.semaphore("st"))
        blk = es.enter_context(nc.Block())

        @blk.gpsimd
        def _(g):
            g.dma_start(out=shft[:], in_=shf[:, :]).then_inc(ld, 16)
            g.dma_start(out=idnt[:], in_=idn[:, :]).then_inc(ld, 16)

        ld_n = [32]
        dq_n = [0]
        tp_n = [0]
        tb_n = [0]
        st_n = [0]

        def emit_tile(w_out, ti, K_, body_loads, body_dequant):
            bi = dq_n[0] % 2
            body_loads(bi)

            @blk.vector
            def _(v):
                v.wait_ge(ld, ld_n[0])
                if dq_n[0] >= 2:
                    v.wait_ge(s_tp, tp_marks[dq_n[0] - 2])
                body_dequant(v, bi)
                v.drain().then_inc(s_dq, 1)
            dq_n[0] += 1

            nchunk = K_ // 128
            groups = [(gi, min(4, nchunk - gi * 4)) for gi in range((nchunk + 3) // 4)]
            ng = len(groups)

            @blk.tensor
            def _(te):
                te.wait_ge(s_dq, dq_n[0])
                if dq_n[0] == 1:
                    te.wait_ge(ld, 32)
                for gi, gl in groups:
                    pb = (tp_n[0] + gi) % 2
                    te.wait_ge(s_tb, max(0, tp_n[0] + gi - 1))
                    for q4 in range(gl):
                        kc = gi * 4 + q4
                        mm = te.transpose(pst[pb][:, q4 * 128:(q4 + 1) * 128],
                                          outw[bi][:, kc * 128:(kc + 1) * 128], idnt[:])
                        if q4 == gl - 1:
                            mm.then_inc(s_tp, 1)

            @blk.vector
            def _(v):
                for gi, gl in groups:
                    pb = (tp_n[0] + gi) % 2
                    v.wait_ge(s_tp, tp_n[0] + gi + 1)
                    if dq_n[0] >= 2 and gi == 0:
                        v.wait_ge(st, st_marks[dq_n[0] - 2])
                    v.tensor_copy(tbuf[:, gi * 512:gi * 512 + gl * 128], pst[pb][:, :gl * 128])
                    v.drain().then_inc(s_tb, 1)
            tp_n[0] += ng
            tb_n[0] += ng
            tp_marks.append(tp_n[0])

            @blk.sync
            def _(s):
                s.wait_ge(s_tb, tb_n[0])
                s.dma_start(out=w_out[ti * 128:(ti + 1) * 128, :K_],
                            in_=tbuf[:, :K_]).then_inc(st, 16)
            st_n[0] += 16
            st_marks.append(st_n[0])

        tp_marks = []
        st_marks = []

        def gptq(q_d, s_d, z_d, w_out, ti, nwords, ngrp, K_):
            def loads(bi):
                @blk.gpsimd
                def _(g):
                    if dq_n[0] >= 2:
                        g.wait_ge(s_dq, dq_n[0] - 1)
                    g.dma_start(out=qt[bi][:, :nwords], in_=q_d[ti * 128:(ti + 1) * 128, :]).then_inc(ld, 16)
                    g.dma_start(out=svec[bi][:, :ngrp], in_=s_d[ti * 128:(ti + 1) * 128, :]).then_inc(ld, 16)
                    g.dma_start(out=zvec[bi][:, :ngrp], in_=z_d[ti * 128:(ti + 1) * 128, :]).then_inc(ld, 16)
                ld_n[0] += 48

            def dequant(v, bi):
                for j in range(8):
                    outap = nib[:, j:K_:8]
                    if j == 0:
                        v.tensor_scalar(outap, qt[bi][:, :nwords], 15, None,
                                        ALU.bitwise_and, ALU.bypass)
                    else:
                        v.tensor_scalar(outap, qt[bi][:, :nwords], 4 * j, 15,
                                        ALU.logical_shift_right, ALU.bitwise_and)
                v.tensor_scalar(zvec[bi][:, :ngrp], zvec[bi][:, :ngrp], shft[:, 0:1], 15,
                                ALU.logical_shift_right, ALU.bitwise_and)
                v.tensor_copy(sf32[:, :ngrp], svec[bi][:, :ngrp])
                v.drain()
                v.scalar_tensor_tensor(zpf[:, :ngrp], zvec[bi][:, :ngrp], 1.0,
                                       sf32[:, :ngrp], ALU.add, ALU.mult)
                v.drain()
                for g_ in range(ngrp):
                    v.tensor_scalar(outw[bi][:, g_ * 128:(g_ + 1) * 128],
                                    nib[:, g_ * 128:(g_ + 1) * 128],
                                    sf32[:, g_:g_ + 1], zpf[:, g_:g_ + 1],
                                    ALU.mult, ALU.subtract)
            emit_tile(w_out, ti, K_, loads, dequant)

        for ti in range(11):
            gptq(gq, gs, gz, gw, ti, 512, 32, 4096)
        for ti in range(11):
            gptq(uq, us, uz, uw, ti, 512, 32, 4096)
        for ti in range(4):
            gptq(dq, ds, dz, dw, ti, 1376, 86, 11008)

        for ti in range(4):
            def loads(bi, ti=ti):
                @blk.gpsimd
                def _(g):
                    g.wait_ge(s_dq, dq_n[0] - 1)
                    g.dma_start(out=qt[bi][:, :2048], in_=wp[ti * 128:(ti + 1) * 128, :]).then_inc(ld, 16)
                    g.dma_start(out=l3sc[bi][:], in_=l3sz[ti * 128:(ti + 1) * 128, :]).then_inc(ld, 16)
                ld_n[0] += 32

            def dequant(v, bi, ti=ti):
                v.tensor_scalar(nib[:, 0:4096:2], qt[bi][:, :2048], 15, None,
                                ALU.bitwise_and, ALU.bypass)
                v.tensor_scalar(nib[:, 1:4096:2], qt[bi][:, :2048], 4, 15,
                                ALU.logical_shift_right, ALU.bitwise_and)
                v.drain()
                v.tensor_scalar(outw[bi][:, :4096], nib[:, :4096],
                                l3sc[bi][:, 1:2], l3sc[bi][:, 0:1],
                                ALU.subtract, ALU.mult)
            emit_tile(l3w, ti, 4096, loads, dequant)

        @blk.sync
        def _(s):
            s.wait_ge(st, st_n[0])
    return nc


def build_ffn0():
    NI, NIP = 11008, 11264
    nc = bass.Bass("TRN2", target_bir_lowering=False)
    x_d = nc.dram_tensor("x", [TPC, C], BF16, kind="ExternalInput").ap()
    gw_h = nc.dram_tensor("gwt", [C, NIP], BF16, kind="ExternalInput")
    uw_h = nc.dram_tensor("uwt", [C, NIP], BF16, kind="ExternalInput")
    dw_h = nc.dram_tensor("dwt", [NI, C], BF16, kind="ExternalInput")
    y_d = nc.dram_tensor("y", [TPC, C], F16, kind="ExternalOutput").ap()
    TP = 512
    npan = TPC // TP
    nk = C // 128
    nnc = NI // 128
    OG = 256
    with ExitStack() as es:
        def sb(name, shape, dt):
            return es.enter_context(nc.sbuf_tensor(name, shape, dt))
        inter = sb("inter", [128, nnc * TP], BF16)
        arena = sb("arena", [128, 3 * nk * 512], BF16)
        xfp = arena[:, 0:nk * TP]
        wg = [arena[:, nk * 512 + i * nk * 256:nk * 512 + (i + 1) * nk * 256] for i in range(2)]
        wu = [arena[:, 2 * nk * 512 + i * nk * 256:2 * nk * 512 + (i + 1) * nk * 256] for i in range(2)]
        dwb = [arena[:, i * nnc * OG:(i + 1) * nnc * OG] for i in range(2)]
        gsil = sb("gsil", [128, 2 * TP], F32)
        outb = [sb(f"outb{i}", [128, OG], F16) for i in range(2)]
        ps_g = [es.enter_context(nc.psum_tensor(f"psg{i}", [128, TP], F32)) for i in range(2)]
        ps_d = [es.enter_context(nc.psum_tensor(f"psd{i}", [128, 512], F32)) for i in range(4)]
        ldx = es.enter_context(nc.semaphore("ldx"))
        ldw = es.enter_context(nc.semaphore("ldw"))
        ldd = es.enter_context(nc.semaphore("ldd"))
        s_gmm = es.enter_context(nc.semaphore("s_gmm"))
        s_sil = es.enter_context(nc.semaphore("s_sil"))
        s_int = es.enter_context(nc.semaphore("s_int"))
        s_dmm = es.enter_context(nc.semaphore("s_dmm"))
        s_ev = es.enter_context(nc.semaphore("s_ev"))
        st = es.enter_context(nc.semaphore("st"))
        blk = es.enter_context(nc.Block())

        @blk.scalar
        def _(sc):
            sc.preload_act_table(AF.Silu)

        ldx_n = [0]
        ldw_n = [0]
        ldd_n = [0]
        gmm_n = [0]
        sil_n = [0]
        int_n = [0]
        dmm_n = [0]
        ev_n = [0]
        st_n = [0]

        for pan in range(npan):
            t0 = pan * TP

            @blk.sync
            def _(s):
                if pan > 0:
                    s.wait_ge(s_int, pan * nnc)
                    s.wait_ge(s_dmm, pan * (C // OG) * (TP // 128))
                    s.wait_ge(st, pan * (C // OG) * (TP // 128) * 16)
                for kc in range(nk):
                    s.dma_start_transpose(
                        out=xfp[:, kc * TP:(kc + 1) * TP],
                        in_=x_d[t0:t0 + TP, kc * 128:(kc + 1) * 128]).then_inc(ldx, 16)
            ldx_n[0] += nk * 16

            for grp in range(NIP // 256):
                n0 = grp * 256
                gb = grp % 2
                dnow = dmm_n[0]
                gprev = gmm_n[0]

                @blk.gpsimd
                def _(g):
                    if gprev > 4:
                        g.wait_ge(s_gmm, gprev - 4)
                    if grp <= 1 and dnow > 0:
                        g.wait_ge(s_dmm, dnow)
                    src_g = bass.AP(gw_h, n0, [[NIP, 128], [128 * NIP, nk], [1, 256]])
                    src_u = bass.AP(uw_h, n0, [[NIP, 128], [128 * NIP, nk], [1, 256]])
                    g.dma_start(out=wg[gb].rearrange("p (kc n) -> p kc n", n=256), in_=src_g).then_inc(ldw, 16)
                    g.dma_start(out=wu[gb].rearrange("p (kc n) -> p kc n", n=256), in_=src_u).then_inc(ldw, 16)
                ldw_n[0] += 32

                sub = [grp * 2 + j for j in range(2) if grp * 2 + j < nnc]
                pgs = {}
                for j, ncg in enumerate(sub):
                    pgs[j] = int_n[0] % 2

                    @blk.tensor
                    def _(te, j=j, ncg=ncg):
                        if gmm_n[0] == 0:
                            te.wait_ge(ldx, ldx_n[0])
                        elif ncg == 0:
                            te.wait_ge(ldx, ldx_n[0])
                        te.wait_ge(ldw, ldw_n[0])
                        if int_n[0] >= 2:
                            te.wait_ge(s_int, int_n[0] - 1)
                        for kc in range(nk):
                            mm = te.matmul(ps_g[pgs[j]][:],
                                           wg[gb][:, kc * 256 + j * 128: kc * 256 + (j + 1) * 128],
                                           xfp[:, kc * TP:(kc + 1) * TP],
                                           start=(kc == 0), stop=(kc == nk - 1))
                            if kc == nk - 1:
                                mm.then_inc(s_gmm, 1)
                    gmm_n[0] += 1
                    int_n[0] += 1

                    @blk.scalar
                    def _(sc, j=j):
                        sc.wait_ge(s_gmm, gmm_n[0])
                        sc.activation(gsil[:, j * TP:(j + 1) * TP], ps_g[pgs[j]][:], AF.Silu)
                        sc.drain().then_inc(s_sil, 1)
                    sil_n[0] += 1
                int_n[0] -= len(sub)

                for j, ncg in enumerate(sub):
                    @blk.tensor
                    def _(te, j=j, ncg=ncg):
                        te.wait_ge(s_sil, sil_n[0] - len(sub) + 1 + j)
                        for kc in range(nk):
                            mm = te.matmul(ps_g[pgs[j]][:],
                                           wu[gb][:, kc * 256 + j * 128: kc * 256 + (j + 1) * 128],
                                           xfp[:, kc * TP:(kc + 1) * TP],
                                           start=(kc == 0), stop=(kc == nk - 1))
                            if kc == nk - 1:
                                mm.then_inc(s_gmm, 1)
                    gmm_n[0] += 1

                    @blk.vector
                    def _(v, j=j, ncg=ncg):
                        v.wait_ge(s_gmm, gmm_n[0])
                        v.tensor_tensor(inter[:, ncg * TP:(ncg + 1) * TP],
                                        gsil[:, j * TP:(j + 1) * TP], ps_g[pgs[j]][:], ALU.mult)
                        v.drain().then_inc(s_int, 1)
                    int_n[0] += 1

            def load_dw(og, buf):
                dm_now = dmm_n[0]
                int_need = int_n[0]

                @blk.sync
                def _(s):
                    if dm_now > 0:
                        s.wait_ge(s_dmm, dm_now)
                    if og < 2:
                        s.wait_ge(s_int, int_need)
                    src = bass.AP(dw_h, og * OG, [[C, 128], [128 * C, nnc], [1, OG]])
                    s.dma_start(out=dwb[buf].rearrange("p (kc o) -> p kc o", o=OG),
                                in_=src).then_inc(ldd, 16)

            load_dw(0, 0)
            load_dw(1, 1)
            for og in range(C // OG):
                buf = og % 2
                ldd_n[0] += 16
                for tt in range(TP // 128):
                    pd = tt

                    @blk.tensor
                    def _(te):
                        te.wait_ge(ldd, (og + 1) * 16 + pan * (C // OG) * 16)
                        if tt == 0 and og == 0:
                            te.wait_ge(s_int, (pan + 1) * nnc)
                        if dmm_n[0] >= 4:
                            te.wait_ge(s_ev, dmm_n[0] - 3)
                        for kc in range(nnc):
                            mm = te.matmul(ps_d[pd][:, :OG],
                                           inter[:, kc * TP + tt * 128: kc * TP + (tt + 1) * 128],
                                           dwb[buf][:, kc * OG:(kc + 1) * OG],
                                           start=(kc == 0), stop=(kc == nnc - 1))
                            if kc == nnc - 1:
                                mm.then_inc(s_dmm, 1)
                    dmm_n[0] += 1

                    @blk.vector
                    def _(v):
                        v.wait_ge(s_dmm, dmm_n[0])
                        if ev_n[0] >= 2:
                            v.wait_ge(st, (ev_n[0] - 1) * 16)
                        v.tensor_copy(outb[ev_n[0] % 2][:], ps_d[pd][:, :OG])
                        v.drain().then_inc(s_ev, 1)
                    ev_n[0] += 1

                    @blk.sync
                    def _(s):
                        s.wait_ge(s_ev, ev_n[0])
                        s.dma_start(out=y_d[t0 + tt * 128: t0 + (tt + 1) * 128,
                                            og * OG:(og + 1) * OG],
                                    in_=outb[(ev_n[0] - 1) % 2][:]).then_inc(st, 16)
                    st_n[0] += 16
                if og + 2 < C // OG:
                    load_dw(og + 2, buf)
        @blk.sync
        def _(s):
            s.wait_ge(st, st_n[0])
    return nc


def run_ffn0(xs, GWT, UWT, DWT):
    nc = _cached(("ffn0",), build_ffn0)
    maps = [{"x": xs[i], "gwt": GWT, "uwt": UWT, "dwt": DWT} for i in range(NCORE)]
    res = run_bass_kernel_spmd(nc, maps, core_ids=list(range(NCORE)))
    return [r["y"] for r in res.results]


def build_attn1():
    nc = bass.Bass("TRN2", target_bir_lowering=False)
    xn_h = nc.dram_tensor("xnb", [T, 2048], BF16, kind="ExternalInput")
    xn_d = xn_h.ap()
    bv_d = nc.dram_tensor("bvs", [6, 1024], F32, kind="ExternalInput").ap()
    bvf_d = nc.dram_tensor("bvf", [1, 6144], F32, kind="ExternalInput").ap()
    idn_d = nc.dram_tensor("idn32", [128, 128], F32, kind="ExternalInput").ap()
    one_d = nc.dram_tensor("onesr", [1, 128], F32, kind="ExternalInput").ap()
    tri_d = nc.dram_tensor("triT", [128, 128], BF16, kind="ExternalInput").ap()
    y_d = nc.dram_tensor("y", [T, 2048], BF16, kind="ExternalOutput").ap()
    HD = 1024
    ndc = HD // 128
    NG = T // 512
    NKB = T // 128
    SCL = 4.0 / 32.0
    with ExitStack() as es:
        def sb(name, shape, dt):
            return es.enter_context(nc.sbuf_tensor(name, shape, dt))
        bvt = sb("bvt", [6, 1024], F32)
        sgn = sb("sgn", [6, 1024], F32)
        u2 = sb("u2", [6, 1024], F32)
        alpha = sb("alpha", [6, 1], F32)
        bcol = sb("bcol", [128, 8 * 6], F32)
        bvr = sb("bvr", [1, 6144], F32)
        sgr = sb("sgr", [1, 6144], F32)
        u2r = sb("u2r", [1, 6144], F32)
        alr = sb("alr", [1, 6], F32)
        onesc = sb("onesc", [1, 128], F32)
        idnt = sb("idnt", [128, 128], F32)
        trit = sb("trit", [128, 128], BF16)
        vbb = sb("vbb", [128, HD], F32)
        qf = sb("qf", [128, ndc * T], BF16)
        kf = sb("kf", [128, ndc * T], BF16)
        xv = sb("xv", [128, NKB * HD], BF16)
        pt = sb("pt", [128, NKB * 512], BF16)
        outb = [sb(f"outb{i}", [128, 512], BF16) for i in range(2)]
        ps_s = [es.enter_context(nc.psum_tensor(f"pss{i}", [128, 512], F32)) for i in range(2)]
        ps_o = [es.enter_context(nc.psum_tensor(f"pso{i}", [128, 512], F32)) for i in range(4)]
        ps_t = es.enter_context(nc.psum_tensor("pstr", [128, 512], F32))
        ld = es.enter_context(nc.semaphore("ld"))
        s_b = es.enter_context(nc.semaphore("s_b"))
        s_qk = es.enter_context(nc.semaphore("s_qk"))
        s_vb = es.enter_context(nc.semaphore("s_vb"))
        s_sm = es.enter_context(nc.semaphore("s_sm"))
        s_sg = es.enter_context(nc.semaphore("s_sg"))
        s_p = es.enter_context(nc.semaphore("s_p"))
        s_pv = es.enter_context(nc.semaphore("s_pv"))
        s_ev = es.enter_context(nc.semaphore("s_ev"))
        st = es.enter_context(nc.semaphore("st"))
        blk = es.enter_context(nc.Block())

        @blk.scalar
        def _(sc):
            sc.preload_act_table(AF.Sigmoid)

        @blk.gpsimd
        def _(g):
            g.dma_start(out=bvt[:], in_=bv_d[:, :]).then_inc(ld, 16)
            g.dma_start(out=bvr[:], in_=bvf_d[:, :]).then_inc(ld, 16)
            g.dma_start(out=idnt[:], in_=idn_d[:, :]).then_inc(ld, 16)
            g.dma_start(out=trit[:], in_=tri_d[:, :]).then_inc(ld, 16)
            g.dma_start(out=onesc[:], in_=one_d[:, :]).then_inc(ld, 16)

        @blk.vector
        def _(v):
            v.wait_ge(ld, 80)
            v.tensor_reduce(alpha[:], bvt[:], mybir.AxisListType.X, ALU.add,
                            apply_absolute_value=True)
            v.tensor_reduce(alr[:], bvr[:].rearrange("p (r h) -> p r h", h=1024),
                            mybir.AxisListType.X, ALU.add, apply_absolute_value=True)
            v.drain().then_inc(s_b, 1)

        @blk.scalar
        def _(sc):
            sc.wait_ge(s_b, 1)
            sc.activation(sgn[:], bvt[:], AF.Sign)
            sc.activation(sgr[:], bvr[:], AF.Sign)
            sc.drain().then_inc(s_b, 1)

        @blk.vector
        def _(v):
            v.wait_ge(s_b, 2)
            for (S_, U_, A_, B_, nseg) in ((sgn, u2, alpha, bvt, None), (sgr, u2r, alr, bvr, 6)):
                v.tensor_tensor(U_[:], S_[:], S_[:], ALU.mult)
                v.drain()
                v.scalar_tensor_tensor(U_[:], U_[:], -1.0, S_[:], ALU.mult, ALU.add)
                v.tensor_scalar(U_[:], U_[:], 1.0, None, ALU.add)
                v.tensor_scalar(A_[:], A_[:], 1.0 / 1024.0, None, ALU.mult)
                v.drain()
                if nseg is None:
                    v.tensor_scalar(B_[:], U_[:], A_[:], None, ALU.mult)
                else:
                    for r in range(nseg):
                        v.tensor_scalar(B_[0:1, r * 1024:(r + 1) * 1024],
                                        U_[0:1, r * 1024:(r + 1) * 1024],
                                        A_[0:1, r:r + 1], None, ALU.mult)
                v.drain()
            v.drain().then_inc(s_b, 1)

        @blk.tensor
        def _(te):
            te.wait_ge(s_b, 3)
            for dc in range(8):
                mm = te.transpose(ps_t[:, dc * 8:dc * 8 + 6],
                                  bvt[:, dc * 128:(dc + 1) * 128], idnt[0:6, 0:6])
                if dc == 7:
                    mm.then_inc(s_b, 1)

        @blk.vector
        def _(v):
            v.wait_ge(s_b, 4)
            for dc in range(8):
                v.tensor_copy(bcol[:, dc * 6:(dc + 1) * 6], ps_t[:, dc * 8:dc * 8 + 6])
            v.drain().then_inc(s_b, 1)

        st_n = [0]
        sm_n = [0]
        sg_n = [0]
        p_n = [0]
        pv_n = [0]
        ev_n = [0]
        ld_n = [80]

        for hh in range(2):
            c0 = hh * 1024

            vb_ps = [ps_t, ps_o[0]]

            @blk.tensor
            def _(te):
                te.wait_ge(s_b, 5)
                if hh > 0:
                    te.wait_ge(s_ev, ev_n[0])
                for half in range(2):
                    te.matmul(vb_ps[half][:, :512], onesc[:, :],
                              bvr[0:1, 4096 + c0 + half * 512: 4096 + c0 + (half + 1) * 512],
                              start=True, stop=True).then_inc(s_vb, 1)

            for half in range(2):
                @blk.vector
                def _(v):
                    v.wait_ge(s_vb, hh * 2 + half + 1)
                    v.tensor_copy(vbb[:, half * 512:(half + 1) * 512], vb_ps[half][:, :512])
                    v.drain().then_inc(s_b, 1)

            @blk.sync
            def _(s):
                if hh > 0:
                    s.wait_ge(s_pv, pv_n[0])
                for dc in range(ndc):
                    s.dma_start_transpose(
                        out=qf[:, dc * T:(dc + 1) * T],
                        in_=xn_d[:, c0 + dc * 128:c0 + (dc + 1) * 128]).then_inc(ld, 16)
                s.dma_start(out=xv.rearrange("p (kb d) -> p kb d", d=HD),
                            in_=xn_h.ap().rearrange("(kb p) d -> p kb d", p=128)[:, :, c0:c0 + HD]
                            ).then_inc(ld, 16)
            ld_n[0] += (ndc + 1) * 16

            ldq = ld_n[0]

            @blk.vector
            def _(v):
                v.wait_ge(s_b, 5)
                v.wait_ge(ld, ldq)
                if hh > 0:
                    v.wait_ge(s_pv, pv_n[0])
                for dc in range(ndc):
                    qcol = bcol[:, dc * 6 + hh:dc * 6 + hh + 1]
                    kcol = bcol[:, dc * 6 + 2 + hh:dc * 6 + 2 + hh + 1]
                    v.tensor_scalar(kf[:, dc * T:(dc + 1) * T], qf[:, dc * T:(dc + 1) * T],
                                    kcol, None, ALU.mult)
                    v.tensor_scalar(qf[:, dc * T:(dc + 1) * T], qf[:, dc * T:(dc + 1) * T],
                                    qcol, None, ALU.mult)
                v.drain().then_inc(s_qk, 1)

            for g_ in range(NG):
                trips = 4 * (g_ + 1)
                for kb in range(trips):
                    pb = (sm_n[0]) % 2

                    @blk.tensor
                    def _(te):
                        te.wait_ge(s_qk, hh + 1)
                        if g_ == 0 and kb == 0 and hh == 0:
                            pass
                        if sm_n[0] >= 2:
                            te.wait_ge(s_p, sm_n[0] - 1)
                        if kb == 0:
                            te.wait_ge(s_pv, pv_n[0])
                        for dc in range(ndc):
                            mm = te.matmul(ps_s[pb][:],
                                           kf[:, dc * T + kb * 128:dc * T + (kb + 1) * 128],
                                           qf[:, dc * T + g_ * 512:dc * T + (g_ + 1) * 512],
                                           start=(dc == 0), stop=(dc == ndc - 1))
                            if dc == ndc - 1:
                                mm.then_inc(s_sm, 1)
                    sm_n[0] += 1

                    @blk.scalar
                    def _(sc):
                        sc.wait_ge(s_sm, sm_n[0])
                        sc.activation(pt[:, kb * 512:(kb + 1) * 512], ps_s[pb][:],
                                      AF.Sigmoid, scale=SCL)
                        sc.drain().then_inc(s_sg, 1)
                    sg_n[0] += 1

                    @blk.vector
                    def _(v):
                        v.wait_ge(s_sg, sg_n[0])
                        jstar = kb - 4 * g_
                        if jstar >= 0:
                            for j in range(4):
                                sl = pt[:, kb * 512 + j * 128: kb * 512 + (j + 1) * 128]
                                if j < jstar:
                                    v.memset(sl, 0.0)
                                elif j == jstar:
                                    v.tensor_tensor(sl, sl, trit[:], ALU.mult)
                        v.drain().then_inc(s_p, 1)
                    p_n[0] += 1

                for dg in range(2):
                    for j in range(4):
                        pj = j

                        @blk.tensor
                        def _(te):
                            te.wait_ge(s_p, p_n[0])
                            te.wait_ge(s_b, 7 + hh * 2)
                            if pv_n[0] >= 4:
                                te.wait_ge(s_ev, pv_n[0] - 3)
                            for kb in range(trips):
                                mm = te.matmul(ps_o[pj][:],
                                               pt[:, kb * 512 + j * 128:kb * 512 + (j + 1) * 128],
                                               xv[:, kb * HD + dg * 512:kb * HD + (dg + 1) * 512],
                                               start=(kb == 0), stop=(kb == trips - 1))
                                if kb == trips - 1:
                                    mm.then_inc(s_pv, 1)
                        pv_n[0] += 1

                        @blk.vector
                        def _(v):
                            v.wait_ge(s_pv, pv_n[0])
                            v.wait_ge(s_b, 7 + hh * 2)
                            if ev_n[0] >= 2:
                                v.wait_ge(st, (ev_n[0] - 1) * 16)
                            v.tensor_tensor(outb[ev_n[0] % 2][:], ps_o[pj][:],
                                            vbb[:, dg * 512:(dg + 1) * 512], ALU.mult)
                            v.drain().then_inc(s_ev, 1)
                        ev_n[0] += 1

                        @blk.sync
                        def _(s):
                            s.wait_ge(s_ev, ev_n[0])
                            s.dma_start(
                                out=y_d[g_ * 512 + j * 128:g_ * 512 + (j + 1) * 128,
                                        c0 + dg * 512:c0 + (dg + 1) * 512],
                                in_=outb[(ev_n[0] - 1) % 2][:]).then_inc(st, 16)
                        st_n[0] += 16
        @blk.sync
        def _(s):
            s.wait_ge(st, st_n[0])
    return nc


def run_attn1(xn_full, bv_q, bv_k, bv_v):
    nc = _cached(("attn1",), build_attn1)
    idn32 = np.eye(128, dtype=np.float32)
    onesr = np.ones((1, 128), np.float32)
    i_ = np.arange(128)
    triT = (i_[None, :] >= i_[:, None]).astype(BF)
    maps = []
    for c in range(NCORE):
        b_, half = c // 2, c % 2
        h0 = half * 2
        xnb = np.ascontiguousarray(xn_full[b_][:, h0 * 1024:(h0 + 2) * 1024])
        bvs = np.concatenate([bv_q[h0:h0 + 2], bv_k[h0:h0 + 2], bv_v[h0:h0 + 2]], 0).astype(np.float32)
        maps.append({"xnb": xnb, "bvs": np.ascontiguousarray(bvs),
                     "bvf": np.ascontiguousarray(bvs.reshape(1, -1)),
                     "idn32": idn32, "onesr": onesr, "triT": triT})
    res = run_bass_kernel_spmd(nc, maps, core_ids=list(range(NCORE)))
    out = np.zeros((B, T, C), BF)
    for c in range(NCORE):
        b_, half = c // 2, c % 2
        out[b_][:, half * 2048:(half + 1) * 2048] = res.results[c]["y"]
    return out


def build_attn_qk(layer, dbg_phase1=False, dbg_heads=16):
    NQ = 2048
    NKV = 2048 if layer == 0 else 512
    QKVD = NQ + 2 * NKV
    nc = bass.Bass("TRN2", target_bir_lowering=False)
    xn_h = nc.dram_tensor("xnb", [T, C], BF16, kind="ExternalInput")
    w_h = nc.dram_tensor("wcat", [C, QKVD], F32, kind="ExternalInput")
    ct_d = nc.dram_tensor("ropeC", [T, 128], F32, kind="ExternalInput").ap()
    sx_d = nc.dram_tensor("ropeSx", [T, 128], F32, kind="ExternalInput").ap()
    cst_d = nc.dram_tensor("cstb", [128, 128], F32, kind="ExternalInput").ap()
    oneb_d = nc.dram_tensor("onesb", [128, 128], BF16, kind="ExternalInput").ap()
    idb_d = nc.dram_tensor("idnb", [128, 128], BF16, kind="ExternalInput").ap()
    id3_d = nc.dram_tensor("idn32", [128, 128], F32, kind="ExternalInput").ap()
    tri_d = nc.dram_tensor("triT", [128, 128], BF16, kind="ExternalInput").ap()
    y_d = nc.dram_tensor("y", [T, NQ], BF16, kind="ExternalOutput").ap()
    qkv_h = nc.dram_tensor("qkv", [T, QKVD], BF16,
                           kind="ExternalOutput" if dbg_phase1 else "Internal")
    qkv_d = qkv_h.ap()
    nk = C // 128
    ntt = T // 128
    nog = QKVD // 512
    NG = T // 512
    NKB = T // 128
    SCL = 1.0 / math.sqrt(128.0)
    with ExitStack() as es:
        def sb(name, shape, dt):
            return es.enter_context(nc.sbuf_tensor(name, shape, dt))
        xf = sb("xf", [128, nk * T], BF16)
        wblk = sb("wblk", [128, nk * 512], BF16)
        ctall = sb("ctall", [128, 16 * 128], F32)
        sxall = sb("sxall", [128, 16 * 128], F32)
        evf = sb("evf", [128, 512], F32)
        rsc = sb("rsc", [128, 512], F32)
        outq = [sb(f"outq{i}", [128, 512], BF16) for i in range(2)]
        cstt = sb("cstt", [128, 128], F32)
        onesb = sb("onesb_t", [128, 128], BF16)
        idnb = sb("idnb_t", [128, 128], BF16)
        idn3 = sb("idn3_t", [128, 128], F32)
        trit = sb("trit", [128, 128], BF16)
        qh = [xf[:, i * 3 * T:i * 3 * T + T] for i in range(2)]
        kh = [xf[:, i * 3 * T + T:i * 3 * T + 2 * T] for i in range(2)]
        vh = [xf[:, i * 3 * T + 2 * T:i * 3 * T + 2 * T + NKB * 128] for i in range(2)]
        pt = xf[:, 6 * T:6 * T + NKB * 512]
        ofm = xf[:, 6 * T + NKB * 512:6 * T + NKB * 512 + 512]
        lrow = sb("lrow", [1, 512], F32)
        lcol = sb("lcol", [128, 4], F32)
        outb = [sb(f"outb{i}", [128, 128], BF16) for i in range(2)]
        ps_p = [es.enter_context(nc.psum_tensor(f"psp{i}", [128, 512], F32)) for i in range(2)]
        ps_s = [es.enter_context(nc.psum_tensor(f"pss{i}", [128, 512], F32)) for i in range(2)]
        ps_l = es.enter_context(nc.psum_tensor("psl", [128, 512], F32))
        ps_o = es.enter_context(nc.psum_tensor("pso", [128, 512], F32))
        ps_tr = es.enter_context(nc.psum_tensor("pstr", [128, 512], F32))
        ps_tb = es.enter_context(nc.psum_tensor("pstb", [128, 512], BF16))
        ld = es.enter_context(nc.semaphore("ld"))
        ldw = es.enter_context(nc.semaphore("ldw"))
        ldh = es.enter_context(nc.semaphore("ldh"))
        s_pm = es.enter_context(nc.semaphore("s_pm"))
        s_pe = es.enter_context(nc.semaphore("s_pe"))
        s_sm = es.enter_context(nc.semaphore("s_sm"))
        s_sg = es.enter_context(nc.semaphore("s_sg"))
        s_p = es.enter_context(nc.semaphore("s_p"))
        s_l = es.enter_context(nc.semaphore("s_l"))
        s_pv = es.enter_context(nc.semaphore("s_pv"))
        s_tr = es.enter_context(nc.semaphore("s_tr"))
        s_ev = es.enter_context(nc.semaphore("s_ev"))
        st = es.enter_context(nc.semaphore("st"))
        stq = es.enter_context(nc.semaphore("stq"))
        blk = es.enter_context(nc.Block())

        @blk.scalar
        def _(sc):
            sc.preload_act_table(AF.Exp)

        @blk.gpsimd
        def _(g):
            g.dma_start(out=cstt[:], in_=cst_d[:, :]).then_inc(ld, 16)
            g.dma_start(out=onesb[:], in_=oneb_d[:, :]).then_inc(ld, 16)
            g.dma_start(out=idnb[:], in_=idb_d[:, :]).then_inc(ld, 16)
            g.dma_start(out=idn3[:], in_=id3_d[:, :]).then_inc(ld, 16)
            g.dma_start(out=trit[:], in_=tri_d[:, :]).then_inc(ld, 16)

        @blk.sync
        def _(s):
            for kc in range(nk):
                s.dma_start_transpose(
                    out=xf[:, kc * T:(kc + 1) * T],
                    in_=xn_h.ap()[:, kc * 128:(kc + 1) * 128]).then_inc(ld, 16)

        @blk.gpsimd
        def _(g):
            g.dma_start(out=ctall.rearrange("p (tt j) -> p tt j", j=128),
                        in_=ct_d.rearrange("(tt p) j -> p tt j", p=128)).then_inc(ldh, 16)
            g.dma_start(out=sxall.rearrange("p (tt j) -> p tt j", j=128),
                        in_=sx_d.rearrange("(tt p) j -> p tt j", p=128)).then_inc(ldh, 16)

        ldw_n = [0]
        pm_n = [0]
        pe_n = [0]
        stq_n = [0]

        def load_w(og):
            @blk.gpsimd
            def _(g):
                if og > 0:
                    g.wait_ge(s_pm, og * ntt)
                src = bass.AP(w_h, og * 512, [[QKVD, 128], [128 * QKVD, nk], [1, 512]])
                g.dma_start(out=wblk.rearrange("p (kc n) -> p kc n", n=512),
                            in_=src).then_inc(ldw, 16)

        for og in range(nog):
            load_w(og)
            if layer == 0:
                is_rope = og < 8
            else:
                is_rope = og < 5
            for tt in range(ntt):
                pb = pm_n[0] % 2

                @blk.tensor
                def _(te):
                    if og == 0 and tt == 0:
                        te.wait_ge(ld, 80 + nk * 16)
                    te.wait_ge(ldw, (og + 1) * 16)
                    if pm_n[0] >= 2:
                        te.wait_ge(s_pe, pm_n[0] - 1)
                    for kc in range(nk):
                        mm = te.matmul(ps_p[pb][:],
                                       xf[:, kc * T + tt * 128:kc * T + (tt + 1) * 128],
                                       wblk[:, kc * 512:(kc + 1) * 512],
                                       start=(kc == 0), stop=(kc == nk - 1))
                        if kc == nk - 1:
                            mm.then_inc(s_pm, 1)
                pm_n[0] += 1

                if is_rope:
                    @blk.vector
                    def _(v):
                        v.wait_ge(s_pm, pm_n[0])
                        v.wait_ge(ldh, 32)
                        if pe_n[0] >= 2:
                            v.wait_ge(stq, (pe_n[0] - 1) * 16)
                        v.tensor_copy(evf[:], ps_p[pb][:])
                        v.drain()
                        crep = bass.AP(ctall, tt * 128, [[16 * 128, 128], [0, 4], [1, 128]])
                        swap = bass.AP(evf, 64, [[512, 128], [128, 4], [-64, 2], [1, 64]])
                        sxr = bass.AP(sxall, tt * 128, [[16 * 128, 128], [0, 4], [64, 2], [1, 64]])
                        v.tensor_tensor(
                            rsc[:].rearrange("p (h r j) -> p h r j", h=4, r=2),
                            swap, sxr, ALU.mult)
                        v.tensor_tensor(
                            evf[:].rearrange("p (h j) -> p h j", j=128),
                            evf[:].rearrange("p (h j) -> p h j", j=128),
                            crep, ALU.mult)
                        v.drain()
                        v.tensor_tensor(outq[pb][:], evf[:], rsc[:], ALU.add)
                        v.drain().then_inc(s_pe, 1)
                else:
                    @blk.vector
                    def _(v):
                        v.wait_ge(s_pm, pm_n[0])
                        if pe_n[0] >= 2:
                            v.wait_ge(stq, (pe_n[0] - 1) * 16)
                        v.tensor_copy(outq[pb][:], ps_p[pb][:])
                        v.drain().then_inc(s_pe, 1)
                pe_n[0] += 1

                @blk.sync
                def _(s2):
                    s2.wait_ge(s_pe, pe_n[0])
                    s2.dma_start(out=qkv_d[tt * 128:(tt + 1) * 128, og * 512:(og + 1) * 512],
                                 in_=outq[pb][:]).then_inc(stq, 16)
                stq_n[0] += 16

        sm_n = [0]
        sg_n = [0]
        p_n = [0]
        l_n = [0]
        pv_n = [0]
        tr_n = [0]
        ev_n = [0]
        st_n = [0]
        ldh_n = [32]

        if dbg_phase1:
            @blk.sync
            def _(s2):
                s2.wait_ge(stq, stq_n[0])
                s2.dma_start(out=y_d[0:128, 0:512], in_=outq[0][:]).then_inc(st, 16)
                s2.wait_ge(st, 16)
            es.close() if False else None
        def head_cols(head):
            if layer == 0:
                return head * 128, NQ + head * 128, NQ + NKV + head * 128
            return head * 128, NQ + (head // 4) * 128, NQ + NKV + (head // 4) * 128

        pv_marks = {}

        def load_head(head):
            hb = head % 2
            qc, kc0, vc0 = head_cols(head)

            @blk.sync
            def _(s2):
                if head == 0:
                    s2.wait_ge(stq, stq_n[0])
                elif head >= 2:
                    s2.wait_ge(s_pv, pv_marks[head - 2])
                s2.dma_start_transpose(out=qh[hb][:], in_=qkv_d[:, qc:qc + 128]).then_inc(ldh, 16)
                s2.dma_start_transpose(out=kh[hb][:], in_=qkv_d[:, kc0:kc0 + 128]).then_inc(ldh, 16)
                for kb in range(NKB):
                    s2.dma_start(out=vh[hb][:, kb * 128:(kb + 1) * 128],
                                 in_=qkv_d[kb * 128:(kb + 1) * 128, vc0:vc0 + 128]).then_inc(ldh, 16)
            ldh_n[0] += (2 + NKB) * 16

        load_head(0 if not dbg_phase1 else 16)
        if dbg_heads > 1:
            load_head(1)
        for head in range(0 if not dbg_phase1 else 16, dbg_heads):
            hb = head % 2
            qh_c, kh_c, vh_c = qh[hb], kh[hb], vh[hb]
            ldh_need = 32 + (head + 1) * (2 + NKB) * 16

            for g_ in range(NG):
                trips = 4 * (g_ + 1)
                for kb in range(trips):
                    pb = sm_n[0] % 2

                    @blk.tensor
                    def _(te):
                        te.wait_ge(ldh, ldh_need)
                        if sm_n[0] >= 2:
                            te.wait_ge(s_sg, sm_n[0] - 1)
                        if kb == 0:
                            te.wait_ge(s_pv, pv_n[0])
                        te.matmul(ps_s[pb][:], kh_c[:, kb * 128:(kb + 1) * 128],
                                  qh_c[:, g_ * 512:(g_ + 1) * 512],
                                  start=True, stop=True).then_inc(s_sm, 1)
                    sm_n[0] += 1

                    @blk.scalar
                    def _(sc):
                        sc.wait_ge(s_sm, sm_n[0])
                        sc.activation(pt[:, kb * 512:(kb + 1) * 512], ps_s[pb][:],
                                      AF.Exp, bias=cstt[:, 0:1], scale=SCL)
                        sc.drain().then_inc(s_sg, 1)
                    sg_n[0] += 1

                    jstar = kb - 4 * g_
                    if jstar >= 0:
                        @blk.vector
                        def _(v):
                            v.wait_ge(s_sg, sg_n[0])
                            for j in range(4):
                                sl = pt[:, kb * 512 + j * 128: kb * 512 + (j + 1) * 128]
                                if j < jstar:
                                    v.memset(sl, 0.0)
                                elif j == jstar:
                                    v.tensor_tensor(sl, sl, trit[:], ALU.mult)
                            v.drain().then_inc(s_p, 1)
                        p_n[0] += 1

                @blk.tensor
                def _(te):
                    te.wait_ge(s_sg, sg_n[0])
                    te.wait_ge(s_p, p_n[0])
                    te.wait_ge(s_l, max(0, l_n[0] - 2))
                    for kb in range(trips):
                        mm = te.matmul(ps_l[0:1, :], onesb[:, 0:1], pt[:, kb * 512:(kb + 1) * 512],
                                       start=(kb == 0), stop=(kb == trips - 1))
                        if kb == trips - 1:
                            mm.then_inc(s_l, 1)
                l_n[0] += 1

                @blk.vector
                def _(v):
                    v.wait_ge(s_l, l_n[0])
                    v.tensor_copy(lrow[:], ps_l[0:1, :])
                    v.drain().then_inc(s_l, 1)
                l_n[0] += 1

                @blk.tensor
                def _(te):
                    te.wait_ge(s_l, l_n[0])
                    for j in range(4):
                        mm = te.transpose(ps_tr[:, j:j + 1], lrow[0:1, j * 128:(j + 1) * 128],
                                          idn3[0:1, 0:1])
                        if j == 3:
                            mm.then_inc(s_l, 1)
                l_n[0] += 1

                @blk.vector
                def _(v):
                    v.wait_ge(s_l, l_n[0])
                    v.tensor_copy(lcol[:], ps_tr[:, 0:4])
                    v.drain()
                    v.reciprocal(lcol[:], lcol[:])
                    v.drain().then_inc(s_l, 1)
                l_n[0] += 1

                @blk.tensor
                def _(te):
                    te.wait_ge(s_sg, sg_n[0])
                    te.wait_ge(s_p, p_n[0])
                    if pv_n[0] >= 1:
                        te.wait_ge(s_tr, tr_n[0])
                    for kb in range(trips):
                        mm = te.matmul(ps_o[:], vh_c[:, kb * 128:(kb + 1) * 128],
                                       pt[:, kb * 512:(kb + 1) * 512],
                                       start=(kb == 0), stop=(kb == trips - 1))
                        if kb == trips - 1:
                            mm.then_inc(s_pv, 1)
                pv_n[0] += 1

                @blk.vector
                def _(v):
                    v.wait_ge(s_pv, pv_n[0])
                    v.tensor_copy(ofm[:], ps_o[:])
                    v.drain().then_inc(s_pv, 1)
                pv_n[0] += 1

                @blk.tensor
                def _(te):
                    te.wait_ge(s_pv, pv_n[0])
                    if tr_n[0] >= 1:
                        te.wait_ge(s_ev, ev_n[0])
                    for j in range(4):
                        mm = te.transpose(ps_tb[:, j * 128:(j + 1) * 128],
                                          ofm[:, j * 128:(j + 1) * 128], idnb[:])
                        if j == 3:
                            mm.then_inc(s_tr, 1)
                tr_n[0] += 1

                for j in range(4):
                    @blk.vector
                    def _(v):
                        v.wait_ge(s_tr, tr_n[0])
                        if ev_n[0] >= 2:
                            v.wait_ge(st, (ev_n[0] - 1) * 16)
                        v.tensor_scalar(outb[ev_n[0] % 2][:], ps_tb[:, j * 128:(j + 1) * 128],
                                        lcol[:, j:j + 1], None, ALU.mult)
                        v.drain().then_inc(s_ev, 1)
                    ev_n[0] += 1

                    @blk.sync
                    def _(s2):
                        s2.wait_ge(s_ev, ev_n[0])
                        s2.dma_start(out=y_d[g_ * 512 + j * 128:g_ * 512 + (j + 1) * 128,
                                             head_cols(head)[0]:head_cols(head)[0] + 128],
                                     in_=outb[(ev_n[0] - 1) % 2][:]).then_inc(st, 16)
                    st_n[0] += 16
            pv_marks[head] = pv_n[0]
            if head + 2 < dbg_heads:
                load_head(head + 2)
        @blk.sync
        def _(s2):
            s2.wait_ge(st, st_n[0])
    return nc


def _rope_tables(layer):
    pos = np.arange(T, dtype=np.float64)
    th = 1.0 / 10000.0 ** (np.arange(0, 128, 2, dtype=np.float64) / 128.0)
    ptab = np.outer(pos, th)
    c = np.cos(ptab).astype(np.float32)
    s = np.sin(ptab).astype(np.float32)
    Ct = np.concatenate([c, c], 1)
    Sx = np.concatenate([-s, s], 1)
    return Ct, Sx


def run_attn_qk(layer, xn_full, inputs, dbg_phase1=False, dbg_heads=16):
    nc = _cached(("attnqk", layer, dbg_phase1, dbg_heads),
                 lambda: build_attn_qk(layer, dbg_phase1, dbg_heads))
    Ct, Sx = _rope_tables(layer)
    cstb = np.zeros((128, 128), np.float32)
    cstb[:, 0] = -25.0
    onesb = np.zeros((128, 128), np.float32)
    onesb[:, 0] = 1.0
    onesb = onesb.astype(BF)
    idnb = np.eye(128, dtype=BF)
    idn32 = np.eye(128, dtype=np.float32)
    i_ = np.arange(128)
    triT = (i_[None, :] >= i_[:, None]).astype(BF)
    if layer == 0:
        wqT = np.asarray(inputs["L0_attn_wq"]).T
        wkT = np.asarray(inputs["L0_attn_wk"]).T
        wvT = np.asarray(inputs["L0_attn_wv"]).T
    else:
        WT = np.asarray(inputs["L2_attn_Wqkv"]).T
    maps = []
    for cix in range(NCORE):
        b_, half = cix // 2, cix % 2
        if layer == 0:
            wcat = np.concatenate([wqT[:, half * 2048:(half + 1) * 2048],
                                   wkT[:, half * 2048:(half + 1) * 2048],
                                   wvT[:, half * 2048:(half + 1) * 2048]], 1)
        else:
            q = WT[:, half * 2048:(half + 1) * 2048]
            k = WT[:, 4096 + half * 512:4096 + (half + 1) * 512]
            v = WT[:, 5120 + half * 512:5120 + (half + 1) * 512]
            wcat = np.concatenate([q, k, v], 1)
        maps.append({"xnb": np.ascontiguousarray(xn_full[b_]),
                     "wcat": np.ascontiguousarray(wcat.astype(np.float32)),
                     "ropeC": Ct, "ropeSx": Sx, "cstb": cstb, "onesb": onesb,
                     "idnb": idnb, "idn32": idn32, "triT": triT})
    res = run_bass_kernel_spmd(nc, maps, core_ids=list(range(NCORE)))
    if dbg_phase1:
        return [r["qkv"] for r in res.results]
    out = np.zeros((B, T, C), BF)
    for cix in range(NCORE):
        b_, half = cix // 2, cix % 2
        out[b_][:, half * 2048:(half + 1) * 2048] = res.results[cix]["y"]
    return out


def build_ffn1():
    nc = bass.Bass("TRN2", target_bir_lowering=False)
    x_d = nc.dram_tensor("x", [TPC, C], BF16, kind="ExternalInput").ap()
    w_h = nc.dram_tensor("w", [C, C], F32, kind="ExternalInput")
    b_d = nc.dram_tensor("b", [1, C], F32, kind="ExternalInput").ap()
    aux_h = nc.dram_tensor("waux", [C, 128], F32, kind="ExternalInput")
    baux_d = nc.dram_tensor("baux", [1, 68], F32, kind="ExternalInput").ap()
    bt_h = nc.dram_tensor("bT", [64, C], F32, kind="ExternalInput")
    one_d = nc.dram_tensor("onesr", [1, 128], F32, kind="ExternalInput").ap()
    idb_d = nc.dram_tensor("idnb", [128, 128], BF16, kind="ExternalInput").ap()
    y_d = nc.dram_tensor("y", [TPC, C], BF16, kind="ExternalOutput").ap()
    nt = TPC // 128
    nk = C // 128
    no = C // 512
    with ExitStack() as es:
        def sb(name, shape, dt):
            return es.enter_context(nc.sbuf_tensor(name, shape, dt))
        xf = sb("xf", [128, nk * TPC], BF16)
        wblk = [sb(f"wblk{i}", [128, nk * 512], BF16) for i in range(2)]
        auxb = sb("auxb", [128, nk * 128], BF16)
        btb = sb("btb", [64, C], BF16)
        bias = sb("bias", [1, C], F32)
        bauxb = sb("bauxb", [1, 68], F32)
        ones = sb("ones", [1, 128], F32)
        idnb = sb("idnb_t", [128, 128], BF16)
        us = sb("us", [128, 68], F32)
        msk = sb("msk", [128, 8], F32)
        umb = sb("umb", [128, 64], BF16)
        umT = sb("umT", [64, nt * 128], BF16)
        outb = [sb(f"outb{i}", [128, 512], BF16) for i in range(2)]
        ps_u = es.enter_context(nc.psum_tensor("psu", [128, 512], F32))
        ps_tr = es.enter_context(nc.psum_tensor("pstr", [128, 512], BF16))
        ps = [es.enter_context(nc.psum_tensor(f"ps{i}", [128, 512], F32)) for i in range(2)]
        ldx = es.enter_context(nc.semaphore("ldx"))
        ldw = es.enter_context(nc.semaphore("ldw"))
        s_um = es.enter_context(nc.semaphore("s_um"))
        s_mm = es.enter_context(nc.semaphore("s_mm"))
        s_ev = es.enter_context(nc.semaphore("s_ev"))
        st = es.enter_context(nc.semaphore("st"))
        blk = es.enter_context(nc.Block())

        @blk.gpsimd
        def _(g):
            g.dma_start(out=bias[:], in_=b_d[:, :]).then_inc(ldx, 16)
            g.dma_start(out=bauxb[:], in_=baux_d[:, :]).then_inc(ldx, 16)
            g.dma_start(out=ones[:], in_=one_d[:, :]).then_inc(ldx, 16)
            g.dma_start(out=idnb[:], in_=idb_d[:, :]).then_inc(ldx, 16)
            src = bass.AP(aux_h, 0, [[128, 128], [128 * 128, nk], [1, 128]])
            g.dma_start(out=auxb.rearrange("p (kc n) -> p kc n", n=128), in_=src).then_inc(ldx, 16)
            g.dma_start(out=btb[:], in_=bt_h.ap()[:, :]).then_inc(ldx, 16)

        @blk.sync
        def _(s):
            for kc in range(nk):
                s.dma_start_transpose(
                    out=xf[:, kc * TPC:(kc + 1) * TPC],
                    in_=x_d[:, kc * 128:(kc + 1) * 128]).then_inc(ldx, 16)
        ldx_all = 96 + nk * 16

        um_n = [0]
        for t in range(nt):
            @blk.tensor
            def _(te):
                te.wait_ge(ldx, ldx_all)
                if t > 0:
                    te.wait_ge(s_um, um_n[0])
                for kc in range(nk):
                    te.matmul(ps_u[:, :68], xf[:, kc * TPC + t * 128:kc * TPC + (t + 1) * 128],
                              auxb[:, kc * 128:kc * 128 + 68], start=(kc == 0), stop=False)
                te.matmul(ps_u[:, :68], ones[:, :], bauxb[:, :],
                          start=False, stop=True).then_inc(s_um, 1)
            um_n[0] += 1

            @blk.vector
            def _(v):
                v.wait_ge(s_um, um_n[0])
                v.tensor_copy(us[:], ps_u[:, :68])
                v.drain()
                v.tensor_reduce(msk[:, 0:1], us[:, 64:68], mybir.AxisListType.X, ALU.max)
                v.tensor_tensor(msk[:, 1:2], us[:, 64:65], us[:, 65:66], ALU.max)
                v.drain()
                v.tensor_tensor(msk[:, 2:3], msk[:, 1:2], us[:, 66:67], ALU.max)
                v.drain()
                v.tensor_scalar(msk[:, 3:4], us[:, 64:65], msk[:, 0:1], None, ALU.is_lt)
                v.tensor_scalar(msk[:, 4:5], msk[:, 1:2], msk[:, 0:1], None, ALU.is_lt)
                v.tensor_scalar(msk[:, 5:6], msk[:, 2:3], msk[:, 0:1], None, ALU.is_lt)
                v.drain()
                v.tensor_scalar(us[:, 8:16], us[:, 8:16], msk[:, 3:4], None, ALU.mult)
                v.tensor_scalar(us[:, 16:32], us[:, 16:32], msk[:, 4:5], None, ALU.mult)
                v.tensor_scalar(us[:, 32:64], us[:, 32:64], msk[:, 5:6], None, ALU.mult)
                v.drain()
                v.tensor_copy(umb[:], us[:, 0:64])
                v.drain().then_inc(s_um, 1)
            um_n[0] += 1

            @blk.tensor
            def _(te):
                te.wait_ge(s_um, um_n[0])
                te.transpose(ps_tr[0:64, 0:128], umb[:], idnb[:]).then_inc(s_um, 1)
            um_n[0] += 1

            @blk.vector
            def _(v):
                v.wait_ge(s_um, um_n[0])
                v.tensor_copy(umT[:, t * 128:(t + 1) * 128], ps_tr[0:64, 0:128])
                v.drain().then_inc(s_um, 1)
            um_n[0] += 1

        mm_n = [0]
        ev_n = [0]
        st_n = [0]

        def load_w(og, buf):
            @blk.gpsimd
            def _(g):
                if og >= 2:
                    g.wait_ge(s_mm, (og - 1) * nt)
                src = bass.AP(w_h, og * 512, [[C, 128], [128 * C, nk], [1, 512]])
                g.dma_start(out=wblk[buf].rearrange("p (kc n) -> p kc n", n=512),
                            in_=src).then_inc(ldw, 16)

        load_w(0, 0)
        load_w(1, 1)
        for og in range(no):
            buf = og % 2
            for t in range(nt):
                pb = mm_n[0] % 2

                @blk.tensor
                def _(te):
                    te.wait_ge(ldw, (og + 1) * 16)
                    if og == 0 and t == 0:
                        te.wait_ge(s_um, um_n[0])
                    if mm_n[0] >= 2:
                        te.wait_ge(s_ev, mm_n[0] - 1)
                    for kc in range(nk):
                        te.matmul(ps[pb][:], xf[:, kc * TPC + t * 128:kc * TPC + (t + 1) * 128],
                                  wblk[buf][:, kc * 512:(kc + 1) * 512],
                                  start=(kc == 0), stop=False)
                    te.matmul(ps[pb][:], umT[:, t * 128:(t + 1) * 128],
                              btb[:, og * 512:(og + 1) * 512], start=False, stop=False)
                    te.matmul(ps[pb][:], ones[:, :], bias[:, og * 512:(og + 1) * 512],
                              start=False, stop=True).then_inc(s_mm, 1)
                mm_n[0] += 1

                @blk.vector
                def _(v):
                    v.wait_ge(s_mm, mm_n[0])
                    if ev_n[0] >= 2:
                        v.wait_ge(st, (ev_n[0] - 1) * 16)
                    v.tensor_copy(outb[pb][:], ps[pb][:])
                    v.drain().then_inc(s_ev, 1)
                ev_n[0] += 1

                @blk.sync
                def _(s):
                    s.wait_ge(s_ev, ev_n[0])
                    s.dma_start(out=y_d[t * 128:(t + 1) * 128, og * 512:(og + 1) * 512],
                                in_=outb[pb][:]).then_inc(st, 16)
                st_n[0] += 16
            if og + 2 < no:
                load_w(og + 2, buf)
        @blk.sync
        def _(s):
            s.wait_ge(st, st_n[0])
    return nc


def run_ffn1(xs, inputs):
    nc = _cached(("ffn1",), build_ffn1)
    A = np.asarray(inputs["L1_ffn_A"]).astype(np.float32)
    gw = np.asarray(inputs["L1_ffn_gate_w"]).astype(np.float32)
    waux = np.concatenate([A.T, gw.T, np.zeros((C, 60), np.float32)], 1)
    baux = np.zeros((1, 68), np.float32)
    baux[0, 64:] = np.asarray(inputs["L1_ffn_gate_b"])
    m = {
        "w": np.ascontiguousarray(np.asarray(inputs["L1_ffn_W"]).astype(np.float32).T),
        "b": np.asarray(inputs["L1_ffn_b"]).reshape(1, C).astype(np.float32),
        "waux": np.ascontiguousarray(waux),
        "baux": baux,
        "bT": np.ascontiguousarray(np.asarray(inputs["L1_ffn_B"]).astype(np.float32).T),
        "onesr": np.ones((1, 128), np.float32),
        "idnb": np.eye(128, dtype=BF),
    }
    maps = [dict(m, x=xs[i]) for i in range(NCORE)]
    res = run_bass_kernel_spmd(nc, maps, core_ids=list(range(NCORE)))
    return [r["y"] for r in res.results]


def build_ffn2():
    nc = bass.Bass("TRN2", target_bir_lowering=False)
    x_d = nc.dram_tensor("x", [TPC, C], BF16, kind="ExternalInput").ap()
    w_h = nc.dram_tensor("w", [C, C], F32, kind="ExternalInput")
    b_d = nc.dram_tensor("b", [1, C], F32, kind="ExternalInput").ap()
    aux_h = nc.dram_tensor("waux", [C, 128], F32, kind="ExternalInput")
    cor_d = nc.dram_tensor("coresF", [128, 16], F32, kind="ExternalInput").ap()
    bw_d = nc.dram_tensor("bwT", [16, C], F32, kind="ExternalInput").ap()
    one_d = nc.dram_tensor("onesr", [1, 128], F32, kind="ExternalInput").ap()
    idb_d = nc.dram_tensor("idnb", [128, 128], BF16, kind="ExternalInput").ap()
    y_d = nc.dram_tensor("y", [TPC, C], BF16, kind="ExternalOutput").ap()
    nt = TPC // 128
    nk = C // 128
    no = C // 512
    with ExitStack() as es:
        def sb(name, shape, dt):
            return es.enter_context(nc.sbuf_tensor(name, shape, dt))
        xf = sb("xf", [128, nk * TPC], BF16)
        wblk = [sb(f"wblk{i}", [128, nk * 512], BF16) for i in range(2)]
        auxb = sb("auxb", [128, nk * 128], BF16)
        corb = sb("corb", [128, 16], BF16)
        bwb = sb("bwb", [16, C], BF16)
        bias = sb("bias", [1, C], F32)
        ones = sb("ones", [1, 128], F32)
        idnb = sb("idnb_t", [128, 128], BF16)
        us = sb("us", [128, 24], F32)
        pr = sb("pr", [128, 8], F32)
        nm = sb("nm", [128, 2], F32)
        opb = sb("opb", [128, 128], BF16)
        mxb = sb("mxb", [128, 16], BF16)
        mxT = sb("mxT", [16, nt * 128], BF16)
        outb = [sb(f"outb{i}", [128, 512], BF16) for i in range(2)]
        ps_u = es.enter_context(nc.psum_tensor("psu", [128, 512], F32))
        ps_tr = es.enter_context(nc.psum_tensor("pstr", [128, 512], BF16))
        ps_m = es.enter_context(nc.psum_tensor("psm", [128, 512], F32))
        ps = [es.enter_context(nc.psum_tensor(f"ps{i}", [128, 512], F32)) for i in range(2)]
        ldx = es.enter_context(nc.semaphore("ldx"))
        ldw = es.enter_context(nc.semaphore("ldw"))
        s_um = es.enter_context(nc.semaphore("s_um"))
        s_mm = es.enter_context(nc.semaphore("s_mm"))
        s_ev = es.enter_context(nc.semaphore("s_ev"))
        st = es.enter_context(nc.semaphore("st"))
        blk = es.enter_context(nc.Block())

        @blk.scalar
        def _(sc):
            sc.preload_act_table(AF.Exp)

        @blk.gpsimd
        def _(g):
            g.dma_start(out=bias[:], in_=b_d[:, :]).then_inc(ldx, 16)
            g.dma_start(out=ones[:], in_=one_d[:, :]).then_inc(ldx, 16)
            g.dma_start(out=idnb[:], in_=idb_d[:, :]).then_inc(ldx, 16)
            g.dma_start(out=corb[:], in_=cor_d[:, :]).then_inc(ldx, 16)
            g.dma_start(out=bwb[:], in_=bw_d[:, :]).then_inc(ldx, 16)
            src = bass.AP(aux_h, 0, [[128, 128], [128 * 128, nk], [1, 128]])
            g.dma_start(out=auxb.rearrange("p (kc n) -> p kc n", n=128), in_=src).then_inc(ldx, 16)

        @blk.sync
        def _(s):
            for kc in range(nk):
                s.dma_start_transpose(
                    out=xf[:, kc * TPC:(kc + 1) * TPC],
                    in_=x_d[:, kc * 128:(kc + 1) * 128]).then_inc(ldx, 16)
        ldx_all = 96 + nk * 16

        um_n = [0]
        for t in range(nt):
            @blk.tensor
            def _(te):
                te.wait_ge(ldx, ldx_all)
                if t > 0:
                    te.wait_ge(s_um, um_n[0])
                for kc in range(nk):
                    te.matmul(ps_u[:, :24], xf[:, kc * TPC + t * 128:kc * TPC + (t + 1) * 128],
                              auxb[:, kc * 128:kc * 128 + 24],
                              start=(kc == 0), stop=(kc == nk - 1)).then_inc(s_um, 1) \
                        if kc == nk - 1 else te.matmul(
                            ps_u[:, :24], xf[:, kc * TPC + t * 128:kc * TPC + (t + 1) * 128],
                            auxb[:, kc * 128:kc * 128 + 24], start=(kc == 0), stop=False)
            um_n[0] += 1

            @blk.vector
            def _(v):
                v.wait_ge(s_um, um_n[0])
                v.tensor_copy(us[:], ps_u[:, :24])
                v.drain()
                v.tensor_reduce(nm[:, 0:1], us[:, 16:24], mybir.AxisListType.X, ALU.max,
                                negate=True)
                v.drain().then_inc(s_um, 1)
            um_n[0] += 1

            @blk.scalar
            def _(sc):
                sc.wait_ge(s_um, um_n[0])
                sc.activation(pr[:], us[:, 16:24], AF.Exp, bias=nm[:, 0:1],
                              accum_out=nm[:, 1:2])
                sc.drain().then_inc(s_um, 1)
            um_n[0] += 1

            @blk.vector
            def _(v):
                v.wait_ge(s_um, um_n[0])
                v.reciprocal(nm[:, 1:2], nm[:, 1:2])
                v.drain()
                v.tensor_scalar(pr[:], pr[:], nm[:, 1:2], None, ALU.mult)
                v.drain()
                for e in range(8):
                    v.tensor_scalar(opb[:, e * 16:(e + 1) * 16], us[:, 0:16],
                                    pr[:, e:e + 1], None, ALU.mult)
                v.drain().then_inc(s_um, 1)
            um_n[0] += 1

            @blk.tensor
            def _(te):
                te.wait_ge(s_um, um_n[0])
                te.transpose(ps_tr[:, 0:128], opb[:], idnb[:]).then_inc(s_um, 1)
            um_n[0] += 1

            @blk.vector
            def _(v):
                v.wait_ge(s_um, um_n[0])
                v.tensor_copy(opb[:], ps_tr[:, 0:128])
                v.drain().then_inc(s_um, 1)
            um_n[0] += 1

            @blk.tensor
            def _(te):
                te.wait_ge(s_um, um_n[0])
                te.matmul(ps_m[:, :16], opb[:], corb[:], start=True, stop=True).then_inc(s_um, 1)
            um_n[0] += 1

            @blk.vector
            def _(v):
                v.wait_ge(s_um, um_n[0])
                v.tensor_scalar(mxb[:], ps_m[:, :16], 2.0, None, ALU.mult)
                v.drain().then_inc(s_um, 1)
            um_n[0] += 1

            @blk.tensor
            def _(te):
                te.wait_ge(s_um, um_n[0])
                te.transpose(ps_tr[0:16, 0:128], mxb[:], idnb[:]).then_inc(s_um, 1)
            um_n[0] += 1

            @blk.vector
            def _(v):
                v.wait_ge(s_um, um_n[0])
                v.tensor_copy(mxT[:, t * 128:(t + 1) * 128], ps_tr[0:16, 0:128])
                v.drain().then_inc(s_um, 1)
            um_n[0] += 1

        mm_n = [0]
        ev_n = [0]
        st_n = [0]

        def load_w(og, buf):
            @blk.gpsimd
            def _(g):
                if og >= 2:
                    g.wait_ge(s_mm, (og - 1) * nt)
                src = bass.AP(w_h, og * 512, [[C, 128], [128 * C, nk], [1, 512]])
                g.dma_start(out=wblk[buf].rearrange("p (kc n) -> p kc n", n=512),
                            in_=src).then_inc(ldw, 16)

        load_w(0, 0)
        load_w(1, 1)
        for og in range(no):
            buf = og % 2
            for t in range(nt):
                pb = mm_n[0] % 2

                @blk.tensor
                def _(te):
                    te.wait_ge(ldw, (og + 1) * 16)
                    if og == 0 and t == 0:
                        te.wait_ge(s_um, um_n[0])
                    if mm_n[0] >= 2:
                        te.wait_ge(s_ev, mm_n[0] - 1)
                    for kc in range(nk):
                        te.matmul(ps[pb][:], xf[:, kc * TPC + t * 128:kc * TPC + (t + 1) * 128],
                                  wblk[buf][:, kc * 512:(kc + 1) * 512],
                                  start=(kc == 0), stop=False)
                    te.matmul(ps[pb][:], mxT[:, t * 128:(t + 1) * 128],
                              bwb[:, og * 512:(og + 1) * 512], start=False, stop=False)
                    te.matmul(ps[pb][:], ones[:, :], bias[:, og * 512:(og + 1) * 512],
                              start=False, stop=True).then_inc(s_mm, 1)
                mm_n[0] += 1

                @blk.vector
                def _(v):
                    v.wait_ge(s_mm, mm_n[0])
                    if ev_n[0] >= 2:
                        v.wait_ge(st, (ev_n[0] - 1) * 16)
                    v.tensor_copy(outb[pb][:], ps[pb][:])
                    v.drain().then_inc(s_ev, 1)
                ev_n[0] += 1

                @blk.sync
                def _(s):
                    s.wait_ge(s_ev, ev_n[0])
                    s.dma_start(out=y_d[t * 128:(t + 1) * 128, og * 512:(og + 1) * 512],
                                in_=outb[pb][:]).then_inc(st, 16)
                st_n[0] += 16
            if og + 2 < no:
                load_w(og + 2, buf)
        @blk.sync
        def _(s):
            s.wait_ge(st, st_n[0])
    return nc


def run_ffn2(xs, inputs):
    nc = _cached(("ffn2",), build_ffn2)
    Aw = np.asarray(inputs["L2_ffn_A_w"]).astype(np.float32)
    rw = np.asarray(inputs["L2_ffn_router_w"]).astype(np.float32)
    waux = np.concatenate([Aw.T, rw.T, np.zeros((C, 104), np.float32)], 1)
    cores = np.asarray(inputs["L2_ffn_cores"]).astype(np.float32)
    m = {
        "w": np.ascontiguousarray(np.asarray(inputs["L2_ffn_W_base"]).astype(np.float32).T),
        "b": np.asarray(inputs["L2_ffn_b_base"]).reshape(1, C).astype(np.float32),
        "waux": np.ascontiguousarray(waux),
        "coresF": np.ascontiguousarray(cores.reshape(128, 16)),
        "bwT": np.ascontiguousarray(np.asarray(inputs["L2_ffn_B_w"]).astype(np.float32).T),
        "onesr": np.ones((1, 128), np.float32),
        "idnb": np.eye(128, dtype=BF),
    }
    maps = [dict(m, x=xs[i]) for i in range(NCORE)]
    res = run_bass_kernel_spmd(nc, maps, core_ids=list(range(NCORE)))
    return [r["y"] for r in res.results]


def build_attn3():
    nc = bass.Bass("TRN2", target_bir_lowering=False)
    x_d = nc.dram_tensor("x", [TPC, C], BF16, kind="ExternalInput").ap()
    wq_h = nc.dram_tensor("wqselT", [C, 1024], F32, kind="ExternalInput")
    bq_d = nc.dram_tensor("bqsel", [1, 1024], F32, kind="ExternalInput").ap()
    mz_d = nc.dram_tensor("mzcat", [128, 8 * 129], F32, kind="ExternalInput").ap()
    wo_h = nc.dram_tensor("woT", [C, C], F32, kind="ExternalInput")
    one_d = nc.dram_tensor("onesr", [1, 128], F32, kind="ExternalInput").ap()
    idb_d = nc.dram_tensor("idnb", [128, 128], BF16, kind="ExternalInput").ap()
    y_d = nc.dram_tensor("y", [TPC, C], BF16, kind="ExternalOutput").ap()
    nt = TPC // 128
    nk = C // 128
    no = C // 512
    EPS3 = 1e-6
    with ExitStack() as es:
        def sb(name, shape, dt):
            return es.enter_context(nc.sbuf_tensor(name, shape, dt))
        xf = sb("xf", [128, nk * TPC], BF16)
        wqb = sb("wqb", [128, nk * 1024], BF16)
        wblk = [sb("wblk0", [128, nk * 512], BF16)] * 2
        mzb = sb("mzb", [128, 8 * 129 + 8], BF16)
        bq = sb("bq", [1, 1024], F32)
        ones = sb("ones", [1, 128], F32)
        idnb = sb("idnb_t", [128, 128], BF16)
        sqf = sb("sqf", [128, 1024], F32)
        sqm = sb("sqm", [128, 1024], F32)
        sqb = sb("sqb", [128, 1024], BF16)
        den = sb("den", [128, 1], F32)
        rfm = sb("rfm", [128, nt * 1024], BF16)
        rtm = sb("rtm", [128, 128], BF16)
        outb = [sb(f"outb{i}", [128, 512], BF16) for i in range(2)]
        ps_q = es.enter_context(nc.psum_tensor("psq", [128, 512], F32))
        ps_r = es.enter_context(nc.psum_tensor("psr", [128, 512], F32))
        ps_tr = es.enter_context(nc.psum_tensor("pstr", [128, 512], BF16))
        ps = [es.enter_context(nc.psum_tensor(f"ps{i}", [128, 512], F32)) for i in range(2)]
        ldx = es.enter_context(nc.semaphore("ldx"))
        ldw = es.enter_context(nc.semaphore("ldw"))
        s_q = es.enter_context(nc.semaphore("s_q"))
        s_mm = es.enter_context(nc.semaphore("s_mm"))
        s_ev = es.enter_context(nc.semaphore("s_ev"))
        st = es.enter_context(nc.semaphore("st"))
        blk = es.enter_context(nc.Block())

        @blk.scalar
        def _(sc):
            sc.preload_act_table(AF.Exp)

        @blk.gpsimd
        def _(g):
            g.dma_start(out=bq[:], in_=bq_d[:, :]).then_inc(ldx, 16)
            g.dma_start(out=ones[:], in_=one_d[:, :]).then_inc(ldx, 16)
            g.dma_start(out=idnb[:], in_=idb_d[:, :]).then_inc(ldx, 16)
            g.dma_start(out=mzb[:, :8 * 129], in_=mz_d[:, :]).then_inc(ldx, 16)
            src = bass.AP(wq_h, 0, [[1024, 128], [128 * 1024, nk], [1, 1024]])
            g.dma_start(out=wqb.rearrange("p (kc n) -> p kc n", n=1024), in_=src).then_inc(ldx, 16)

        @blk.sync
        def _(s):
            for kc in range(nk):
                s.dma_start_transpose(
                    out=xf[:, kc * TPC:(kc + 1) * TPC],
                    in_=x_d[:, kc * 128:(kc + 1) * 128]).then_inc(ldx, 16)
        ldx_all = 80 + nk * 16

        q_n = [0]
        for t in range(nt):
            for half in range(2):
                @blk.tensor
                def _(te):
                    te.wait_ge(ldx, ldx_all)
                    if t > 0 and half == 0:
                        te.wait_ge(s_q, q_n[0])
                    elif half == 1:
                        te.wait_ge(s_q, q_n[0] - 1)
                    for kc in range(nk):
                        te.matmul(ps_q[:], xf[:, kc * TPC + t * 128:kc * TPC + (t + 1) * 128],
                                  wqb[:, kc * 1024 + half * 512:kc * 1024 + (half + 1) * 512],
                                  start=(kc == 0), stop=False)
                    te.matmul(ps_q[:], ones[:, :], bq[:, half * 512:(half + 1) * 512],
                              start=False, stop=True).then_inc(s_q, 1)
                q_n[0] += 1

                @blk.scalar
                def _(sc):
                    sc.wait_ge(s_q, q_n[0])
                    sc.activation(sqf[:, half * 512:(half + 1) * 512], ps_q[:], AF.Relu)
                    sc.drain().then_inc(s_q, 1)
                q_n[0] += 1

                @blk.vector
                def _(v):
                    v.wait_ge(s_q, q_n[0])
                    v.tensor_scalar(sqm[:, half * 512:(half + 1) * 512], ps_q[:],
                                    0.0, None, ALU.min)
                    v.drain().then_inc(s_q, 1)
                q_n[0] += 1

                @blk.scalar
                def _(sc):
                    sc.wait_ge(s_q, q_n[0])
                    sc.activation(sqm[:, half * 512:(half + 1) * 512],
                                  sqm[:, half * 512:(half + 1) * 512], AF.Exp)
                    sc.drain().then_inc(s_q, 1)
                q_n[0] += 1

            @blk.vector
            def _(v):
                v.wait_ge(s_q, q_n[0])
                v.tensor_tensor(sqb[:], sqf[:], sqm[:], ALU.add)
                v.drain().then_inc(s_q, 1)
            q_n[0] += 1

            for j in range(8):
                @blk.tensor
                def _(te):
                    te.wait_ge(s_q, q_n[0])
                    te.transpose(ps_tr[:, 0:128], sqb[:, j * 128:(j + 1) * 128],
                                 idnb[:]).then_inc(s_q, 1)
                q_n[0] += 1

                @blk.vector
                def _(v):
                    v.wait_ge(s_q, q_n[0])
                    v.tensor_copy(rtm[:], ps_tr[:, 0:128])
                    v.drain().then_inc(s_q, 1)
                q_n[0] += 1

                @blk.tensor
                def _(te):
                    te.wait_ge(s_q, q_n[0])
                    te.matmul(ps_r[:, :129], rtm[:], mzb[:, j * 129:(j + 1) * 129],
                              start=True, stop=True).then_inc(s_q, 1)
                q_n[0] += 1

                @blk.vector
                def _(v):
                    v.wait_ge(s_q, q_n[0])
                    v.tensor_scalar(den[:], ps_r[:, 128:129], EPS3, None, ALU.add)
                    v.drain()
                    v.reciprocal(den[:], den[:])
                    v.drain()
                    v.tensor_scalar(rtm[:], ps_r[:, 0:128], den[:], None, ALU.mult)
                    v.drain().then_inc(s_q, 1)
                q_n[0] += 1

                @blk.tensor
                def _(te):
                    te.wait_ge(s_q, q_n[0])
                    te.transpose(ps_tr[:, 0:128], rtm[:], idnb[:]).then_inc(s_q, 1)
                q_n[0] += 1

                @blk.vector
                def _(v):
                    v.wait_ge(s_q, q_n[0])
                    v.tensor_copy(rfm[:, t * 1024 + j * 128:t * 1024 + (j + 1) * 128],
                                  ps_tr[:, 0:128])
                    v.drain().then_inc(s_q, 1)
                q_n[0] += 1

        mm_n = [0]
        ev_n = [0]
        st_n = [0]

        def load_w(og, buf):
            @blk.gpsimd
            def _(g):
                if og >= 1:
                    g.wait_ge(s_mm, og * nt)
                src = bass.AP(wo_h, og * 512, [[C, 128], [128 * C, nk], [1, 512]])
                g.dma_start(out=wblk[buf].rearrange("p (kc n) -> p kc n", n=512),
                            in_=src).then_inc(ldw, 16)

        load_w(0, 0)
        for og in range(no):
            buf = og % 2
            for t in range(nt):
                pb = mm_n[0] % 2

                @blk.tensor
                def _(te):
                    te.wait_ge(ldw, (og + 1) * 16)
                    if og == 0 and t == 0:
                        te.wait_ge(s_q, q_n[0])
                    if mm_n[0] >= 2:
                        te.wait_ge(s_ev, mm_n[0] - 1)
                    for kc in range(nk):
                        mm = te.matmul(ps[pb][:],
                                       rfm[:, t * 1024 + (kc // 4) * 128:t * 1024 + (kc // 4 + 1) * 128],
                                       wblk[buf][:, kc * 512:(kc + 1) * 512],
                                       start=(kc == 0), stop=(kc == nk - 1))
                        if kc == nk - 1:
                            mm.then_inc(s_mm, 1)
                mm_n[0] += 1

                @blk.vector
                def _(v):
                    v.wait_ge(s_mm, mm_n[0])
                    if ev_n[0] >= 2:
                        v.wait_ge(st, (ev_n[0] - 1) * 16)
                    v.tensor_copy(outb[pb][:], ps[pb][:])
                    v.drain().then_inc(s_ev, 1)
                ev_n[0] += 1

                @blk.sync
                def _(s):
                    s.wait_ge(s_ev, ev_n[0])
                    s.dma_start(out=y_d[t * 128:(t + 1) * 128, og * 512:(og + 1) * 512],
                                in_=outb[pb][:]).then_inc(st, 16)
                st_n[0] += 16
            if og + 1 < no:
                load_w(og + 1, buf)
        @blk.sync
        def _(s):
            s.wait_ge(st, st_n[0])
    return nc


def run_attn3(xs, inputs):
    nc = _cached(("attn3",), build_attn3)
    Wq = np.asarray(inputs["L3_attn_Wq"]).astype(np.float32)
    bq = np.asarray(inputs["L3_attn_bq"]).astype(np.float32)
    sel = np.concatenate([np.arange(h * 128, (h + 1) * 128) for h in range(0, 32, 4)])
    wqselT = np.ascontiguousarray(Wq[sel].T)
    bqsel = bq[sel].reshape(1, 1024)
    M = np.asarray(inputs["L3_attn_M"]).astype(np.float32)
    z = np.asarray(inputs["L3_attn_z"]).astype(np.float32)
    mz = np.concatenate([np.concatenate([M[j], z[j][:, None]], 1) for j in range(8)], 1)
    m = {
        "wqselT": wqselT, "bqsel": bqsel,
        "mzcat": np.ascontiguousarray(mz),
        "woT": np.ascontiguousarray(np.asarray(inputs["L3_attn_Wo"]).astype(np.float32).T),
        "onesr": np.ones((1, 128), np.float32),
        "idnb": np.eye(128, dtype=BF),
    }
    maps = [dict(m, x=xs[i]) for i in range(NCORE)]
    res = run_bass_kernel_spmd(nc, maps, core_ids=list(range(NCORE)))
    return [r["y"] for r in res.results]


def _prep_maps(inputs):
    def pad(qT, rows):
        qT = np.ascontiguousarray(qT)
        if qT.shape[0] < rows:
            qT = np.concatenate([qT, np.zeros((rows - qT.shape[0],) + qT.shape[1:], qT.dtype)], 0)
        return qT

    def tpad(q, rows):
        return pad(np.asarray(q).T, rows)
    gqT = tpad(inputs["L0_ffn_gate_qweight"], 11264)
    gsT = tpad(inputs["L0_ffn_gate_scales"], 11264)
    gzR = pad(np.repeat(np.asarray(inputs["L0_ffn_gate_zeros"]).T, 8, axis=0), 11264)
    uqT = tpad(inputs["L0_ffn_up_qweight"], 11264)
    usT = tpad(inputs["L0_ffn_up_scales"], 11264)
    uzR = pad(np.repeat(np.asarray(inputs["L0_ffn_up_zeros"]).T, 8, axis=0), 11264)
    dqT = np.ascontiguousarray(np.asarray(inputs["L0_ffn_down_qweight"]).T)
    dsT = np.ascontiguousarray(np.asarray(inputs["L0_ffn_down_scales"]).T)
    dzR = np.ascontiguousarray(np.repeat(np.asarray(inputs["L0_ffn_down_zeros"]).T, 8, axis=0))
    wpr = np.asarray(inputs["L3_ffn_weight_packed"]).reshape(4096, 2048)
    l3sz = np.zeros((4096, 128), np.float32)
    l3sz[:, 0] = np.asarray(inputs["L3_ffn_scale"]).reshape(-1)
    l3sz[:, 1] = np.asarray(inputs["L3_ffn_zero_point"]).reshape(-1)
    shf = np.zeros((128, 128), np.int32)
    shf[:, 0] = 4 * (np.arange(128) % 8)
    idn = np.eye(128, dtype=BF)
    maps = []
    for c in range(NCORE):
        g0, g1 = c * 1408, (c + 1) * 1408
        d0, d1 = c * 512, (c + 1) * 512
        maps.append({
            "gq": np.ascontiguousarray(gqT[g0:g1]), "gs": np.ascontiguousarray(gsT[g0:g1]),
            "gz": np.ascontiguousarray(gzR[g0:g1]),
            "uq": np.ascontiguousarray(uqT[g0:g1]), "us": np.ascontiguousarray(usT[g0:g1]),
            "uz": np.ascontiguousarray(uzR[g0:g1]),
            "dq": np.ascontiguousarray(dqT[d0:d1]), "ds": np.ascontiguousarray(dsT[d0:d1]),
            "dz": np.ascontiguousarray(dzR[d0:d1]),
            "wp": np.ascontiguousarray(wpr[d0:d1]), "l3sz": np.ascontiguousarray(l3sz[d0:d1]),
            "shf": shf, "idn": idn,
        })
    return maps


def run_prep(inputs):
    nc = _cached(("prep",), build_prep)
    maps = _prep_maps(inputs)
    res = run_bass_kernel_spmd(nc, maps, core_ids=list(range(NCORE)))

    def unscramble(a, K_):
        nt = a.shape[0] // 128
        return a.reshape(nt, 128, K_ // 128, 128).transpose(2, 1, 0, 3).reshape(K_, nt * 128)
    GWT = np.concatenate([unscramble(r["gw"], 4096) for r in res.results], 1)
    UWT = np.concatenate([unscramble(r["uw"], 4096) for r in res.results], 1)
    DWT = np.concatenate([unscramble(r["dw"], 11008) for r in res.results], 1)
    L3WT = np.concatenate([unscramble(r["l3w"], 4096) for r in res.results], 1)
    return GWT, UWT, DWT, L3WT


def run_proj(xs, w, bias=None, w_mode="f32T", out_dt=BF16, K=C, O=C):
    nc = _cached(("proj", K, O, w_mode, bias is not None, str(out_dt)),
                 lambda: build_proj(K, O, w_mode, bias is not None, out_dt))
    maps = []
    for i in range(NCORE):
        m = {"x": xs[i], "w": w}
        if bias is not None:
            m["b"] = bias.reshape(1, O).astype(np.float32)
            m["onesr"] = np.ones((1, 128), np.float32)
        maps.append(m)
    res = run_bass_kernel_spmd(nc, maps, core_ids=list(range(NCORE)))
    return [r["y"] for r in res.results]


def run_ln(xs, as_=None, fs=None, emit_xn=True):
    wb = as_ is not None
    f_dtype = BF16
    if wb and fs[0].dtype == np.float16:
        f_dtype = F16
    nc = _cached(("ln", wb, emit_xn, str(None if not wb else fs[0].dtype)),
                 lambda: build_ln(wb, emit_xn, f_dtype))
    cst = np.full((128, 128), LNEPS, np.float32)
    maps = []
    for i in range(NCORE):
        m = {"x": xs[i], "cst": cst}
        if wb:
            m["a"] = as_[i]
            m["f"] = fs[i]
        maps.append(m)
    res = run_bass_kernel_spmd(nc, maps, core_ids=list(range(NCORE)))
    xo = [r.get("xo") for r in res.results] if wb else xs
    xn = [r.get("xn") for r in res.results] if emit_xn else None
    return xo, xn


def _shard_rows(full):
    return [np.ascontiguousarray(full[c // 2, (c % 2) * TPC:(c % 2 + 1) * TPC])
            for c in range(NCORE)]


def _join_rows(parts, dtype):
    out = np.empty((B, T) + parts[0].shape[1:], dtype)
    for c in range(NCORE):
        out[c // 2, (c % 2) * TPC:(c % 2 + 1) * TPC] = parts[c]
    return out


def kernel(**inputs):
    inputs = {k_: np.asarray(v) for k_, v in inputs.items()}
    x0 = inputs["L0_attn_x"].astype(np.float32)
    xs = _shard_rows(x0)

    GWT, UWT, DWT, L3WT = run_prep(inputs)
    GWT = np.ascontiguousarray(GWT)
    UWT = np.ascontiguousarray(UWT)
    DWT = np.ascontiguousarray(DWT)
    L3WT = np.ascontiguousarray(L3WT)

    _, xn = run_ln(xs, emit_xn=True)
    xn_full = _join_rows(xn, BF)
    O0 = run_attn_qk(0, xn_full, inputs)
    a0 = run_proj(_shard_rows(O0),
                  np.ascontiguousarray(inputs["L0_attn_wo"].astype(np.float32).T))
    f0 = run_ffn0(xn, GWT, UWT, DWT)

    xs1, xn1 = run_ln(xs, a0, f0)
    xn1_full = _join_rows(xn1, BF)
    O1 = run_attn1(xn1_full, inputs["L1_attn_bv_q"].astype(np.float32),
                   inputs["L1_attn_bv_k"].astype(np.float32),
                   inputs["L1_attn_bv_v"].astype(np.float32))
    a1 = _shard_rows(O1)
    f1 = run_ffn1(xn1, inputs)

    xs2, xn2 = run_ln(xs1, a1, f1)
    xn2_full = _join_rows(xn2, BF)
    O2 = run_attn_qk(2, xn2_full, inputs)
    a2 = run_proj(_shard_rows(O2),
                  np.ascontiguousarray(inputs["L2_attn_Wo"].astype(np.float32).T))
    f2 = run_ffn2(xn2, inputs)

    xs3, xn3 = run_ln(xs2, a2, f2)
    a3 = run_attn3(xn3, inputs)
    f3 = run_proj(xn3, L3WT, bias=inputs["L3_ffn_bias"].astype(np.float32),
                  w_mode="bf16T")

    xs4, _ = run_ln(xs3, a3, f3, emit_xn=False)
    return _join_rows(xs4, np.float32).reshape(B, T, C)


# version 03 (final)
# speedup vs baseline: 1.2380x; 1.0256x over previous
import math
from contextlib import ExitStack

import numpy as np
import ml_dtypes

import concourse.bass as bass
import concourse.mybir as mybir
from concourse.bass_utils import run_bass_kernel_spmd

F32 = mybir.dt.float32
BF16 = mybir.dt.bfloat16
F16 = mybir.dt.float16
I32 = mybir.dt.int32
AF = mybir.ActivationFunctionType
ALU = mybir.AluOpType
BF = ml_dtypes.bfloat16

B, T, C = 4, 2048, 4096
NCORE = 8
TPC = 1024
LNEPS = 1e-5

_NC_CACHE = {}


def _cached(key, fn):
    if key not in _NC_CACHE:
        _NC_CACHE[key] = fn()
    return _NC_CACHE[key]


def build_ln(with_branches, emit_xn, f_dtype=BF16):
    nc = bass.Bass("TRN2", target_bir_lowering=False)
    x_d = nc.dram_tensor("x", [TPC, C], F32, kind="ExternalInput").ap()
    if with_branches:
        a_d = nc.dram_tensor("a", [TPC, C], BF16, kind="ExternalInput").ap()
        f_d = nc.dram_tensor("f", [TPC, C], f_dtype, kind="ExternalInput").ap()
        xo_d = nc.dram_tensor("xo", [TPC, C], F32, kind="ExternalOutput").ap()
    if emit_xn:
        xn_d = nc.dram_tensor("xn", [TPC, C], BF16, kind="ExternalOutput").ap()
    nt = TPC // 128
    with ExitStack() as es:
        def sb(name, shape, dt):
            return es.enter_context(nc.sbuf_tensor(name, shape, dt))
        xt = [sb(f"xt{i}", [128, C], F32) for i in range(2)]
        at = [sb(f"at{i}", [128, C], F32) for i in range(2)] if with_branches else None
        ft = [sb(f"ft{i}", [128, C], F32) for i in range(2)] if with_branches else None
        stats2 = [sb(f"stats{i}", [128, 48], F32) for i in range(2)]
        mv2 = [sb(f"mv{i}", [128, 2], F32) for i in range(2)]
        rstd2 = [sb(f"rstd{i}", [128, 1], F32) for i in range(2)]
        xnt2 = [sb(f"xnt2_{i}", [128, C], BF16) for i in range(2)] if emit_xn else None
        ld = es.enter_context(nc.semaphore("ld"))
        s_add = es.enter_context(nc.semaphore("s_add"))
        s_agg = es.enter_context(nc.semaphore("s_agg"))
        s_sq = es.enter_context(nc.semaphore("s_sq"))
        s_ts = es.enter_context(nc.semaphore("s_ts"))
        st = es.enter_context(nc.semaphore("st"))
        ini = es.enter_context(nc.semaphore("ini"))
        cstt = sb("cstt", [128, 128], F32)
        blk = es.enter_context(nc.Block())
        cst_d = nc.dram_tensor("cst", [128, 128], F32, kind="ExternalInput").ap()

        @blk.gpsimd
        def _(g):
            g.dma_start(out=cstt[:], in_=cst_d[:, :]).then_inc(ini, 16)
            g.wait_ge(ini, 16)

        @blk.scalar
        def _(sc):
            sc.wait_ge(ini, 16)

        n_in = 3 if with_branches else 1
        st_per = (16 if with_branches else 0) + (16 if emit_xn else 0)
        ld_n = [0]
        st_n = [0]
        done_mark = [0] * (nt + 2)

        def load_tile(i):
            bi = i % 2
            @blk.gpsimd
            def _(g):
                if i >= 2:
                    g.wait_ge(s_ts if emit_xn else s_add, i - 1)
                    g.wait_ge(st, (i - 1) * st_per)
                g.dma_start(out=xt[bi][:], in_=x_d[i * 128:(i + 1) * 128, :]).then_inc(ld, 16)
                if with_branches:
                    g.dma_start(out=at[bi][:], in_=a_d[i * 128:(i + 1) * 128, :]).then_inc(ld, 16)
                    g.dma_start(out=ft[bi][:], in_=f_d[i * 128:(i + 1) * 128, :]).then_inc(ld, 16)
                ld_n[0] += 16 * n_in

        load_tile(0)
        if nt > 1:
            load_tile(1)
        for i in range(nt):
            bi = i % 2
            need_ld = (i + 1) * 16 * n_in

            @blk.vector
            def _(v):
                v.wait_ge(ld, need_ld)
                if emit_xn and i >= 2:
                    v.wait_ge(st, (i - 1) * st_per)
                if with_branches:
                    v.tensor_tensor(xt[bi][:], xt[bi][:], at[bi][:], ALU.add)
                    v.tensor_tensor(xt[bi][:], xt[bi][:], ft[bi][:], ALU.add)
                    v.drain().then_inc(s_add, 1)
                if emit_xn:
                    for j in range(8):
                        v.bn_stats(stats2[bi][:, j * 6:(j + 1) * 6], xt[bi][:, j * 512:(j + 1) * 512])
                    v.drain()
                    v.bn_aggr(mv2[bi][:], stats2[bi][:])
                    v.drain().then_inc(s_agg, 1)
            if with_branches:
                @blk.sync
                def _(s):
                    s.wait_ge(s_add, i + 1)
                    s.dma_start(out=xo_d[i * 128:(i + 1) * 128, :], in_=xt[bi][:]).then_inc(st, 16)
                    st_n[0] += 16
            if emit_xn:
                @blk.scalar
                def _(sc):
                    sc.wait_ge(s_agg, i + 1)
                    sc.activation(rstd2[bi][:], mv2[bi][:, 1:2], AF.Sqrt, bias=cstt[:, 0:1], scale=1.0)
                    sc.drain().then_inc(s_sq, 1)

                @blk.vector
                def _(v):
                    v.wait_ge(s_sq, i + 1)
                    v.reciprocal(rstd2[bi][:], rstd2[bi][:])
                    v.tensor_scalar(xnt2[bi][:], xt[bi][:], mv2[bi][:, 0:1], rstd2[bi][:], ALU.subtract, ALU.mult)
                    v.drain().then_inc(s_ts, 1)

                @blk.sync
                def _(s):
                    s.wait_ge(s_ts, i + 1)
                    s.dma_start(out=xn_d[i * 128:(i + 1) * 128, :], in_=xnt2[bi][:]).then_inc(st, 16)
                    st_n[0] += 16
            if i + 2 < nt:
                load_tile(i + 2)
        @blk.sync
        def _(s):
            s.wait_ge(st, st_n[0])
    return nc


def build_proj(K, O, w_mode, with_bias, out_dt=BF16):
    nc = bass.Bass("TRN2", target_bir_lowering=False)
    x_d = nc.dram_tensor("x", [TPC, K], BF16, kind="ExternalInput").ap()
    if w_mode == "f32T":
        w_h = nc.dram_tensor("w", [K, O], F32, kind="ExternalInput")
    else:
        w_h = nc.dram_tensor("w", [K, O], BF16, kind="ExternalInput")
    w_d = w_h.ap()
    if with_bias:
        b_d = nc.dram_tensor("b", [1, O], F32, kind="ExternalInput").ap()
        o_d = nc.dram_tensor("onesr", [1, 128], F32, kind="ExternalInput").ap()
    y_d = nc.dram_tensor("y", [TPC, O], out_dt, kind="ExternalOutput").ap()
    nt = TPC // 128
    nk = K // 128
    no = O // 512
    with ExitStack() as es:
        def sb(name, shape, dt):
            return es.enter_context(nc.sbuf_tensor(name, shape, dt))
        xf = sb("xf", [128, nk * TPC], BF16)
        wblk = [sb(f"wblk{i}", [128, nk * 512], BF16) for i in range(2)]
        ones = sb("ones", [1, 128], F32)
        bias = sb("bias", [1, O], F32) if with_bias else None
        outb = [sb(f"outb{i}", [128, 512], out_dt) for i in range(2)]
        ps = [es.enter_context(nc.psum_tensor(f"ps{i}", [128, 512], F32)) for i in range(2)]
        ldx = es.enter_context(nc.semaphore("ldx"))
        ldw = es.enter_context(nc.semaphore("ldw"))
        s_mm = es.enter_context(nc.semaphore("s_mm"))
        s_ev = es.enter_context(nc.semaphore("s_ev"))
        st = es.enter_context(nc.semaphore("st"))
        blk = es.enter_context(nc.Block())

        @blk.sync
        def _(s):
            for kc in range(nk):
                s.dma_start_transpose(
                    out=xf[:, kc * TPC:(kc + 1) * TPC],
                    in_=x_d[:, kc * 128:(kc + 1) * 128]).then_inc(ldx, 16)

        if with_bias:
            @blk.gpsimd
            def _(g):
                g.dma_start(out=bias[:, :], in_=b_d[:, :]).then_inc(ldx, 16)
                g.dma_start(out=ones[:, :], in_=o_d[:, :]).then_inc(ldx, 16)

        def load_w(o, buf):
            @blk.gpsimd
            def _(g):
                if o >= 2:
                    g.wait_ge(s_mm, (o - 1) * nt)
                src = bass.AP(w_h, o * 512,
                              [[O, 128], [128 * O, nk], [1, 512]])
                dst = wblk[buf].rearrange("p (kc o) -> p kc o", kc=nk)
                g.dma_start(out=dst, in_=src).then_inc(ldw, 16)

        w_dmas = 1
        load_w(0, 0)
        if no > 1:
            load_w(1, 1)
        mm_n = 0
        ev_n = 0
        st_n = 0
        for o in range(no):
            buf = o % 2
            for t in range(nt):
                pb = (o * nt + t) % 2

                @blk.tensor
                def _(te):
                    if o == 0 and t == 0:
                        te.wait_ge(ldx, (nk + (2 if with_bias else 0)) * 16)
                    te.wait_ge(ldw, (o + 1) * w_dmas * 16)
                    if mm_n >= 2:
                        te.wait_ge(s_ev, mm_n - 1)
                    for kc in range(nk):
                        last = (kc == nk - 1 and not with_bias)
                        mm = te.matmul(ps[pb][:],
                                       xf[:, kc * TPC + t * 128: kc * TPC + (t + 1) * 128],
                                       wblk[buf][:, kc * 512:(kc + 1) * 512],
                                       start=(kc == 0), stop=last)
                        if last:
                            mm.then_inc(s_mm, 1)
                    if with_bias:
                        te.matmul(ps[pb][:], ones[:, :],
                                  bias[:, o * 512:(o + 1) * 512],
                                  start=False, stop=True).then_inc(s_mm, 1)
                mm_n += 1

                @blk.vector
                def _(v):
                    v.wait_ge(s_mm, mm_n)
                    if ev_n >= 2:
                        v.wait_ge(st, (ev_n - 1) * 16)
                    v.tensor_copy(outb[pb][:], ps[pb][:])
                    v.drain().then_inc(s_ev, 1)
                ev_n += 1

                @blk.sync
                def _(s):
                    s.wait_ge(s_ev, ev_n)
                    s.dma_start(out=y_d[t * 128:(t + 1) * 128, o * 512:(o + 1) * 512],
                                in_=outb[pb][:]).then_inc(st, 16)
                st_n += 16
            if o + 2 < no:
                load_w(o + 2, buf)
        @blk.sync
        def _(s):
            s.wait_ge(st, st_n)
    return nc


def build_prep():
    nc = bass.Bass("TRN2", target_bir_lowering=False)
    gq = nc.dram_tensor("gq", [1408, 512], I32, kind="ExternalInput").ap()
    gs = nc.dram_tensor("gs", [1408, 32], F16, kind="ExternalInput").ap()
    gz = nc.dram_tensor("gz", [1408, 32], I32, kind="ExternalInput").ap()
    uq = nc.dram_tensor("uq", [1408, 512], I32, kind="ExternalInput").ap()
    us = nc.dram_tensor("us", [1408, 32], F16, kind="ExternalInput").ap()
    uz = nc.dram_tensor("uz", [1408, 32], I32, kind="ExternalInput").ap()
    dq = nc.dram_tensor("dq", [512, 1376], I32, kind="ExternalInput").ap()
    ds = nc.dram_tensor("ds", [512, 86], F16, kind="ExternalInput").ap()
    dz = nc.dram_tensor("dz", [512, 86], I32, kind="ExternalInput").ap()
    wp = nc.dram_tensor("wp", [512, 2048], I32, kind="ExternalInput").ap()
    l3sz = nc.dram_tensor("l3sz", [512, 128], F32, kind="ExternalInput").ap()
    shf = nc.dram_tensor("shf", [128, 128], I32, kind="ExternalInput").ap()
    idn = nc.dram_tensor("idn", [128, 128], BF16, kind="ExternalInput").ap()
    gw = nc.dram_tensor("gw", [1408, 4096], BF16, kind="ExternalOutput").ap()
    uw = nc.dram_tensor("uw", [1408, 4096], BF16, kind="ExternalOutput").ap()
    dw = nc.dram_tensor("dw", [512, 11008], BF16, kind="ExternalOutput").ap()
    l3w = nc.dram_tensor("l3w", [512, 4096], BF16, kind="ExternalOutput").ap()

    with ExitStack() as es:
        def sb(name, shape, dt):
            return es.enter_context(nc.sbuf_tensor(name, shape, dt))
        qt = [sb(f"qt{i}", [128, 2048], I32) for i in range(2)]
        nib = sb("nib", [128, 11008], I32)
        outw = [sb(f"outw{i}", [128, 11008], BF16) for i in range(2)]
        tbuf = sb("tbuf", [128, 11008], BF16)
        svec = [sb(f"svec{i}", [128, 96], F16) for i in range(2)]
        zvec = [sb(f"zvec{i}", [128, 96], I32) for i in range(2)]
        sf32 = sb("sf32", [128, 96], F32)
        zpf = sb("zpf", [128, 96], F32)
        shft = sb("shft", [128, 128], I32)
        idnt = sb("idnt", [128, 128], BF16)
        l3sc = [sb(f"l3sc{i}", [128, 128], F32) for i in range(2)]
        pst = [es.enter_context(nc.psum_tensor(f"pst{i}", [128, 512], BF16)) for i in range(2)]
        ld = es.enter_context(nc.semaphore("ld"))
        s_dq = es.enter_context(nc.semaphore("s_dq"))
        s_tp = es.enter_context(nc.semaphore("s_tp"))
        s_tb = es.enter_context(nc.semaphore("s_tb"))
        st = es.enter_context(nc.semaphore("st"))
        blk = es.enter_context(nc.Block())

        @blk.gpsimd
        def _(g):
            g.dma_start(out=shft[:], in_=shf[:, :]).then_inc(ld, 16)
            g.dma_start(out=idnt[:], in_=idn[:, :]).then_inc(ld, 16)

        ld_n = [32]
        dq_n = [0]
        tp_n = [0]
        tb_n = [0]
        st_n = [0]

        def emit_tile(w_out, ti, K_, body_loads, body_dequant):
            bi = dq_n[0] % 2
            body_loads(bi)

            @blk.vector
            def _(v):
                v.wait_ge(ld, ld_n[0])
                if dq_n[0] >= 2:
                    v.wait_ge(s_tp, tp_marks[dq_n[0] - 2])
                body_dequant(v, bi)
                v.drain().then_inc(s_dq, 1)
            dq_n[0] += 1

            nchunk = K_ // 128
            groups = [(gi, min(4, nchunk - gi * 4)) for gi in range((nchunk + 3) // 4)]
            ng = len(groups)

            @blk.tensor
            def _(te):
                te.wait_ge(s_dq, dq_n[0])
                if dq_n[0] == 1:
                    te.wait_ge(ld, 32)
                for gi, gl in groups:
                    pb = (tp_n[0] + gi) % 2
                    te.wait_ge(s_tb, max(0, tp_n[0] + gi - 1))
                    for q4 in range(gl):
                        kc = gi * 4 + q4
                        mm = te.transpose(pst[pb][:, q4 * 128:(q4 + 1) * 128],
                                          outw[bi][:, kc * 128:(kc + 1) * 128], idnt[:])
                        if q4 == gl - 1:
                            mm.then_inc(s_tp, 1)

            @blk.vector
            def _(v):
                for gi, gl in groups:
                    pb = (tp_n[0] + gi) % 2
                    v.wait_ge(s_tp, tp_n[0] + gi + 1)
                    if dq_n[0] >= 2 and gi == 0:
                        v.wait_ge(st, st_marks[dq_n[0] - 2])
                    v.tensor_copy(tbuf[:, gi * 512:gi * 512 + gl * 128], pst[pb][:, :gl * 128])
                    v.drain().then_inc(s_tb, 1)
            tp_n[0] += ng
            tb_n[0] += ng
            tp_marks.append(tp_n[0])

            @blk.sync
            def _(s):
                s.wait_ge(s_tb, tb_n[0])
                s.dma_start(out=w_out[ti * 128:(ti + 1) * 128, :K_],
                            in_=tbuf[:, :K_]).then_inc(st, 16)
            st_n[0] += 16
            st_marks.append(st_n[0])

        tp_marks = []
        st_marks = []

        def gptq(q_d, s_d, z_d, w_out, ti, nwords, ngrp, K_):
            def loads(bi):
                @blk.gpsimd
                def _(g):
                    if dq_n[0] >= 2:
                        g.wait_ge(s_dq, dq_n[0] - 1)
                    g.dma_start(out=qt[bi][:, :nwords], in_=q_d[ti * 128:(ti + 1) * 128, :]).then_inc(ld, 16)
                    g.dma_start(out=svec[bi][:, :ngrp], in_=s_d[ti * 128:(ti + 1) * 128, :]).then_inc(ld, 16)
                    g.dma_start(out=zvec[bi][:, :ngrp], in_=z_d[ti * 128:(ti + 1) * 128, :]).then_inc(ld, 16)
                ld_n[0] += 48

            def dequant(v, bi):
                for j in range(8):
                    outap = nib[:, j:K_:8]
                    if j == 0:
                        v.tensor_scalar(outap, qt[bi][:, :nwords], 15, None,
                                        ALU.bitwise_and, ALU.bypass)
                    else:
                        v.tensor_scalar(outap, qt[bi][:, :nwords], 4 * j, 15,
                                        ALU.logical_shift_right, ALU.bitwise_and)
                v.tensor_scalar(zvec[bi][:, :ngrp], zvec[bi][:, :ngrp], shft[:, 0:1], 15,
                                ALU.logical_shift_right, ALU.bitwise_and)
                v.tensor_copy(sf32[:, :ngrp], svec[bi][:, :ngrp])
                v.drain()
                v.scalar_tensor_tensor(zpf[:, :ngrp], zvec[bi][:, :ngrp], 1.0,
                                       sf32[:, :ngrp], ALU.add, ALU.mult)
                v.drain()
                for g_ in range(ngrp):
                    v.tensor_scalar(outw[bi][:, g_ * 128:(g_ + 1) * 128],
                                    nib[:, g_ * 128:(g_ + 1) * 128],
                                    sf32[:, g_:g_ + 1], zpf[:, g_:g_ + 1],
                                    ALU.mult, ALU.subtract)
            emit_tile(w_out, ti, K_, loads, dequant)

        for ti in range(11):
            gptq(gq, gs, gz, gw, ti, 512, 32, 4096)
        for ti in range(11):
            gptq(uq, us, uz, uw, ti, 512, 32, 4096)
        for ti in range(4):
            gptq(dq, ds, dz, dw, ti, 1376, 86, 11008)

        for ti in range(4):
            def loads(bi, ti=ti):
                @blk.gpsimd
                def _(g):
                    g.wait_ge(s_dq, dq_n[0] - 1)
                    g.dma_start(out=qt[bi][:, :2048], in_=wp[ti * 128:(ti + 1) * 128, :]).then_inc(ld, 16)
                    g.dma_start(out=l3sc[bi][:], in_=l3sz[ti * 128:(ti + 1) * 128, :]).then_inc(ld, 16)
                ld_n[0] += 32

            def dequant(v, bi, ti=ti):
                v.tensor_scalar(nib[:, 0:4096:2], qt[bi][:, :2048], 15, None,
                                ALU.bitwise_and, ALU.bypass)
                v.tensor_scalar(nib[:, 1:4096:2], qt[bi][:, :2048], 4, 15,
                                ALU.logical_shift_right, ALU.bitwise_and)
                v.drain()
                v.tensor_scalar(outw[bi][:, :4096], nib[:, :4096],
                                l3sc[bi][:, 1:2], l3sc[bi][:, 0:1],
                                ALU.subtract, ALU.mult)
            emit_tile(l3w, ti, 4096, loads, dequant)

        @blk.sync
        def _(s):
            s.wait_ge(st, st_n[0])
    return nc


def build_ffn0():
    NI, NIP = 11008, 11264
    nc = bass.Bass("TRN2", target_bir_lowering=False)
    x_d = nc.dram_tensor("x", [TPC, C], BF16, kind="ExternalInput").ap()
    gw_h = nc.dram_tensor("gwt", [C, NIP], BF16, kind="ExternalInput")
    uw_h = nc.dram_tensor("uwt", [C, NIP], BF16, kind="ExternalInput")
    dw_h = nc.dram_tensor("dwt", [NI, C], BF16, kind="ExternalInput")
    y_d = nc.dram_tensor("y", [TPC, C], F16, kind="ExternalOutput").ap()
    TP = 512
    npan = TPC // TP
    nk = C // 128
    nnc = NI // 128
    OG = 256
    with ExitStack() as es:
        def sb(name, shape, dt):
            return es.enter_context(nc.sbuf_tensor(name, shape, dt))
        inter = sb("inter", [128, nnc * TP], BF16)
        arena = sb("arena", [128, 3 * nk * 512], BF16)
        xfp = arena[:, 0:nk * TP]
        wg = [arena[:, nk * 512 + i * nk * 256:nk * 512 + (i + 1) * nk * 256] for i in range(2)]
        wu = [arena[:, 2 * nk * 512 + i * nk * 256:2 * nk * 512 + (i + 1) * nk * 256] for i in range(2)]
        dwb = [arena[:, i * nnc * OG:(i + 1) * nnc * OG] for i in range(2)]
        gsil = sb("gsil", [128, 2 * TP], F32)
        outb = [sb(f"outb{i}", [128, OG], F16) for i in range(2)]
        ps_g = [es.enter_context(nc.psum_tensor(f"psg{i}", [128, TP], F32)) for i in range(2)]
        ps_d = [es.enter_context(nc.psum_tensor(f"psd{i}", [128, 512], F32)) for i in range(4)]
        ldx = es.enter_context(nc.semaphore("ldx"))
        ldw = es.enter_context(nc.semaphore("ldw"))
        ldd = es.enter_context(nc.semaphore("ldd"))
        s_gmm = es.enter_context(nc.semaphore("s_gmm"))
        s_sil = es.enter_context(nc.semaphore("s_sil"))
        s_int = es.enter_context(nc.semaphore("s_int"))
        s_dmm = es.enter_context(nc.semaphore("s_dmm"))
        s_ev = es.enter_context(nc.semaphore("s_ev"))
        st = es.enter_context(nc.semaphore("st"))
        blk = es.enter_context(nc.Block())

        @blk.scalar
        def _(sc):
            sc.preload_act_table(AF.Silu)

        ldx_n = [0]
        ldw_n = [0]
        ldd_n = [0]
        gmm_n = [0]
        sil_n = [0]
        int_n = [0]
        dmm_n = [0]
        ev_n = [0]
        st_n = [0]

        for pan in range(npan):
            t0 = pan * TP

            @blk.sync
            def _(s):
                if pan > 0:
                    s.wait_ge(s_int, pan * nnc)
                    s.wait_ge(s_dmm, pan * (C // OG) * (TP // 128))
                    s.wait_ge(st, pan * (C // OG) * (TP // 128) * 16)
                for kc in range(nk):
                    s.dma_start_transpose(
                        out=xfp[:, kc * TP:(kc + 1) * TP],
                        in_=x_d[t0:t0 + TP, kc * 128:(kc + 1) * 128]).then_inc(ldx, 16)
            ldx_n[0] += nk * 16

            for grp in range(NIP // 256):
                n0 = grp * 256
                gb = grp % 2
                dnow = dmm_n[0]
                gprev = gmm_n[0]

                @blk.gpsimd
                def _(g):
                    if gprev > 4:
                        g.wait_ge(s_gmm, gprev - 4)
                    if grp <= 1 and dnow > 0:
                        g.wait_ge(s_dmm, dnow)
                    src_g = bass.AP(gw_h, n0, [[NIP, 128], [128 * NIP, nk], [1, 256]])
                    src_u = bass.AP(uw_h, n0, [[NIP, 128], [128 * NIP, nk], [1, 256]])
                    g.dma_start(out=wg[gb].rearrange("p (kc n) -> p kc n", n=256), in_=src_g).then_inc(ldw, 16)
                    g.dma_start(out=wu[gb].rearrange("p (kc n) -> p kc n", n=256), in_=src_u).then_inc(ldw, 16)
                ldw_n[0] += 32

                sub = [grp * 2 + j for j in range(2) if grp * 2 + j < nnc]
                pgs = {}
                for j, ncg in enumerate(sub):
                    pgs[j] = int_n[0] % 2

                    @blk.tensor
                    def _(te, j=j, ncg=ncg):
                        if gmm_n[0] == 0:
                            te.wait_ge(ldx, ldx_n[0])
                        elif ncg == 0:
                            te.wait_ge(ldx, ldx_n[0])
                        te.wait_ge(ldw, ldw_n[0])
                        if int_n[0] >= 2:
                            te.wait_ge(s_int, int_n[0] - 1)
                        for kc in range(nk):
                            mm = te.matmul(ps_g[pgs[j]][:],
                                           wg[gb][:, kc * 256 + j * 128: kc * 256 + (j + 1) * 128],
                                           xfp[:, kc * TP:(kc + 1) * TP],
                                           start=(kc == 0), stop=(kc == nk - 1))
                            if kc == nk - 1:
                                mm.then_inc(s_gmm, 1)
                    gmm_n[0] += 1
                    int_n[0] += 1

                    @blk.scalar
                    def _(sc, j=j):
                        sc.wait_ge(s_gmm, gmm_n[0])
                        sc.activation(gsil[:, j * TP:(j + 1) * TP], ps_g[pgs[j]][:], AF.Silu)
                        sc.drain().then_inc(s_sil, 1)
                    sil_n[0] += 1
                int_n[0] -= len(sub)

                for j, ncg in enumerate(sub):
                    @blk.tensor
                    def _(te, j=j, ncg=ncg):
                        te.wait_ge(s_sil, sil_n[0] - len(sub) + 1 + j)
                        for kc in range(nk):
                            mm = te.matmul(ps_g[pgs[j]][:],
                                           wu[gb][:, kc * 256 + j * 128: kc * 256 + (j + 1) * 128],
                                           xfp[:, kc * TP:(kc + 1) * TP],
                                           start=(kc == 0), stop=(kc == nk - 1))
                            if kc == nk - 1:
                                mm.then_inc(s_gmm, 1)
                    gmm_n[0] += 1

                    @blk.vector
                    def _(v, j=j, ncg=ncg):
                        v.wait_ge(s_gmm, gmm_n[0])
                        v.tensor_tensor(inter[:, ncg * TP:(ncg + 1) * TP],
                                        gsil[:, j * TP:(j + 1) * TP], ps_g[pgs[j]][:], ALU.mult)
                        v.drain().then_inc(s_int, 1)
                    int_n[0] += 1

            def load_dw(og, buf):
                dm_now = dmm_n[0]
                int_need = int_n[0]

                @blk.sync
                def _(s):
                    if dm_now > 0:
                        s.wait_ge(s_dmm, dm_now)
                    if og < 2:
                        s.wait_ge(s_int, int_need)
                    src = bass.AP(dw_h, og * OG, [[C, 128], [128 * C, nnc], [1, OG]])
                    s.dma_start(out=dwb[buf].rearrange("p (kc o) -> p kc o", o=OG),
                                in_=src).then_inc(ldd, 16)

            load_dw(0, 0)
            load_dw(1, 1)
            for og in range(C // OG):
                buf = og % 2
                ldd_n[0] += 16
                for tt in range(TP // 128):
                    pd = tt

                    @blk.tensor
                    def _(te):
                        te.wait_ge(ldd, (og + 1) * 16 + pan * (C // OG) * 16)
                        if tt == 0 and og == 0:
                            te.wait_ge(s_int, (pan + 1) * nnc)
                        if dmm_n[0] >= 4:
                            te.wait_ge(s_ev, dmm_n[0] - 3)
                        for kc in range(nnc):
                            mm = te.matmul(ps_d[pd][:, :OG],
                                           inter[:, kc * TP + tt * 128: kc * TP + (tt + 1) * 128],
                                           dwb[buf][:, kc * OG:(kc + 1) * OG],
                                           start=(kc == 0), stop=(kc == nnc - 1))
                            if kc == nnc - 1:
                                mm.then_inc(s_dmm, 1)
                    dmm_n[0] += 1

                    @blk.vector
                    def _(v):
                        v.wait_ge(s_dmm, dmm_n[0])
                        if ev_n[0] >= 2:
                            v.wait_ge(st, (ev_n[0] - 1) * 16)
                        v.tensor_copy(outb[ev_n[0] % 2][:], ps_d[pd][:, :OG])
                        v.drain().then_inc(s_ev, 1)
                    ev_n[0] += 1

                    @blk.sync
                    def _(s):
                        s.wait_ge(s_ev, ev_n[0])
                        s.dma_start(out=y_d[t0 + tt * 128: t0 + (tt + 1) * 128,
                                            og * OG:(og + 1) * OG],
                                    in_=outb[(ev_n[0] - 1) % 2][:]).then_inc(st, 16)
                    st_n[0] += 16
                if og + 2 < C // OG:
                    load_dw(og + 2, buf)
        @blk.sync
        def _(s):
            s.wait_ge(st, st_n[0])
    return nc


def run_ffn0(xs, GWT, UWT, DWT):
    nc = _cached(("ffn0",), build_ffn0)
    maps = [{"x": xs[i], "gwt": GWT, "uwt": UWT, "dwt": DWT} for i in range(NCORE)]
    res = run_bass_kernel_spmd(nc, maps, core_ids=list(range(NCORE)))
    return [r["y"] for r in res.results]


def build_attn1():
    nc = bass.Bass("TRN2", target_bir_lowering=False)
    xn_h = nc.dram_tensor("xnb", [T, 2048], BF16, kind="ExternalInput")
    xn_d = xn_h.ap()
    bv_d = nc.dram_tensor("bvs", [6, 1024], F32, kind="ExternalInput").ap()
    bvf_d = nc.dram_tensor("bvf", [1, 6144], F32, kind="ExternalInput").ap()
    idn_d = nc.dram_tensor("idn32", [128, 128], F32, kind="ExternalInput").ap()
    one_d = nc.dram_tensor("onesr", [1, 128], F32, kind="ExternalInput").ap()
    tri_d = nc.dram_tensor("triT", [128, 128], BF16, kind="ExternalInput").ap()
    y_d = nc.dram_tensor("y", [T, 2048], BF16, kind="ExternalOutput").ap()
    HD = 1024
    ndc = HD // 128
    NG = T // 512
    NKB = T // 128
    SCL = 4.0 / 32.0
    with ExitStack() as es:
        def sb(name, shape, dt):
            return es.enter_context(nc.sbuf_tensor(name, shape, dt))
        bvt = sb("bvt", [6, 1024], F32)
        sgn = sb("sgn", [6, 1024], F32)
        u2 = sb("u2", [6, 1024], F32)
        alpha = sb("alpha", [6, 1], F32)
        bcol = sb("bcol", [128, 8 * 6], F32)
        bvr = sb("bvr", [1, 6144], F32)
        sgr = sb("sgr", [1, 6144], F32)
        u2r = sb("u2r", [1, 6144], F32)
        alr = sb("alr", [1, 6], F32)
        onesc = sb("onesc", [1, 128], F32)
        idnt = sb("idnt", [128, 128], F32)
        trit = sb("trit", [128, 128], BF16)
        vbb = sb("vbb", [128, HD], F32)
        qf = sb("qf", [128, ndc * T], BF16)
        kf = sb("kf", [128, ndc * T], BF16)
        xv = sb("xv", [128, NKB * HD], BF16)
        pt = sb("pt", [128, NKB * 512], BF16)
        outb = [sb(f"outb{i}", [128, 512], BF16) for i in range(2)]
        ps_s = [es.enter_context(nc.psum_tensor(f"pss{i}", [128, 512], F32)) for i in range(2)]
        ps_o = [es.enter_context(nc.psum_tensor(f"pso{i}", [128, 512], F32)) for i in range(4)]
        ps_t = es.enter_context(nc.psum_tensor("pstr", [128, 512], F32))
        ld = es.enter_context(nc.semaphore("ld"))
        s_b = es.enter_context(nc.semaphore("s_b"))
        s_qk = es.enter_context(nc.semaphore("s_qk"))
        s_vb = es.enter_context(nc.semaphore("s_vb"))
        s_sm = es.enter_context(nc.semaphore("s_sm"))
        s_sg = es.enter_context(nc.semaphore("s_sg"))
        s_p = es.enter_context(nc.semaphore("s_p"))
        s_pv = es.enter_context(nc.semaphore("s_pv"))
        s_ev = es.enter_context(nc.semaphore("s_ev"))
        st = es.enter_context(nc.semaphore("st"))
        blk = es.enter_context(nc.Block())

        @blk.scalar
        def _(sc):
            sc.preload_act_table(AF.Sigmoid)

        @blk.gpsimd
        def _(g):
            g.dma_start(out=bvt[:], in_=bv_d[:, :]).then_inc(ld, 16)
            g.dma_start(out=bvr[:], in_=bvf_d[:, :]).then_inc(ld, 16)
            g.dma_start(out=idnt[:], in_=idn_d[:, :]).then_inc(ld, 16)
            g.dma_start(out=trit[:], in_=tri_d[:, :]).then_inc(ld, 16)
            g.dma_start(out=onesc[:], in_=one_d[:, :]).then_inc(ld, 16)

        @blk.vector
        def _(v):
            v.wait_ge(ld, 80)
            v.tensor_reduce(alpha[:], bvt[:], mybir.AxisListType.X, ALU.add,
                            apply_absolute_value=True)
            v.tensor_reduce(alr[:], bvr[:].rearrange("p (r h) -> p r h", h=1024),
                            mybir.AxisListType.X, ALU.add, apply_absolute_value=True)
            v.drain().then_inc(s_b, 1)

        @blk.scalar
        def _(sc):
            sc.wait_ge(s_b, 1)
            sc.activation(sgn[:], bvt[:], AF.Sign)
            sc.activation(sgr[:], bvr[:], AF.Sign)
            sc.drain().then_inc(s_b, 1)

        @blk.vector
        def _(v):
            v.wait_ge(s_b, 2)
            for (S_, U_, A_, B_, nseg) in ((sgn, u2, alpha, bvt, None), (sgr, u2r, alr, bvr, 6)):
                v.tensor_tensor(U_[:], S_[:], S_[:], ALU.mult)
                v.drain()
                v.scalar_tensor_tensor(U_[:], U_[:], -1.0, S_[:], ALU.mult, ALU.add)
                v.tensor_scalar(U_[:], U_[:], 1.0, None, ALU.add)
                v.tensor_scalar(A_[:], A_[:], 1.0 / 1024.0, None, ALU.mult)
                v.drain()
                if nseg is None:
                    v.tensor_scalar(B_[:], U_[:], A_[:], None, ALU.mult)
                else:
                    for r in range(nseg):
                        v.tensor_scalar(B_[0:1, r * 1024:(r + 1) * 1024],
                                        U_[0:1, r * 1024:(r + 1) * 1024],
                                        A_[0:1, r:r + 1], None, ALU.mult)
                v.drain()
            v.drain().then_inc(s_b, 1)

        @blk.tensor
        def _(te):
            te.wait_ge(s_b, 3)
            for dc in range(8):
                mm = te.transpose(ps_t[:, dc * 8:dc * 8 + 6],
                                  bvt[:, dc * 128:(dc + 1) * 128], idnt[0:6, 0:6])
                if dc == 7:
                    mm.then_inc(s_b, 1)

        @blk.vector
        def _(v):
            v.wait_ge(s_b, 4)
            for dc in range(8):
                v.tensor_copy(bcol[:, dc * 6:(dc + 1) * 6], ps_t[:, dc * 8:dc * 8 + 6])
            v.drain().then_inc(s_b, 1)

        st_n = [0]
        sm_n = [0]
        sg_n = [0]
        p_n = [0]
        pv_n = [0]
        ev_n = [0]
        ld_n = [80]

        for hh in range(2):
            c0 = hh * 1024

            vb_ps = [ps_t, ps_o[0]]

            @blk.tensor
            def _(te):
                te.wait_ge(s_b, 5)
                if hh > 0:
                    te.wait_ge(s_ev, ev_n[0])
                for half in range(2):
                    te.matmul(vb_ps[half][:, :512], onesc[:, :],
                              bvr[0:1, 4096 + c0 + half * 512: 4096 + c0 + (half + 1) * 512],
                              start=True, stop=True).then_inc(s_vb, 1)

            for half in range(2):
                @blk.vector
                def _(v):
                    v.wait_ge(s_vb, hh * 2 + half + 1)
                    v.tensor_copy(vbb[:, half * 512:(half + 1) * 512], vb_ps[half][:, :512])
                    v.drain().then_inc(s_b, 1)

            @blk.sync
            def _(s):
                if hh > 0:
                    s.wait_ge(s_pv, pv_n[0])
                for dc in range(ndc):
                    s.dma_start_transpose(
                        out=qf[:, dc * T:(dc + 1) * T],
                        in_=xn_d[:, c0 + dc * 128:c0 + (dc + 1) * 128]).then_inc(ld, 16)
                s.dma_start(out=xv.rearrange("p (kb d) -> p kb d", d=HD),
                            in_=xn_h.ap().rearrange("(kb p) d -> p kb d", p=128)[:, :, c0:c0 + HD]
                            ).then_inc(ld, 16)
            ld_n[0] += (ndc + 1) * 16

            ldq = ld_n[0]

            @blk.vector
            def _(v):
                v.wait_ge(s_b, 5)
                v.wait_ge(ld, ldq)
                if hh > 0:
                    v.wait_ge(s_pv, pv_n[0])
                for dc in range(ndc):
                    qcol = bcol[:, dc * 6 + hh:dc * 6 + hh + 1]
                    kcol = bcol[:, dc * 6 + 2 + hh:dc * 6 + 2 + hh + 1]
                    v.tensor_scalar(kf[:, dc * T:(dc + 1) * T], qf[:, dc * T:(dc + 1) * T],
                                    kcol, None, ALU.mult)
                    v.tensor_scalar(qf[:, dc * T:(dc + 1) * T], qf[:, dc * T:(dc + 1) * T],
                                    qcol, None, ALU.mult)
                v.drain().then_inc(s_qk, 1)

            for g_ in range(NG):
                trips = 4 * (g_ + 1)
                for kb in range(trips):
                    pb = (sm_n[0]) % 2

                    @blk.tensor
                    def _(te):
                        te.wait_ge(s_qk, hh + 1)
                        if g_ == 0 and kb == 0 and hh == 0:
                            pass
                        if sm_n[0] >= 2:
                            te.wait_ge(s_p, sm_n[0] - 1)
                        if kb == 0:
                            te.wait_ge(s_pv, pv_n[0])
                        for dc in range(ndc):
                            mm = te.matmul(ps_s[pb][:],
                                           kf[:, dc * T + kb * 128:dc * T + (kb + 1) * 128],
                                           qf[:, dc * T + g_ * 512:dc * T + (g_ + 1) * 512],
                                           start=(dc == 0), stop=(dc == ndc - 1))
                            if dc == ndc - 1:
                                mm.then_inc(s_sm, 1)
                    sm_n[0] += 1

                    @blk.scalar
                    def _(sc):
                        sc.wait_ge(s_sm, sm_n[0])
                        sc.activation(pt[:, kb * 512:(kb + 1) * 512], ps_s[pb][:],
                                      AF.Sigmoid, scale=SCL)
                        sc.drain().then_inc(s_sg, 1)
                    sg_n[0] += 1

                    @blk.vector
                    def _(v):
                        v.wait_ge(s_sg, sg_n[0])
                        jstar = kb - 4 * g_
                        if jstar >= 0:
                            for j in range(4):
                                sl = pt[:, kb * 512 + j * 128: kb * 512 + (j + 1) * 128]
                                if j < jstar:
                                    v.memset(sl, 0.0)
                                elif j == jstar:
                                    v.tensor_tensor(sl, sl, trit[:], ALU.mult)
                        v.drain().then_inc(s_p, 1)
                    p_n[0] += 1

                for dg in range(2):
                    for j in range(4):
                        pj = j

                        @blk.tensor
                        def _(te):
                            te.wait_ge(s_p, p_n[0])
                            te.wait_ge(s_b, 7 + hh * 2)
                            if pv_n[0] >= 4:
                                te.wait_ge(s_ev, pv_n[0] - 3)
                            for kb in range(trips):
                                mm = te.matmul(ps_o[pj][:],
                                               pt[:, kb * 512 + j * 128:kb * 512 + (j + 1) * 128],
                                               xv[:, kb * HD + dg * 512:kb * HD + (dg + 1) * 512],
                                               start=(kb == 0), stop=(kb == trips - 1))
                                if kb == trips - 1:
                                    mm.then_inc(s_pv, 1)
                        pv_n[0] += 1

                        @blk.vector
                        def _(v):
                            v.wait_ge(s_pv, pv_n[0])
                            v.wait_ge(s_b, 7 + hh * 2)
                            if ev_n[0] >= 2:
                                v.wait_ge(st, (ev_n[0] - 1) * 16)
                            v.tensor_tensor(outb[ev_n[0] % 2][:], ps_o[pj][:],
                                            vbb[:, dg * 512:(dg + 1) * 512], ALU.mult)
                            v.drain().then_inc(s_ev, 1)
                        ev_n[0] += 1

                        @blk.sync
                        def _(s):
                            s.wait_ge(s_ev, ev_n[0])
                            s.dma_start(
                                out=y_d[g_ * 512 + j * 128:g_ * 512 + (j + 1) * 128,
                                        c0 + dg * 512:c0 + (dg + 1) * 512],
                                in_=outb[(ev_n[0] - 1) % 2][:]).then_inc(st, 16)
                        st_n[0] += 16
        @blk.sync
        def _(s):
            s.wait_ge(st, st_n[0])
    return nc


def run_attn1(xn_full, bv_q, bv_k, bv_v):
    nc = _cached(("attn1",), build_attn1)
    idn32 = np.eye(128, dtype=np.float32)
    onesr = np.ones((1, 128), np.float32)
    i_ = np.arange(128)
    triT = (i_[None, :] >= i_[:, None]).astype(BF)
    maps = []
    for c in range(NCORE):
        b_, half = c // 2, c % 2
        h0 = half * 2
        xnb = np.ascontiguousarray(xn_full[b_][:, h0 * 1024:(h0 + 2) * 1024])
        bvs = np.concatenate([bv_q[h0:h0 + 2], bv_k[h0:h0 + 2], bv_v[h0:h0 + 2]], 0).astype(np.float32)
        maps.append({"xnb": xnb, "bvs": np.ascontiguousarray(bvs),
                     "bvf": np.ascontiguousarray(bvs.reshape(1, -1)),
                     "idn32": idn32, "onesr": onesr, "triT": triT})
    res = run_bass_kernel_spmd(nc, maps, core_ids=list(range(NCORE)))
    out = np.zeros((B, T, C), BF)
    for c in range(NCORE):
        b_, half = c // 2, c % 2
        out[b_][:, half * 2048:(half + 1) * 2048] = res.results[c]["y"]
    return out


def build_attn_qk(layer, dbg_phase1=False, dbg_heads=16):
    NQ = 2048
    NKV = 2048 if layer == 0 else 512
    QKVD = NQ + 2 * NKV
    nc = bass.Bass("TRN2", target_bir_lowering=False)
    xn_h = nc.dram_tensor("xnb", [T, C], BF16, kind="ExternalInput")
    w_h = nc.dram_tensor("wcat", [C, QKVD], F32, kind="ExternalInput")
    ct_d = nc.dram_tensor("ropeC", [T, 128], F32, kind="ExternalInput").ap()
    sx_d = nc.dram_tensor("ropeSx", [T, 128], F32, kind="ExternalInput").ap()
    cst_d = nc.dram_tensor("cstb", [128, 128], F32, kind="ExternalInput").ap()
    oneb_d = nc.dram_tensor("onesb", [128, 128], BF16, kind="ExternalInput").ap()
    idb_d = nc.dram_tensor("idnb", [128, 128], BF16, kind="ExternalInput").ap()
    id3_d = nc.dram_tensor("idn32", [128, 128], F32, kind="ExternalInput").ap()
    tri_d = nc.dram_tensor("triT", [128, 128], BF16, kind="ExternalInput").ap()
    y_d = nc.dram_tensor("y", [T, NQ], BF16, kind="ExternalOutput").ap()
    qkv_h = nc.dram_tensor("qkv", [T, QKVD], BF16,
                           kind="ExternalOutput" if dbg_phase1 else "Internal")
    qkv_d = qkv_h.ap()
    nk = C // 128
    ntt = T // 128
    nog = QKVD // 512
    NG = T // 512
    NKB = T // 128
    SCL = 1.0 / math.sqrt(128.0)
    with ExitStack() as es:
        def sb(name, shape, dt):
            return es.enter_context(nc.sbuf_tensor(name, shape, dt))
        xf = sb("xf", [128, nk * T], BF16)
        wblk = sb("wblk", [128, nk * 512], BF16)
        ctall = sb("ctall", [128, 16 * 128], F32)
        sxall = sb("sxall", [128, 16 * 128], F32)
        evf = sb("evf", [128, 512], F32)
        rsc = sb("rsc", [128, 512], F32)
        outq = [sb(f"outq{i}", [128, 512], BF16) for i in range(2)]
        cstt = sb("cstt", [128, 128], F32)
        onesb = sb("onesb_t", [128, 128], BF16)
        idnb = sb("idnb_t", [128, 128], BF16)
        idn3 = sb("idn3_t", [128, 128], F32)
        trit = sb("trit", [128, 128], BF16)
        qh = [xf[:, i * 3 * T:i * 3 * T + T] for i in range(2)]
        kh = [xf[:, i * 3 * T + T:i * 3 * T + 2 * T] for i in range(2)]
        vh = [xf[:, i * 3 * T + 2 * T:i * 3 * T + 2 * T + NKB * 128] for i in range(2)]
        pt2 = [xf[:, 6 * T + i * NKB * 512:6 * T + (i + 1) * NKB * 512] for i in range(2)]
        ofm = xf[:, 6 * T + 2 * NKB * 512:6 * T + 2 * NKB * 512 + 512]
        lrow = sb("lrow", [1, 512], F32)
        lcol = sb("lcol", [128, 4], F32)
        outb = [sb(f"outb{i}", [128, 128], BF16) for i in range(2)]
        ps_p = [es.enter_context(nc.psum_tensor(f"psp{i}", [128, 512], F32)) for i in range(2)]
        ps_s = [es.enter_context(nc.psum_tensor(f"pss{i}", [128, 512], F32)) for i in range(2)]
        ps_l = es.enter_context(nc.psum_tensor("psl", [128, 512], F32))
        ps_o = es.enter_context(nc.psum_tensor("pso", [128, 512], F32))
        ps_tr = es.enter_context(nc.psum_tensor("pstr", [128, 512], F32))
        ps_tb = es.enter_context(nc.psum_tensor("pstb", [128, 512], BF16))
        ld = es.enter_context(nc.semaphore("ld"))
        ldw = es.enter_context(nc.semaphore("ldw"))
        ldh = es.enter_context(nc.semaphore("ldh"))
        s_pm = es.enter_context(nc.semaphore("s_pm"))
        s_pe = es.enter_context(nc.semaphore("s_pe"))
        s_sm = es.enter_context(nc.semaphore("s_sm"))
        s_sg = es.enter_context(nc.semaphore("s_sg"))
        s_p = es.enter_context(nc.semaphore("s_p"))
        s_l = es.enter_context(nc.semaphore("s_l"))
        s_pv = es.enter_context(nc.semaphore("s_pv"))
        s_tr = es.enter_context(nc.semaphore("s_tr"))
        s_ev = es.enter_context(nc.semaphore("s_ev"))
        st = es.enter_context(nc.semaphore("st"))
        stq = es.enter_context(nc.semaphore("stq"))
        blk = es.enter_context(nc.Block())

        @blk.scalar
        def _(sc):
            sc.preload_act_table(AF.Exp)

        @blk.gpsimd
        def _(g):
            g.dma_start(out=cstt[:], in_=cst_d[:, :]).then_inc(ld, 16)
            g.dma_start(out=onesb[:], in_=oneb_d[:, :]).then_inc(ld, 16)
            g.dma_start(out=idnb[:], in_=idb_d[:, :]).then_inc(ld, 16)
            g.dma_start(out=idn3[:], in_=id3_d[:, :]).then_inc(ld, 16)
            g.dma_start(out=trit[:], in_=tri_d[:, :]).then_inc(ld, 16)

        @blk.sync
        def _(s):
            for kc in range(nk):
                s.dma_start_transpose(
                    out=xf[:, kc * T:(kc + 1) * T],
                    in_=xn_h.ap()[:, kc * 128:(kc + 1) * 128]).then_inc(ld, 16)

        @blk.gpsimd
        def _(g):
            g.dma_start(out=ctall.rearrange("p (tt j) -> p tt j", j=128),
                        in_=ct_d.rearrange("(tt p) j -> p tt j", p=128)).then_inc(ldh, 16)
            g.dma_start(out=sxall.rearrange("p (tt j) -> p tt j", j=128),
                        in_=sx_d.rearrange("(tt p) j -> p tt j", p=128)).then_inc(ldh, 16)

        ldw_n = [0]
        pm_n = [0]
        pe_n = [0]
        stq_n = [0]

        def load_w(og):
            @blk.gpsimd
            def _(g):
                if og > 0:
                    g.wait_ge(s_pm, og * ntt)
                src = bass.AP(w_h, og * 512, [[QKVD, 128], [128 * QKVD, nk], [1, 512]])
                g.dma_start(out=wblk.rearrange("p (kc n) -> p kc n", n=512),
                            in_=src).then_inc(ldw, 16)

        for og in range(nog):
            load_w(og)
            if layer == 0:
                is_rope = og < 8
            else:
                is_rope = og < 5
            for tt in range(ntt):
                pb = pm_n[0] % 2

                @blk.tensor
                def _(te):
                    if og == 0 and tt == 0:
                        te.wait_ge(ld, 80 + nk * 16)
                    te.wait_ge(ldw, (og + 1) * 16)
                    if pm_n[0] >= 2:
                        te.wait_ge(s_pe, pm_n[0] - 1)
                    for kc in range(nk):
                        mm = te.matmul(ps_p[pb][:],
                                       xf[:, kc * T + tt * 128:kc * T + (tt + 1) * 128],
                                       wblk[:, kc * 512:(kc + 1) * 512],
                                       start=(kc == 0), stop=(kc == nk - 1))
                        if kc == nk - 1:
                            mm.then_inc(s_pm, 1)
                pm_n[0] += 1

                if is_rope:
                    @blk.vector
                    def _(v):
                        v.wait_ge(s_pm, pm_n[0])
                        v.wait_ge(ldh, 32)
                        if pe_n[0] >= 2:
                            v.wait_ge(stq, (pe_n[0] - 1) * 16)
                        v.tensor_copy(evf[:], ps_p[pb][:])
                        v.drain()
                        crep = bass.AP(ctall, tt * 128, [[16 * 128, 128], [0, 4], [1, 128]])
                        swap = bass.AP(evf, 64, [[512, 128], [128, 4], [-64, 2], [1, 64]])
                        sxr = bass.AP(sxall, tt * 128, [[16 * 128, 128], [0, 4], [64, 2], [1, 64]])
                        v.tensor_tensor(
                            rsc[:].rearrange("p (h r j) -> p h r j", h=4, r=2),
                            swap, sxr, ALU.mult)
                        v.tensor_tensor(
                            evf[:].rearrange("p (h j) -> p h j", j=128),
                            evf[:].rearrange("p (h j) -> p h j", j=128),
                            crep, ALU.mult)
                        v.drain()
                        v.tensor_tensor(outq[pb][:], evf[:], rsc[:], ALU.add)
                        v.drain().then_inc(s_pe, 1)
                else:
                    @blk.vector
                    def _(v):
                        v.wait_ge(s_pm, pm_n[0])
                        if pe_n[0] >= 2:
                            v.wait_ge(stq, (pe_n[0] - 1) * 16)
                        v.tensor_copy(outq[pb][:], ps_p[pb][:])
                        v.drain().then_inc(s_pe, 1)
                pe_n[0] += 1

                @blk.sync
                def _(s2):
                    s2.wait_ge(s_pe, pe_n[0])
                    s2.dma_start(out=qkv_d[tt * 128:(tt + 1) * 128, og * 512:(og + 1) * 512],
                                 in_=outq[pb][:]).then_inc(stq, 16)
                stq_n[0] += 16

        sm_n = [0]
        sg_n = [0]
        p_n = [0]
        pv_group_end = {}
        l_n = [0]
        pv_n = [0]
        tr_n = [0]
        ev_n = [0]
        st_n = [0]
        ldh_n = [32]

        if dbg_phase1:
            @blk.sync
            def _(s2):
                s2.wait_ge(stq, stq_n[0])
                s2.dma_start(out=y_d[0:128, 0:512], in_=outq[0][:]).then_inc(st, 16)
                s2.wait_ge(st, 16)
            es.close() if False else None
        def head_cols(head):
            if layer == 0:
                return head * 128, NQ + head * 128, NQ + NKV + head * 128
            return head * 128, NQ + (head // 4) * 128, NQ + NKV + (head // 4) * 128

        pv_marks = {}

        def load_head(head):
            hb = head % 2
            qc, kc0, vc0 = head_cols(head)

            @blk.sync
            def _(s2):
                if head == 0:
                    s2.wait_ge(stq, stq_n[0])
                elif head >= 2:
                    s2.wait_ge(s_pv, pv_marks[head - 2])
                s2.dma_start_transpose(out=qh[hb][:], in_=qkv_d[:, qc:qc + 128]).then_inc(ldh, 16)
                s2.dma_start_transpose(out=kh[hb][:], in_=qkv_d[:, kc0:kc0 + 128]).then_inc(ldh, 16)
                for kb in range(NKB):
                    s2.dma_start(out=vh[hb][:, kb * 128:(kb + 1) * 128],
                                 in_=qkv_d[kb * 128:(kb + 1) * 128, vc0:vc0 + 128]).then_inc(ldh, 16)
            ldh_n[0] += (2 + NKB) * 16

        load_head(0 if not dbg_phase1 else 16)
        if dbg_heads > 1:
            load_head(1)
        for head in range(0 if not dbg_phase1 else 16, dbg_heads):
            hb = head % 2
            qh_c, kh_c, vh_c = qh[hb], kh[hb], vh[hb]
            ldh_need = 32 + (head + 1) * (2 + NKB) * 16

            for g_ in range(NG):
                trips = 4 * (g_ + 1)
                gidx = head * NG + g_
                pt = pt2[gidx % 2]
                for kb in range(trips):
                    pb = sm_n[0] % 2

                    @blk.tensor
                    def _(te):
                        te.wait_ge(ldh, ldh_need)
                        if sm_n[0] >= 2:
                            te.wait_ge(s_sg, sm_n[0] - 1)
                        if kb == 0 and gidx >= 2:
                            te.wait_ge(s_pv, pv_group_end[gidx - 2])
                        te.matmul(ps_s[pb][:], kh_c[:, kb * 128:(kb + 1) * 128],
                                  qh_c[:, g_ * 512:(g_ + 1) * 512],
                                  start=True, stop=True).then_inc(s_sm, 1)
                    sm_n[0] += 1

                    @blk.scalar
                    def _(sc):
                        sc.wait_ge(s_sm, sm_n[0])
                        sc.activation(pt[:, kb * 512:(kb + 1) * 512], ps_s[pb][:],
                                      AF.Exp, bias=cstt[:, 0:1], scale=SCL)
                        sc.drain().then_inc(s_sg, 1)
                    sg_n[0] += 1

                    jstar = kb - 4 * g_
                    if jstar >= 0:
                        @blk.vector
                        def _(v):
                            v.wait_ge(s_sg, sg_n[0])
                            for j in range(4):
                                sl = pt[:, kb * 512 + j * 128: kb * 512 + (j + 1) * 128]
                                if j < jstar:
                                    v.memset(sl, 0.0)
                                elif j == jstar:
                                    v.tensor_tensor(sl, sl, trit[:], ALU.mult)
                            v.drain().then_inc(s_p, 1)
                        p_n[0] += 1

                @blk.tensor
                def _(te):
                    te.wait_ge(s_sg, sg_n[0])
                    te.wait_ge(s_p, p_n[0])
                    te.wait_ge(s_l, max(0, l_n[0] - 2))
                    for kb in range(trips):
                        mm = te.matmul(ps_l[0:1, :], onesb[:, 0:1], pt[:, kb * 512:(kb + 1) * 512],
                                       start=(kb == 0), stop=(kb == trips - 1))
                        if kb == trips - 1:
                            mm.then_inc(s_l, 1)
                l_n[0] += 1

                @blk.vector
                def _(v):
                    v.wait_ge(s_l, l_n[0])
                    v.tensor_copy(lrow[:], ps_l[0:1, :])
                    v.drain().then_inc(s_l, 1)
                l_n[0] += 1

                @blk.tensor
                def _(te):
                    te.wait_ge(s_l, l_n[0])
                    for j in range(4):
                        mm = te.transpose(ps_tr[:, j:j + 1], lrow[0:1, j * 128:(j + 1) * 128],
                                          idn3[0:1, 0:1])
                        if j == 3:
                            mm.then_inc(s_l, 1)
                l_n[0] += 1

                @blk.vector
                def _(v):
                    v.wait_ge(s_l, l_n[0])
                    v.tensor_copy(lcol[:], ps_tr[:, 0:4])
                    v.drain()
                    v.reciprocal(lcol[:], lcol[:])
                    v.drain().then_inc(s_l, 1)
                l_n[0] += 1

                @blk.tensor
                def _(te):
                    te.wait_ge(s_sg, sg_n[0])
                    te.wait_ge(s_p, p_n[0])
                    if pv_n[0] >= 1:
                        te.wait_ge(s_tr, tr_n[0])
                    for kb in range(trips):
                        mm = te.matmul(ps_o[:], vh_c[:, kb * 128:(kb + 1) * 128],
                                       pt[:, kb * 512:(kb + 1) * 512],
                                       start=(kb == 0), stop=(kb == trips - 1))
                        if kb == trips - 1:
                            mm.then_inc(s_pv, 1)
                pv_n[0] += 1

                pv_group_end[gidx] = pv_n[0]

                @blk.vector
                def _(v):
                    v.wait_ge(s_pv, pv_n[0])
                    v.tensor_copy(ofm[:], ps_o[:])
                    v.drain().then_inc(s_pv, 1)
                pv_n[0] += 1

                @blk.tensor
                def _(te):
                    te.wait_ge(s_pv, pv_n[0])
                    if tr_n[0] >= 1:
                        te.wait_ge(s_ev, ev_n[0])
                    for j in range(4):
                        mm = te.transpose(ps_tb[:, j * 128:(j + 1) * 128],
                                          ofm[:, j * 128:(j + 1) * 128], idnb[:])
                        if j == 3:
                            mm.then_inc(s_tr, 1)
                tr_n[0] += 1

                for j in range(4):
                    @blk.vector
                    def _(v):
                        v.wait_ge(s_tr, tr_n[0])
                        if ev_n[0] >= 2:
                            v.wait_ge(st, (ev_n[0] - 1) * 16)
                        v.tensor_scalar(outb[ev_n[0] % 2][:], ps_tb[:, j * 128:(j + 1) * 128],
                                        lcol[:, j:j + 1], None, ALU.mult)
                        v.drain().then_inc(s_ev, 1)
                    ev_n[0] += 1

                    @blk.sync
                    def _(s2):
                        s2.wait_ge(s_ev, ev_n[0])
                        s2.dma_start(out=y_d[g_ * 512 + j * 128:g_ * 512 + (j + 1) * 128,
                                             head_cols(head)[0]:head_cols(head)[0] + 128],
                                     in_=outb[(ev_n[0] - 1) % 2][:]).then_inc(st, 16)
                    st_n[0] += 16
            pv_marks[head] = pv_n[0]
            if head + 2 < dbg_heads:
                load_head(head + 2)
        @blk.sync
        def _(s2):
            s2.wait_ge(st, st_n[0])
    return nc


def _rope_tables(layer):
    pos = np.arange(T, dtype=np.float64)
    th = 1.0 / 10000.0 ** (np.arange(0, 128, 2, dtype=np.float64) / 128.0)
    ptab = np.outer(pos, th)
    c = np.cos(ptab).astype(np.float32)
    s = np.sin(ptab).astype(np.float32)
    Ct = np.concatenate([c, c], 1)
    Sx = np.concatenate([-s, s], 1)
    return Ct, Sx


def run_attn_qk(layer, xn_full, inputs, dbg_phase1=False, dbg_heads=16):
    nc = _cached(("attnqk", layer, dbg_phase1, dbg_heads),
                 lambda: build_attn_qk(layer, dbg_phase1, dbg_heads))
    Ct, Sx = _rope_tables(layer)
    cstb = np.zeros((128, 128), np.float32)
    cstb[:, 0] = -25.0
    onesb = np.zeros((128, 128), np.float32)
    onesb[:, 0] = 1.0
    onesb = onesb.astype(BF)
    idnb = np.eye(128, dtype=BF)
    idn32 = np.eye(128, dtype=np.float32)
    i_ = np.arange(128)
    triT = (i_[None, :] >= i_[:, None]).astype(BF)
    if layer == 0:
        wqT = np.asarray(inputs["L0_attn_wq"]).T
        wkT = np.asarray(inputs["L0_attn_wk"]).T
        wvT = np.asarray(inputs["L0_attn_wv"]).T
    else:
        WT = np.asarray(inputs["L2_attn_Wqkv"]).T
    maps = []
    for cix in range(NCORE):
        b_, half = cix // 2, cix % 2
        if layer == 0:
            wcat = np.concatenate([wqT[:, half * 2048:(half + 1) * 2048],
                                   wkT[:, half * 2048:(half + 1) * 2048],
                                   wvT[:, half * 2048:(half + 1) * 2048]], 1)
        else:
            q = WT[:, half * 2048:(half + 1) * 2048]
            k = WT[:, 4096 + half * 512:4096 + (half + 1) * 512]
            v = WT[:, 5120 + half * 512:5120 + (half + 1) * 512]
            wcat = np.concatenate([q, k, v], 1)
        maps.append({"xnb": np.ascontiguousarray(xn_full[b_]),
                     "wcat": np.ascontiguousarray(wcat.astype(np.float32)),
                     "ropeC": Ct, "ropeSx": Sx, "cstb": cstb, "onesb": onesb,
                     "idnb": idnb, "idn32": idn32, "triT": triT})
    res = run_bass_kernel_spmd(nc, maps, core_ids=list(range(NCORE)))
    if dbg_phase1:
        return [r["qkv"] for r in res.results]
    out = np.zeros((B, T, C), BF)
    for cix in range(NCORE):
        b_, half = cix // 2, cix % 2
        out[b_][:, half * 2048:(half + 1) * 2048] = res.results[cix]["y"]
    return out


def build_ffn1():
    nc = bass.Bass("TRN2", target_bir_lowering=False)
    x_d = nc.dram_tensor("x", [TPC, C], BF16, kind="ExternalInput").ap()
    w_h = nc.dram_tensor("w", [C, C], F32, kind="ExternalInput")
    b_d = nc.dram_tensor("b", [1, C], F32, kind="ExternalInput").ap()
    aux_h = nc.dram_tensor("waux", [C, 128], F32, kind="ExternalInput")
    baux_d = nc.dram_tensor("baux", [1, 68], F32, kind="ExternalInput").ap()
    bt_h = nc.dram_tensor("bT", [64, C], F32, kind="ExternalInput")
    one_d = nc.dram_tensor("onesr", [1, 128], F32, kind="ExternalInput").ap()
    idb_d = nc.dram_tensor("idnb", [128, 128], BF16, kind="ExternalInput").ap()
    y_d = nc.dram_tensor("y", [TPC, C], BF16, kind="ExternalOutput").ap()
    nt = TPC // 128
    nk = C // 128
    no = C // 512
    with ExitStack() as es:
        def sb(name, shape, dt):
            return es.enter_context(nc.sbuf_tensor(name, shape, dt))
        xf = sb("xf", [128, nk * TPC], BF16)
        wblk = [sb(f"wblk{i}", [128, nk * 512], BF16) for i in range(2)]
        auxb = sb("auxb", [128, nk * 128], BF16)
        btb = sb("btb", [64, C], BF16)
        bias = sb("bias", [1, C], F32)
        bauxb = sb("bauxb", [1, 68], F32)
        ones = sb("ones", [1, 128], F32)
        idnb = sb("idnb_t", [128, 128], BF16)
        us = sb("us", [128, 68], F32)
        msk = sb("msk", [128, 8], F32)
        umb = sb("umb", [128, 64], BF16)
        umT = sb("umT", [64, nt * 128], BF16)
        outb = [sb(f"outb{i}", [128, 512], BF16) for i in range(2)]
        ps_u = es.enter_context(nc.psum_tensor("psu", [128, 512], F32))
        ps_tr = es.enter_context(nc.psum_tensor("pstr", [128, 512], BF16))
        ps = [es.enter_context(nc.psum_tensor(f"ps{i}", [128, 512], F32)) for i in range(2)]
        ldx = es.enter_context(nc.semaphore("ldx"))
        ldw = es.enter_context(nc.semaphore("ldw"))
        s_um = es.enter_context(nc.semaphore("s_um"))
        s_mm = es.enter_context(nc.semaphore("s_mm"))
        s_ev = es.enter_context(nc.semaphore("s_ev"))
        st = es.enter_context(nc.semaphore("st"))
        blk = es.enter_context(nc.Block())

        @blk.gpsimd
        def _(g):
            g.dma_start(out=bias[:], in_=b_d[:, :]).then_inc(ldx, 16)
            g.dma_start(out=bauxb[:], in_=baux_d[:, :]).then_inc(ldx, 16)
            g.dma_start(out=ones[:], in_=one_d[:, :]).then_inc(ldx, 16)
            g.dma_start(out=idnb[:], in_=idb_d[:, :]).then_inc(ldx, 16)
            src = bass.AP(aux_h, 0, [[128, 128], [128 * 128, nk], [1, 128]])
            g.dma_start(out=auxb.rearrange("p (kc n) -> p kc n", n=128), in_=src).then_inc(ldx, 16)
            g.dma_start(out=btb[:], in_=bt_h.ap()[:, :]).then_inc(ldx, 16)

        @blk.sync
        def _(s):
            for kc in range(nk):
                s.dma_start_transpose(
                    out=xf[:, kc * TPC:(kc + 1) * TPC],
                    in_=x_d[:, kc * 128:(kc + 1) * 128]).then_inc(ldx, 16)
        ldx_all = 96 + nk * 16

        um_n = [0]
        for t in range(nt):
            @blk.tensor
            def _(te):
                te.wait_ge(ldx, ldx_all)
                if t > 0:
                    te.wait_ge(s_um, um_n[0])
                for kc in range(nk):
                    te.matmul(ps_u[:, :68], xf[:, kc * TPC + t * 128:kc * TPC + (t + 1) * 128],
                              auxb[:, kc * 128:kc * 128 + 68], start=(kc == 0), stop=False)
                te.matmul(ps_u[:, :68], ones[:, :], bauxb[:, :],
                          start=False, stop=True).then_inc(s_um, 1)
            um_n[0] += 1

            @blk.vector
            def _(v):
                v.wait_ge(s_um, um_n[0])
                v.tensor_copy(us[:], ps_u[:, :68])
                v.drain()
                v.tensor_reduce(msk[:, 0:1], us[:, 64:68], mybir.AxisListType.X, ALU.max)
                v.tensor_tensor(msk[:, 1:2], us[:, 64:65], us[:, 65:66], ALU.max)
                v.drain()
                v.tensor_tensor(msk[:, 2:3], msk[:, 1:2], us[:, 66:67], ALU.max)
                v.drain()
                v.tensor_scalar(msk[:, 3:4], us[:, 64:65], msk[:, 0:1], None, ALU.is_lt)
                v.tensor_scalar(msk[:, 4:5], msk[:, 1:2], msk[:, 0:1], None, ALU.is_lt)
                v.tensor_scalar(msk[:, 5:6], msk[:, 2:3], msk[:, 0:1], None, ALU.is_lt)
                v.drain()
                v.tensor_scalar(us[:, 8:16], us[:, 8:16], msk[:, 3:4], None, ALU.mult)
                v.tensor_scalar(us[:, 16:32], us[:, 16:32], msk[:, 4:5], None, ALU.mult)
                v.tensor_scalar(us[:, 32:64], us[:, 32:64], msk[:, 5:6], None, ALU.mult)
                v.drain()
                v.tensor_copy(umb[:], us[:, 0:64])
                v.drain().then_inc(s_um, 1)
            um_n[0] += 1

            @blk.tensor
            def _(te):
                te.wait_ge(s_um, um_n[0])
                te.transpose(ps_tr[0:64, 0:128], umb[:], idnb[:]).then_inc(s_um, 1)
            um_n[0] += 1

            @blk.vector
            def _(v):
                v.wait_ge(s_um, um_n[0])
                v.tensor_copy(umT[:, t * 128:(t + 1) * 128], ps_tr[0:64, 0:128])
                v.drain().then_inc(s_um, 1)
            um_n[0] += 1

        mm_n = [0]
        ev_n = [0]
        st_n = [0]

        def load_w(og, buf):
            @blk.gpsimd
            def _(g):
                if og >= 2:
                    g.wait_ge(s_mm, (og - 1) * nt)
                src = bass.AP(w_h, og * 512, [[C, 128], [128 * C, nk], [1, 512]])
                g.dma_start(out=wblk[buf].rearrange("p (kc n) -> p kc n", n=512),
                            in_=src).then_inc(ldw, 16)

        load_w(0, 0)
        load_w(1, 1)
        for og in range(no):
            buf = og % 2
            for t in range(nt):
                pb = mm_n[0] % 2

                @blk.tensor
                def _(te):
                    te.wait_ge(ldw, (og + 1) * 16)
                    if og == 0 and t == 0:
                        te.wait_ge(s_um, um_n[0])
                    if mm_n[0] >= 2:
                        te.wait_ge(s_ev, mm_n[0] - 1)
                    for kc in range(nk):
                        te.matmul(ps[pb][:], xf[:, kc * TPC + t * 128:kc * TPC + (t + 1) * 128],
                                  wblk[buf][:, kc * 512:(kc + 1) * 512],
                                  start=(kc == 0), stop=False)
                    te.matmul(ps[pb][:], umT[:, t * 128:(t + 1) * 128],
                              btb[:, og * 512:(og + 1) * 512], start=False, stop=False)
                    te.matmul(ps[pb][:], ones[:, :], bias[:, og * 512:(og + 1) * 512],
                              start=False, stop=True).then_inc(s_mm, 1)
                mm_n[0] += 1

                @blk.vector
                def _(v):
                    v.wait_ge(s_mm, mm_n[0])
                    if ev_n[0] >= 2:
                        v.wait_ge(st, (ev_n[0] - 1) * 16)
                    v.tensor_copy(outb[pb][:], ps[pb][:])
                    v.drain().then_inc(s_ev, 1)
                ev_n[0] += 1

                @blk.sync
                def _(s):
                    s.wait_ge(s_ev, ev_n[0])
                    s.dma_start(out=y_d[t * 128:(t + 1) * 128, og * 512:(og + 1) * 512],
                                in_=outb[pb][:]).then_inc(st, 16)
                st_n[0] += 16
            if og + 2 < no:
                load_w(og + 2, buf)
        @blk.sync
        def _(s):
            s.wait_ge(st, st_n[0])
    return nc


def run_ffn1(xs, inputs):
    nc = _cached(("ffn1",), build_ffn1)
    A = np.asarray(inputs["L1_ffn_A"]).astype(np.float32)
    gw = np.asarray(inputs["L1_ffn_gate_w"]).astype(np.float32)
    waux = np.concatenate([A.T, gw.T, np.zeros((C, 60), np.float32)], 1)
    baux = np.zeros((1, 68), np.float32)
    baux[0, 64:] = np.asarray(inputs["L1_ffn_gate_b"])
    m = {
        "w": np.ascontiguousarray(np.asarray(inputs["L1_ffn_W"]).astype(np.float32).T),
        "b": np.asarray(inputs["L1_ffn_b"]).reshape(1, C).astype(np.float32),
        "waux": np.ascontiguousarray(waux),
        "baux": baux,
        "bT": np.ascontiguousarray(np.asarray(inputs["L1_ffn_B"]).astype(np.float32).T),
        "onesr": np.ones((1, 128), np.float32),
        "idnb": np.eye(128, dtype=BF),
    }
    maps = [dict(m, x=xs[i]) for i in range(NCORE)]
    res = run_bass_kernel_spmd(nc, maps, core_ids=list(range(NCORE)))
    return [r["y"] for r in res.results]


def build_ffn2():
    nc = bass.Bass("TRN2", target_bir_lowering=False)
    x_d = nc.dram_tensor("x", [TPC, C], BF16, kind="ExternalInput").ap()
    w_h = nc.dram_tensor("w", [C, C], F32, kind="ExternalInput")
    b_d = nc.dram_tensor("b", [1, C], F32, kind="ExternalInput").ap()
    aux_h = nc.dram_tensor("waux", [C, 128], F32, kind="ExternalInput")
    cor_d = nc.dram_tensor("coresF", [128, 16], F32, kind="ExternalInput").ap()
    bw_d = nc.dram_tensor("bwT", [16, C], F32, kind="ExternalInput").ap()
    one_d = nc.dram_tensor("onesr", [1, 128], F32, kind="ExternalInput").ap()
    idb_d = nc.dram_tensor("idnb", [128, 128], BF16, kind="ExternalInput").ap()
    y_d = nc.dram_tensor("y", [TPC, C], BF16, kind="ExternalOutput").ap()
    nt = TPC // 128
    nk = C // 128
    no = C // 512
    with ExitStack() as es:
        def sb(name, shape, dt):
            return es.enter_context(nc.sbuf_tensor(name, shape, dt))
        xf = sb("xf", [128, nk * TPC], BF16)
        wblk = [sb(f"wblk{i}", [128, nk * 512], BF16) for i in range(2)]
        auxb = sb("auxb", [128, nk * 128], BF16)
        corb = sb("corb", [128, 16], BF16)
        bwb = sb("bwb", [16, C], BF16)
        bias = sb("bias", [1, C], F32)
        ones = sb("ones", [1, 128], F32)
        idnb = sb("idnb_t", [128, 128], BF16)
        us = sb("us", [128, 24], F32)
        pr = sb("pr", [128, 8], F32)
        nm = sb("nm", [128, 2], F32)
        opb = sb("opb", [128, 128], BF16)
        mxb = sb("mxb", [128, 16], BF16)
        mxT = sb("mxT", [16, nt * 128], BF16)
        outb = [sb(f"outb{i}", [128, 512], BF16) for i in range(2)]
        ps_u = es.enter_context(nc.psum_tensor("psu", [128, 512], F32))
        ps_tr = es.enter_context(nc.psum_tensor("pstr", [128, 512], BF16))
        ps_m = es.enter_context(nc.psum_tensor("psm", [128, 512], F32))
        ps = [es.enter_context(nc.psum_tensor(f"ps{i}", [128, 512], F32)) for i in range(2)]
        ldx = es.enter_context(nc.semaphore("ldx"))
        ldw = es.enter_context(nc.semaphore("ldw"))
        s_um = es.enter_context(nc.semaphore("s_um"))
        s_mm = es.enter_context(nc.semaphore("s_mm"))
        s_ev = es.enter_context(nc.semaphore("s_ev"))
        st = es.enter_context(nc.semaphore("st"))
        blk = es.enter_context(nc.Block())

        @blk.scalar
        def _(sc):
            sc.preload_act_table(AF.Exp)

        @blk.gpsimd
        def _(g):
            g.dma_start(out=bias[:], in_=b_d[:, :]).then_inc(ldx, 16)
            g.dma_start(out=ones[:], in_=one_d[:, :]).then_inc(ldx, 16)
            g.dma_start(out=idnb[:], in_=idb_d[:, :]).then_inc(ldx, 16)
            g.dma_start(out=corb[:], in_=cor_d[:, :]).then_inc(ldx, 16)
            g.dma_start(out=bwb[:], in_=bw_d[:, :]).then_inc(ldx, 16)
            src = bass.AP(aux_h, 0, [[128, 128], [128 * 128, nk], [1, 128]])
            g.dma_start(out=auxb.rearrange("p (kc n) -> p kc n", n=128), in_=src).then_inc(ldx, 16)

        @blk.sync
        def _(s):
            for kc in range(nk):
                s.dma_start_transpose(
                    out=xf[:, kc * TPC:(kc + 1) * TPC],
                    in_=x_d[:, kc * 128:(kc + 1) * 128]).then_inc(ldx, 16)
        ldx_all = 96 + nk * 16

        um_n = [0]
        for t in range(nt):
            @blk.tensor
            def _(te):
                te.wait_ge(ldx, ldx_all)
                if t > 0:
                    te.wait_ge(s_um, um_n[0])
                for kc in range(nk):
                    te.matmul(ps_u[:, :24], xf[:, kc * TPC + t * 128:kc * TPC + (t + 1) * 128],
                              auxb[:, kc * 128:kc * 128 + 24],
                              start=(kc == 0), stop=(kc == nk - 1)).then_inc(s_um, 1) \
                        if kc == nk - 1 else te.matmul(
                            ps_u[:, :24], xf[:, kc * TPC + t * 128:kc * TPC + (t + 1) * 128],
                            auxb[:, kc * 128:kc * 128 + 24], start=(kc == 0), stop=False)
            um_n[0] += 1

            @blk.vector
            def _(v):
                v.wait_ge(s_um, um_n[0])
                v.tensor_copy(us[:], ps_u[:, :24])
                v.drain()
                v.tensor_reduce(nm[:, 0:1], us[:, 16:24], mybir.AxisListType.X, ALU.max,
                                negate=True)
                v.drain().then_inc(s_um, 1)
            um_n[0] += 1

            @blk.scalar
            def _(sc):
                sc.wait_ge(s_um, um_n[0])
                sc.activation(pr[:], us[:, 16:24], AF.Exp, bias=nm[:, 0:1],
                              accum_out=nm[:, 1:2])
                sc.drain().then_inc(s_um, 1)
            um_n[0] += 1

            @blk.vector
            def _(v):
                v.wait_ge(s_um, um_n[0])
                v.reciprocal(nm[:, 1:2], nm[:, 1:2])
                v.drain()
                v.tensor_scalar(pr[:], pr[:], nm[:, 1:2], None, ALU.mult)
                v.drain()
                for e in range(8):
                    v.tensor_scalar(opb[:, e * 16:(e + 1) * 16], us[:, 0:16],
                                    pr[:, e:e + 1], None, ALU.mult)
                v.drain().then_inc(s_um, 1)
            um_n[0] += 1

            @blk.tensor
            def _(te):
                te.wait_ge(s_um, um_n[0])
                te.transpose(ps_tr[:, 0:128], opb[:], idnb[:]).then_inc(s_um, 1)
            um_n[0] += 1

            @blk.vector
            def _(v):
                v.wait_ge(s_um, um_n[0])
                v.tensor_copy(opb[:], ps_tr[:, 0:128])
                v.drain().then_inc(s_um, 1)
            um_n[0] += 1

            @blk.tensor
            def _(te):
                te.wait_ge(s_um, um_n[0])
                te.matmul(ps_m[:, :16], opb[:], corb[:], start=True, stop=True).then_inc(s_um, 1)
            um_n[0] += 1

            @blk.vector
            def _(v):
                v.wait_ge(s_um, um_n[0])
                v.tensor_scalar(mxb[:], ps_m[:, :16], 2.0, None, ALU.mult)
                v.drain().then_inc(s_um, 1)
            um_n[0] += 1

            @blk.tensor
            def _(te):
                te.wait_ge(s_um, um_n[0])
                te.transpose(ps_tr[0:16, 0:128], mxb[:], idnb[:]).then_inc(s_um, 1)
            um_n[0] += 1

            @blk.vector
            def _(v):
                v.wait_ge(s_um, um_n[0])
                v.tensor_copy(mxT[:, t * 128:(t + 1) * 128], ps_tr[0:16, 0:128])
                v.drain().then_inc(s_um, 1)
            um_n[0] += 1

        mm_n = [0]
        ev_n = [0]
        st_n = [0]

        def load_w(og, buf):
            @blk.gpsimd
            def _(g):
                if og >= 2:
                    g.wait_ge(s_mm, (og - 1) * nt)
                src = bass.AP(w_h, og * 512, [[C, 128], [128 * C, nk], [1, 512]])
                g.dma_start(out=wblk[buf].rearrange("p (kc n) -> p kc n", n=512),
                            in_=src).then_inc(ldw, 16)

        load_w(0, 0)
        load_w(1, 1)
        for og in range(no):
            buf = og % 2
            for t in range(nt):
                pb = mm_n[0] % 2

                @blk.tensor
                def _(te):
                    te.wait_ge(ldw, (og + 1) * 16)
                    if og == 0 and t == 0:
                        te.wait_ge(s_um, um_n[0])
                    if mm_n[0] >= 2:
                        te.wait_ge(s_ev, mm_n[0] - 1)
                    for kc in range(nk):
                        te.matmul(ps[pb][:], xf[:, kc * TPC + t * 128:kc * TPC + (t + 1) * 128],
                                  wblk[buf][:, kc * 512:(kc + 1) * 512],
                                  start=(kc == 0), stop=False)
                    te.matmul(ps[pb][:], mxT[:, t * 128:(t + 1) * 128],
                              bwb[:, og * 512:(og + 1) * 512], start=False, stop=False)
                    te.matmul(ps[pb][:], ones[:, :], bias[:, og * 512:(og + 1) * 512],
                              start=False, stop=True).then_inc(s_mm, 1)
                mm_n[0] += 1

                @blk.vector
                def _(v):
                    v.wait_ge(s_mm, mm_n[0])
                    if ev_n[0] >= 2:
                        v.wait_ge(st, (ev_n[0] - 1) * 16)
                    v.tensor_copy(outb[pb][:], ps[pb][:])
                    v.drain().then_inc(s_ev, 1)
                ev_n[0] += 1

                @blk.sync
                def _(s):
                    s.wait_ge(s_ev, ev_n[0])
                    s.dma_start(out=y_d[t * 128:(t + 1) * 128, og * 512:(og + 1) * 512],
                                in_=outb[pb][:]).then_inc(st, 16)
                st_n[0] += 16
            if og + 2 < no:
                load_w(og + 2, buf)
        @blk.sync
        def _(s):
            s.wait_ge(st, st_n[0])
    return nc


def run_ffn2(xs, inputs):
    nc = _cached(("ffn2",), build_ffn2)
    Aw = np.asarray(inputs["L2_ffn_A_w"]).astype(np.float32)
    rw = np.asarray(inputs["L2_ffn_router_w"]).astype(np.float32)
    waux = np.concatenate([Aw.T, rw.T, np.zeros((C, 104), np.float32)], 1)
    cores = np.asarray(inputs["L2_ffn_cores"]).astype(np.float32)
    m = {
        "w": np.ascontiguousarray(np.asarray(inputs["L2_ffn_W_base"]).astype(np.float32).T),
        "b": np.asarray(inputs["L2_ffn_b_base"]).reshape(1, C).astype(np.float32),
        "waux": np.ascontiguousarray(waux),
        "coresF": np.ascontiguousarray(cores.reshape(128, 16)),
        "bwT": np.ascontiguousarray(np.asarray(inputs["L2_ffn_B_w"]).astype(np.float32).T),
        "onesr": np.ones((1, 128), np.float32),
        "idnb": np.eye(128, dtype=BF),
    }
    maps = [dict(m, x=xs[i]) for i in range(NCORE)]
    res = run_bass_kernel_spmd(nc, maps, core_ids=list(range(NCORE)))
    return [r["y"] for r in res.results]


def build_attn3():
    nc = bass.Bass("TRN2", target_bir_lowering=False)
    x_d = nc.dram_tensor("x", [TPC, C], BF16, kind="ExternalInput").ap()
    wq_h = nc.dram_tensor("wqselT", [C, 1024], F32, kind="ExternalInput")
    bq_d = nc.dram_tensor("bqsel", [1, 1024], F32, kind="ExternalInput").ap()
    mz_d = nc.dram_tensor("mzcat", [128, 8 * 129], F32, kind="ExternalInput").ap()
    wo_h = nc.dram_tensor("woT", [C, C], F32, kind="ExternalInput")
    one_d = nc.dram_tensor("onesr", [1, 128], F32, kind="ExternalInput").ap()
    idb_d = nc.dram_tensor("idnb", [128, 128], BF16, kind="ExternalInput").ap()
    y_d = nc.dram_tensor("y", [TPC, C], BF16, kind="ExternalOutput").ap()
    nt = TPC // 128
    nk = C // 128
    no = C // 512
    EPS3 = 1e-6
    with ExitStack() as es:
        def sb(name, shape, dt):
            return es.enter_context(nc.sbuf_tensor(name, shape, dt))
        xf = sb("xf", [128, nk * TPC], BF16)
        wqb = sb("wqb", [128, nk * 1024], BF16)
        wblk = [wqb[:, i * nk * 512:(i + 1) * nk * 512] for i in range(2)]
        mzb = sb("mzb", [128, 8 * 129 + 8], BF16)
        bq = sb("bq", [1, 1024], F32)
        ones = sb("ones", [1, 128], F32)
        idnb = sb("idnb_t", [128, 128], BF16)
        sqf = sb("sqf", [128, 1024], F32)
        sqm = sb("sqm", [128, 1024], F32)
        sqb = sb("sqb", [128, 1024], BF16)
        den = sb("den", [128, 2], F32)
        rfm = sb("rfm", [128, nt * 1024], BF16)
        rtma = [sb(f"rtma{i}", [128, 128], BF16) for i in range(2)]
        rtmb = [sb(f"rtmb{i}", [128, 128], BF16) for i in range(2)]
        weff = sb("weff", [128, 8 * 512], BF16)
        outb = [sb(f"outb{i}", [128, 512], BF16) for i in range(2)]
        ps_q = es.enter_context(nc.psum_tensor("psq", [128, 512], F32))
        ps_r = [es.enter_context(nc.psum_tensor(f"psr{i}", [128, 512], F32)) for i in range(2)]
        ps_a = [es.enter_context(nc.psum_tensor(f"psa{i}", [128, 512], BF16)) for i in range(2)]
        ps = [es.enter_context(nc.psum_tensor(f"ps{i}", [128, 512], F32)) for i in range(2)]
        ldx = es.enter_context(nc.semaphore("ldx"))
        ldw = es.enter_context(nc.semaphore("ldw"))
        s_q = es.enter_context(nc.semaphore("s_q"))
        s_t1 = es.enter_context(nc.semaphore("s_t1"))
        s_c1 = es.enter_context(nc.semaphore("s_c1"))
        s_rm = es.enter_context(nc.semaphore("s_rm"))
        s_dn = es.enter_context(nc.semaphore("s_dn"))
        s_t2 = es.enter_context(nc.semaphore("s_t2"))
        s_c2 = es.enter_context(nc.semaphore("s_c2"))
        s_we = es.enter_context(nc.semaphore("s_we"))
        s_mm = es.enter_context(nc.semaphore("s_mm"))
        s_ev = es.enter_context(nc.semaphore("s_ev"))
        st = es.enter_context(nc.semaphore("st"))
        blk = es.enter_context(nc.Block())

        @blk.scalar
        def _(sc):
            sc.preload_act_table(AF.Exp)

        @blk.gpsimd
        def _(g):
            g.dma_start(out=bq[:], in_=bq_d[:, :]).then_inc(ldx, 16)
            g.dma_start(out=ones[:], in_=one_d[:, :]).then_inc(ldx, 16)
            g.dma_start(out=idnb[:], in_=idb_d[:, :]).then_inc(ldx, 16)
            g.dma_start(out=mzb[:, :8 * 129], in_=mz_d[:, :]).then_inc(ldx, 16)
            src = bass.AP(wq_h, 0, [[1024, 128], [128 * 1024, nk], [1, 1024]])
            g.dma_start(out=wqb.rearrange("p (kc n) -> p kc n", n=1024), in_=src).then_inc(ldx, 16)

        @blk.sync
        def _(s):
            for kc in range(nk):
                s.dma_start_transpose(
                    out=xf[:, kc * TPC:(kc + 1) * TPC],
                    in_=x_d[:, kc * 128:(kc + 1) * 128]).then_inc(ldx, 16)
        ldx_all = 80 + nk * 16

        q_n = [0]
        c1_n = [0]
        rm_n = [0]
        dn_n = [0]
        t2_n = [0]
        c2_n = [0]
        for t in range(nt):
            for half in range(2):
                @blk.tensor
                def _(te):
                    te.wait_ge(ldx, ldx_all)
                    if t > 0 and half == 0:
                        te.wait_ge(s_q, q_n[0])
                    elif half == 1:
                        te.wait_ge(s_q, q_n[0] - 1)
                    for kc in range(nk):
                        te.matmul(ps_q[:], xf[:, kc * TPC + t * 128:kc * TPC + (t + 1) * 128],
                                  wqb[:, kc * 1024 + half * 512:kc * 1024 + (half + 1) * 512],
                                  start=(kc == 0), stop=False)
                    te.matmul(ps_q[:], ones[:, :], bq[:, half * 512:(half + 1) * 512],
                              start=False, stop=True).then_inc(s_q, 1)
                q_n[0] += 1

                @blk.scalar
                def _(sc):
                    sc.wait_ge(s_q, q_n[0])
                    sc.activation(sqf[:, half * 512:(half + 1) * 512], ps_q[:], AF.Relu)
                    sc.drain().then_inc(s_q, 1)
                q_n[0] += 1

                @blk.vector
                def _(v):
                    v.wait_ge(s_q, q_n[0])
                    v.tensor_scalar(sqm[:, half * 512:(half + 1) * 512], ps_q[:],
                                    0.0, None, ALU.min)
                    v.drain().then_inc(s_q, 1)
                q_n[0] += 1

                @blk.scalar
                def _(sc):
                    sc.wait_ge(s_q, q_n[0])
                    sc.activation(sqm[:, half * 512:(half + 1) * 512],
                                  sqm[:, half * 512:(half + 1) * 512], AF.Exp)
                    sc.drain().then_inc(s_q, 1)
                q_n[0] += 1

            @blk.vector
            def _(v):
                v.wait_ge(s_q, q_n[0])
                v.tensor_tensor(sqb[:], sqf[:], sqm[:], ALU.add)
                v.drain().then_inc(s_q, 1)
            q_n[0] += 1

            for j in range(8):
                jb = j % 2

                @blk.tensor
                def _(te, j=j, jb=jb):
                    te.wait_ge(s_q, q_n[0])
                    if c2_n[0] >= 2 or t > 0:
                        te.wait_ge(s_c2, max(0, c2_n[0] - 1))
                    te.transpose(ps_a[jb][:, 0:128], sqb[:, j * 128:(j + 1) * 128],
                                 idnb[:]).then_inc(s_t1, 1)

                @blk.vector
                def _(v, j=j, jb=jb):
                    v.wait_ge(s_t1, t * 8 + j + 1)
                    v.tensor_copy(rtma[jb][:], ps_a[jb][:, 0:128])
                    v.drain().then_inc(s_c1, 1)
                c1_n[0] += 1

                @blk.tensor
                def _(te, j=j, jb=jb):
                    te.wait_ge(s_c1, c1_n[0])
                    if dn_n[0] >= 2:
                        te.wait_ge(s_dn, dn_n[0] - 1)
                    te.matmul(ps_r[jb][:, :129], rtma[jb][:], mzb[:, j * 129:(j + 1) * 129],
                              start=True, stop=True).then_inc(s_rm, 1)
                rm_n[0] += 1

                @blk.vector
                def _(v, j=j, jb=jb):
                    v.wait_ge(s_rm, rm_n[0])
                    v.tensor_scalar(den[:, jb:jb + 1], ps_r[jb][:, 128:129], EPS3, None, ALU.add)
                    v.drain()
                    v.reciprocal(den[:, jb:jb + 1], den[:, jb:jb + 1])
                    v.drain()
                    v.tensor_scalar(rtmb[jb][:], ps_r[jb][:, 0:128], den[:, jb:jb + 1],
                                    None, ALU.mult)
                    v.drain().then_inc(s_dn, 1)
                dn_n[0] += 1

                @blk.tensor
                def _(te, j=j, jb=jb):
                    te.wait_ge(s_dn, dn_n[0])
                    te.transpose(ps_a[jb][:, 128:256], rtmb[jb][:], idnb[:]).then_inc(s_t2, 1)
                t2_n[0] += 1

                @blk.vector
                def _(v, j=j, jb=jb):
                    v.wait_ge(s_t2, t2_n[0])
                    v.tensor_copy(rfm[:, t * 1024 + j * 128:t * 1024 + (j + 1) * 128],
                                  ps_a[jb][:, 128:256])
                    v.drain().then_inc(s_c2, 1)
                c2_n[0] += 1

        mm_n = [0]
        ev_n = [0]
        st_n = [0]
        we_n = [0]

        def load_w(og, buf):
            @blk.gpsimd
            def _(g):
                if og == 0:
                    g.wait_ge(s_c2, c2_n[0])
                else:
                    g.wait_ge(s_we, og)
                src = bass.AP(wo_h, og * 512, [[C, 128], [128 * C, nk], [1, 512]])
                g.dma_start(out=wblk[buf].rearrange("p (kc n) -> p kc n", n=512),
                            in_=src).then_inc(ldw, 16)

        load_w(0, 0)
        for og in range(no):
            buf = og % 2

            @blk.vector
            def _(v, og=og):
                v.wait_ge(ldw, (og + 1) * 16)
                if og > 0:
                    v.wait_ge(s_mm, og * nt)
                for j in range(8):
                    base = wblk[buf][:, (4 * j) * 512:(4 * j + 1) * 512]
                    v.tensor_tensor(weff[:, j * 512:(j + 1) * 512], base,
                                    wblk[buf][:, (4 * j + 1) * 512:(4 * j + 2) * 512], ALU.add)
                    v.tensor_tensor(weff[:, j * 512:(j + 1) * 512], weff[:, j * 512:(j + 1) * 512],
                                    wblk[buf][:, (4 * j + 2) * 512:(4 * j + 3) * 512], ALU.add)
                    v.tensor_tensor(weff[:, j * 512:(j + 1) * 512], weff[:, j * 512:(j + 1) * 512],
                                    wblk[buf][:, (4 * j + 3) * 512:(4 * j + 4) * 512], ALU.add)
                v.drain().then_inc(s_we, 1)
            we_n[0] += 1
            if og + 1 < no:
                load_w(og + 1, 1 - buf)

            for t in range(nt):
                pb = mm_n[0] % 2

                @blk.tensor
                def _(te, og=og, t=t, pb=pb):
                    te.wait_ge(s_we, we_n[0])
                    if og == 0 and t == 0:
                        te.wait_ge(s_c2, c2_n[0])
                    if mm_n[0] >= 2:
                        te.wait_ge(s_ev, mm_n[0] - 1)
                    for j in range(8):
                        mm = te.matmul(ps[pb][:],
                                       rfm[:, t * 1024 + j * 128:t * 1024 + (j + 1) * 128],
                                       weff[:, j * 512:(j + 1) * 512],
                                       start=(j == 0), stop=(j == 7))
                        if j == 7:
                            mm.then_inc(s_mm, 1)
                mm_n[0] += 1

                @blk.vector
                def _(v):
                    v.wait_ge(s_mm, mm_n[0])
                    if ev_n[0] >= 2:
                        v.wait_ge(st, (ev_n[0] - 1) * 16)
                    v.tensor_copy(outb[pb][:], ps[pb][:])
                    v.drain().then_inc(s_ev, 1)
                ev_n[0] += 1

                @blk.sync
                def _(s):
                    s.wait_ge(s_ev, ev_n[0])
                    s.dma_start(out=y_d[t * 128:(t + 1) * 128, og * 512:(og + 1) * 512],
                                in_=outb[pb][:]).then_inc(st, 16)
                st_n[0] += 16
        @blk.sync
        def _(s):
            s.wait_ge(st, st_n[0])
    return nc


def run_attn3(xs, inputs):
    nc = _cached(("attn3",), build_attn3)
    Wq = np.asarray(inputs["L3_attn_Wq"]).astype(np.float32)
    bq = np.asarray(inputs["L3_attn_bq"]).astype(np.float32)
    sel = np.concatenate([np.arange(h * 128, (h + 1) * 128) for h in range(0, 32, 4)])
    wqselT = np.ascontiguousarray(Wq[sel].T)
    bqsel = bq[sel].reshape(1, 1024)
    M = np.asarray(inputs["L3_attn_M"]).astype(np.float32)
    z = np.asarray(inputs["L3_attn_z"]).astype(np.float32)
    mz = np.concatenate([np.concatenate([M[j], z[j][:, None]], 1) for j in range(8)], 1)
    m = {
        "wqselT": wqselT, "bqsel": bqsel,
        "mzcat": np.ascontiguousarray(mz),
        "woT": np.ascontiguousarray(np.asarray(inputs["L3_attn_Wo"]).astype(np.float32).T),
        "onesr": np.ones((1, 128), np.float32),
        "idnb": np.eye(128, dtype=BF),
    }
    maps = [dict(m, x=xs[i]) for i in range(NCORE)]
    res = run_bass_kernel_spmd(nc, maps, core_ids=list(range(NCORE)))
    return [r["y"] for r in res.results]


def _prep_maps(inputs):
    def pad(qT, rows):
        qT = np.ascontiguousarray(qT)
        if qT.shape[0] < rows:
            qT = np.concatenate([qT, np.zeros((rows - qT.shape[0],) + qT.shape[1:], qT.dtype)], 0)
        return qT

    def tpad(q, rows):
        return pad(np.asarray(q).T, rows)
    gqT = tpad(inputs["L0_ffn_gate_qweight"], 11264)
    gsT = tpad(inputs["L0_ffn_gate_scales"], 11264)
    gzR = pad(np.repeat(np.asarray(inputs["L0_ffn_gate_zeros"]).T, 8, axis=0), 11264)
    uqT = tpad(inputs["L0_ffn_up_qweight"], 11264)
    usT = tpad(inputs["L0_ffn_up_scales"], 11264)
    uzR = pad(np.repeat(np.asarray(inputs["L0_ffn_up_zeros"]).T, 8, axis=0), 11264)
    dqT = np.ascontiguousarray(np.asarray(inputs["L0_ffn_down_qweight"]).T)
    dsT = np.ascontiguousarray(np.asarray(inputs["L0_ffn_down_scales"]).T)
    dzR = np.ascontiguousarray(np.repeat(np.asarray(inputs["L0_ffn_down_zeros"]).T, 8, axis=0))
    wpr = np.asarray(inputs["L3_ffn_weight_packed"]).reshape(4096, 2048)
    l3sz = np.zeros((4096, 128), np.float32)
    l3sz[:, 0] = np.asarray(inputs["L3_ffn_scale"]).reshape(-1)
    l3sz[:, 1] = np.asarray(inputs["L3_ffn_zero_point"]).reshape(-1)
    shf = np.zeros((128, 128), np.int32)
    shf[:, 0] = 4 * (np.arange(128) % 8)
    idn = np.eye(128, dtype=BF)
    maps = []
    for c in range(NCORE):
        g0, g1 = c * 1408, (c + 1) * 1408
        d0, d1 = c * 512, (c + 1) * 512
        maps.append({
            "gq": np.ascontiguousarray(gqT[g0:g1]), "gs": np.ascontiguousarray(gsT[g0:g1]),
            "gz": np.ascontiguousarray(gzR[g0:g1]),
            "uq": np.ascontiguousarray(uqT[g0:g1]), "us": np.ascontiguousarray(usT[g0:g1]),
            "uz": np.ascontiguousarray(uzR[g0:g1]),
            "dq": np.ascontiguousarray(dqT[d0:d1]), "ds": np.ascontiguousarray(dsT[d0:d1]),
            "dz": np.ascontiguousarray(dzR[d0:d1]),
            "wp": np.ascontiguousarray(wpr[d0:d1]), "l3sz": np.ascontiguousarray(l3sz[d0:d1]),
            "shf": shf, "idn": idn,
        })
    return maps


def run_prep(inputs):
    nc = _cached(("prep",), build_prep)
    maps = _prep_maps(inputs)
    res = run_bass_kernel_spmd(nc, maps, core_ids=list(range(NCORE)))

    def unscramble(a, K_):
        nt = a.shape[0] // 128
        return a.reshape(nt, 128, K_ // 128, 128).transpose(2, 1, 0, 3).reshape(K_, nt * 128)
    GWT = np.concatenate([unscramble(r["gw"], 4096) for r in res.results], 1)
    UWT = np.concatenate([unscramble(r["uw"], 4096) for r in res.results], 1)
    DWT = np.concatenate([unscramble(r["dw"], 11008) for r in res.results], 1)
    L3WT = np.concatenate([unscramble(r["l3w"], 4096) for r in res.results], 1)
    return GWT, UWT, DWT, L3WT


def run_proj(xs, w, bias=None, w_mode="f32T", out_dt=BF16, K=C, O=C):
    nc = _cached(("proj", K, O, w_mode, bias is not None, str(out_dt)),
                 lambda: build_proj(K, O, w_mode, bias is not None, out_dt))
    maps = []
    for i in range(NCORE):
        m = {"x": xs[i], "w": w}
        if bias is not None:
            m["b"] = bias.reshape(1, O).astype(np.float32)
            m["onesr"] = np.ones((1, 128), np.float32)
        maps.append(m)
    res = run_bass_kernel_spmd(nc, maps, core_ids=list(range(NCORE)))
    return [r["y"] for r in res.results]


def run_ln(xs, as_=None, fs=None, emit_xn=True):
    wb = as_ is not None
    f_dtype = BF16
    if wb and fs[0].dtype == np.float16:
        f_dtype = F16
    nc = _cached(("ln", wb, emit_xn, str(None if not wb else fs[0].dtype)),
                 lambda: build_ln(wb, emit_xn, f_dtype))
    cst = np.full((128, 128), LNEPS, np.float32)
    maps = []
    for i in range(NCORE):
        m = {"x": xs[i], "cst": cst}
        if wb:
            m["a"] = as_[i]
            m["f"] = fs[i]
        maps.append(m)
    res = run_bass_kernel_spmd(nc, maps, core_ids=list(range(NCORE)))
    xo = [r.get("xo") for r in res.results] if wb else xs
    xn = [r.get("xn") for r in res.results] if emit_xn else None
    return xo, xn


def _shard_rows(full):
    return [np.ascontiguousarray(full[c // 2, (c % 2) * TPC:(c % 2 + 1) * TPC])
            for c in range(NCORE)]


def _join_rows(parts, dtype):
    out = np.empty((B, T) + parts[0].shape[1:], dtype)
    for c in range(NCORE):
        out[c // 2, (c % 2) * TPC:(c % 2 + 1) * TPC] = parts[c]
    return out


def kernel(**inputs):
    inputs = {k_: np.asarray(v) for k_, v in inputs.items()}
    x0 = inputs["L0_attn_x"].astype(np.float32)
    xs = _shard_rows(x0)

    GWT, UWT, DWT, L3WT = run_prep(inputs)
    GWT = np.ascontiguousarray(GWT)
    UWT = np.ascontiguousarray(UWT)
    DWT = np.ascontiguousarray(DWT)
    L3WT = np.ascontiguousarray(L3WT)

    _, xn = run_ln(xs, emit_xn=True)
    xn_full = _join_rows(xn, BF)
    O0 = run_attn_qk(0, xn_full, inputs)
    a0 = run_proj(_shard_rows(O0),
                  np.ascontiguousarray(inputs["L0_attn_wo"].astype(np.float32).T))
    f0 = run_ffn0(xn, GWT, UWT, DWT)

    xs1, xn1 = run_ln(xs, a0, f0)
    xn1_full = _join_rows(xn1, BF)
    O1 = run_attn1(xn1_full, inputs["L1_attn_bv_q"].astype(np.float32),
                   inputs["L1_attn_bv_k"].astype(np.float32),
                   inputs["L1_attn_bv_v"].astype(np.float32))
    a1 = _shard_rows(O1)
    f1 = run_ffn1(xn1, inputs)

    xs2, xn2 = run_ln(xs1, a1, f1)
    xn2_full = _join_rows(xn2, BF)
    O2 = run_attn_qk(2, xn2_full, inputs)
    a2 = run_proj(_shard_rows(O2),
                  np.ascontiguousarray(inputs["L2_attn_Wo"].astype(np.float32).T))
    f2 = run_ffn2(xn2, inputs)

    xs3, xn3 = run_ln(xs2, a2, f2)
    a3 = run_attn3(xn3, inputs)
    f3 = run_proj(xn3, L3WT, bias=inputs["L3_ffn_bias"].astype(np.float32),
                  w_mode="bf16T")

    xs4, _ = run_ln(xs3, a3, f3, emit_xn=False)
    return _join_rows(xs4, np.float32).reshape(B, T, C)
